# version 36
# baseline (speedup 1.0000x reference)
import contextlib
import numpy as np
import ml_dtypes
import concourse.bass as bass
import concourse.mybir as mybir
from concourse.bass_utils import run_bass_kernel_spmd

F32 = mybir.dt.float32
BF16 = mybir.dt.bfloat16
AF = mybir.ActivationFunctionType
ALU = mybir.AluOpType
AX = mybir.AxisListType
MAGIC = 12582912.0

NT = 4096
D = 1024
DFF = 4096
NH = 16
DH = 64
CH = 128
NCH = NT // CH
NBK = D // 128
NBF = DFF // 128
LN8 = float(np.log(1.0 / 8.0))

ENGINES = ("sync", "gpsimd", "vector", "scalar", "tensor")


class Prog:
    def __init__(self, nc, stack, sem_names):
        self.nc = nc
        self.ops = {e: [] for e in ENGINES}
        self.sem = {n: stack.enter_context(nc.semaphore(n)) for n in sem_names}
        self.cnt = {n: 0 for n in sem_names}

    def emit(self, eng, fn):
        self.ops[eng].append(fn)

    def inc(self, name, k=1):
        self.cnt[name] += k
        return self.sem[name]

    def wait(self, eng, name, val=None):
        v = self.cnt[name] if val is None else val
        hh = self.sem[name]
        self.emit(eng, lambda E, hh=hh, v=v: E.wait_ge(hh, v))

    def run(self, block):
        for eng in ENGINES:
            fns = self.ops[eng]
            if fns:
                getattr(block, eng)(lambda E, fns=fns: [f(E) for f in fns])


def build(nc):
    x_d = nc.dram_tensor("x", [NT, D], F32, kind="ExternalInput").ap()
    wq_d = nc.dram_tensor("Wq", [D, D], F32, kind="ExternalInput").ap()
    wk_d = nc.dram_tensor("Wk", [D, D], F32, kind="ExternalInput").ap()
    wv_d = nc.dram_tensor("Wv", [D, D], F32, kind="ExternalInput").ap()
    wg_d = nc.dram_tensor("Wg", [D, D], F32, kind="ExternalInput").ap()
    wgt_d = nc.dram_tensor("Wgt", [D, NH], F32, kind="ExternalInput").ap()
    wo_d = nc.dram_tensor("Wo", [D, D], F32, kind="ExternalInput").ap()
    W_in = nc.dram_tensor("W_in", [D, DFF], F32, kind="ExternalInput").ap()
    b_in_cols = nc.dram_tensor("b_in_cols", [128, NBF], F32, kind="ExternalInput").ap()
    W_out = nc.dram_tensor("W_out", [DFF, D], F32, kind="ExternalInput").ap()
    b_out_d = nc.dram_tensor("b_out", [1, D], F32, kind="ExternalInput").ap()
    cid_d = nc.dram_tensor("ident_bf", [128, 128], BF16, kind="ExternalInput").ap()
    cidf_d = nc.dram_tensor("ident_f32", [128, 128], F32, kind="ExternalInput").ap()
    ctriu_d = nc.dram_tensor("triu16", [128, 128], F32, kind="ExternalInput").ap()
    cslo_d = nc.dram_tensor("slo16", [128, 128], F32, kind="ExternalInput").ap()
    cones16_d = nc.dram_tensor("onescol16", [128, 1], F32, kind="ExternalInput").ap()
    cones1_d = nc.dram_tensor("ones_1x128", [1, 128], F32, kind="ExternalInput").ap()
    c127_d = nc.dram_tensor("c127_1x128", [1, 128], F32, kind="ExternalInput").ap()
    cmask4_d = nc.dram_tensor("mask4", [128, 512], F32, kind="ExternalInput").ap()
    ccols_d = nc.dram_tensor("c_cols", [128, 8], F32, kind="ExternalInput").ap()
    out_d = nc.dram_tensor("out", [NT, D], F32, kind="ExternalOutput").ap()

    xnT_d = nc.dram_tensor("xnT_spill", [D, NT], BF16, kind="Internal").ap()
    acc_d = nc.dram_tensor("acc_spill", [NT, D], F32, kind="Internal").ap()
    hidT = nc.dram_tensor("hid_spill", [DFF, NT], BF16, kind="Internal").ap()
    wq_bounce = nc.dram_tensor("wq_bounce", [1, DFF], F32, kind="Internal").ap()
    fold_bounce = nc.dram_tensor("fold_bounce", [1, 128], F32, kind="Internal").ap()
    cc1_in = nc.dram_tensor("cc1_in", [1, 128], F32, kind="Internal")
    cc1_out = nc.dram_tensor("cc1_out", [1, 128], F32, kind="Internal")
    cc2_in = nc.dram_tensor("cc2_in", [1, 128], F32, kind="Internal")
    cc2_out = nc.dram_tensor("cc2_out", [1, 128], F32, kind="Internal")

    stack = contextlib.ExitStack()
    with stack:
        def sb(name, shape, dtype=F32):
            return stack.enter_context(nc.sbuf_tensor(name, shape, dtype))

        BA = sb("BA", [128, 75904], BF16)
        FA = sb("FA", [128, 12288], F32)
        WQ = BA[:, 0:8192]; WK = BA[:, 8192:16384]; WV = BA[:, 16384:24576]
        WG = BA[:, 24576:32768]; WO = BA[:, 32768:40960]; WGT = BA[:, 40960:41088]
        xn_bf = [BA[:, 41088:42112], BA[:, 42112:43136]]
        xnT_s = [BA[:, 43136:44160], BA[:, 44160:45184]]
        qe_bf = [BA[:, 45184:46208], BA[:, 46208:47232]]
        ke_bf = [BA[:, 47232:48256], BA[:, 48256:49280]]
        v_bf = [BA[:, 49280:50304], BA[:, 50304:51328]]
        qT_s = [BA[:, 51328:53376], BA[:, 53376:55424]]
        kT_s = [BA[:, 55424:57472], BA[:, 57472:59520]]
        AT_bf = [BA[:, 59520:61568], BA[:, 61568:63616]]
        orms = BA[:, 63616:67712]
        gmac = BA[:, 67712:71808]
        gtmp = BA[:, 71808:72832]
        ogb = BA[:, 72832:73856]
        ogT_s = BA[:, 73856:74880]
        Sdec_bf = BA[:, 74880:75904]
        arenaA = BA[:, 0:32768]
        arenaB = BA[:, 32768:65536]
        hstage = [BA[:, 65536:66560], BA[:, 66560:67584]]
        hdbuf = [BA[:, 67584:68608], BA[:, 68608:69632]]
        xbuf = [FA[:, i*1024:(i+1)*1024] for i in range(4)]
        sqd = FA[:, 4096:5120]
        osq = FA[:, 5120:6144]
        accst = [FA[:, 6144:7168], FA[:, 7168:8192]]
        S_f = FA[0:64, 8192:9216]
        Sdec = FA[0:64, 9216:10240]
        wld = FA[:, 10240:11264]
        lnsq = FA[:, 10240:11264]
        run = FA[:, 0:4096]
        swb = FA[:, 4096:8192]
        wstage = FA[:, 8192:12288]
        ostage = [FA[:, 4096:5120], FA[:, 5120:6144]]
        accbuf = [FA[:, 6144:7168], FA[:, 7168:8192]]
        p3scr = FA[:, 0:1024]
        boutb = FA[:, 8192:9216]
        brow = FA[:, 9216:10240]

        c_id = sb("c_id", [128, 128], BF16)
        c_idf = sb("c_idf", [128, 128], F32)
        c_triu = sb("c_triu", [128, 128], F32)
        c_slo = sb("c_slo", [128, 128], F32)
        c_ones16 = sb("c_ones16", [128, 1], F32)
        c_ones1 = sb("c_ones1", [1, 128], F32)
        c_c127 = sb("c_c127", [1, 128], F32)
        c_mask4 = sb("c_mask4", [128, 512], F32)
        c_cols = sb("s_ccols", [128, 8], F32)
        scal_t = sb("scal_t", [128, 32], F32)
        g16 = [sb(f"g16_{i}", [128, 160], F32) for i in range(2)]
        rst16 = [sb(f"rst16_{i}", [128, 32], F32) for i in range(2)]
        expBc = [sb(f"expBc{i}", [128, 16], F32) for i in range(2)]
        expBrow = [sb(f"expBrow{i}", [1, 16], F32) for i in range(2)]
        xamaxc = sb("xamaxc", [128, 32], F32)
        hqc = sb("hqc", [128, 32], F32)
        amax2c = sb("amax2c", [128, 32], F32)
        sc32amax = sb("sc32amax", [128, 32], F32)
        sc32 = sb("sc32", [128, 32], F32)
        sc2 = sb("sc2", [128, 32], F32)
        s_bic = sb("s_bic", [128, 32], F32)
        rcol = sb("rcol", [128, 1], F32)
        xcol = sb("xcol", [128, 1], F32)
        rows4 = sb("rows4", [1, 512], F32)
        xrow = rows4[0:1, 0:128]; ccs = rows4[0:1, 128:256]
        gs1 = rows4[0:1, 256:384]; gs2 = rows4[0:1, 384:512]
        sc_small = sb("sc_small", [1, 8], F32)
        r_sx_col = sb("r_sx_col", [128, 1], F32)
        s_x_col = sb("s_x_col", [128, 1], F32)
        r_sh_col = sb("r_sh_col", [128, 1], F32)
        s_h_col = sb("s_h_col", [128, 1], F32)
        w32 = sb("w32", [32, 128], F32)
        bnst = sb("bnst", [128, 96], F32)

        def psum(name, shape, dtype=F32):
            return stack.enter_context(nc.psum_tensor(name, shape, dtype))
        pb = [psum(f"pb{i}", [128, 512], F32) for i in range(7)]
        pT = psum("pT", [128, 1024], BF16)

        P = Prog(nc, stack, ["dS", "dG", "dA", "pe", "act", "dve", "cc", "pl"])
        E = P.emit

        def Wv_(w, k):
            return w.rearrange("(b p) n -> b p n", p=128)[k]

        def pe_mm_group(mms):
            for i, mm in enumerate(mms):
                if i == len(mms) - 1:
                    h = P.inc("pe")
                    E("tensor", lambda T, mm=mm, h=h: mm(T).then_inc(h, 1))
                else:
                    E("tensor", lambda T, mm=mm: mm(T))
            return P.cnt["pe"]

        bank_free = {i: ("dve", 0) for i in range(8)}

        def pe_wait_bank(i):
            sname, cnt = bank_free[i]
            if cnt > 0:
                P.wait("tensor", sname, cnt)

        for dst, src in [(c_id, cid_d), (c_idf, cidf_d), (c_triu, ctriu_d),
                         (c_slo, cslo_d), (c_ones16, cones16_d), (c_ones1, cones1_d),
                         (c_c127, c127_d), (c_mask4, cmask4_d), (c_cols, ccols_d),
                         (s_bic, b_in_cols)]:
            h = P.inc("dS", 16)
            E("sync", lambda S, dst=dst, src=src, h=h: S.dma_start(out=dst[:], in_=src).then_inc(h, 16))
        E("vector", lambda V: V.memset(xamaxc[:], 0.0))
        E("vector", lambda V: V.memset(S_f[:], 0.0))
        h = P.inc("dve")
        E("vector", lambda V: V.memset(Sdec_bf[:], 0.0))
        E("vector", lambda V, h=h: V.memset(BA[64:128, 51328:59520], 0.0).then_inc(h, 1))
        wld2 = FA[:, 11264:12288]
        wstg = [wld, wld2]
        wjobs = [(WGT, wgt_d, NH, k) for k in range(NBK)]
        wjobs += [(W, wd, D, k) for (W, wd) in
                  [(WQ, wq_d), (WK, wk_d), (WV, wv_d), (WG, wg_d), (WO, wo_d)]
                  for k in range(NBK)]
        w_cast_at = {}
        w_milestones = {}
        for j, (W_s, w_dd, ncols, k) in enumerate(wjobs):
            qe_, qs_ = (("sync", "dS") if j % 2 == 0 else ("scalar", "dA"))
            ce_, cs_ = (("vector", "dve") if j % 2 == 0 else ("scalar", "act"))
            if (j - 2) in w_cast_at:
                P.wait(qe_, w_cast_at[j - 2][0], w_cast_at[j - 2][1])
            h = P.inc(qs_, 16)
            E(qe_, lambda S, w_dd=w_dd, k=k, ncols=ncols, j=j, h=h: S.dma_start(
                out=wstg[j % 2][:, 0:ncols], in_=Wv_(w_dd, k)).then_inc(h, 16))
            P.wait(ce_, qs_, P.cnt[qs_])
            h = P.inc(cs_)
            if ce_ == "vector":
                E(ce_, lambda V, W_s=W_s, k=k, ncols=ncols, j=j, h=h: V.tensor_copy(
                    W_s[:, k*ncols:(k+1)*ncols], wstg[j % 2][:, 0:ncols]).then_inc(h, 1))
            else:
                E(ce_, lambda S, W_s=W_s, k=k, ncols=ncols, j=j, h=h: S.copy(
                    W_s[:, k*ncols:(k+1)*ncols], wstg[j % 2][:, 0:ncols]).then_inc(h, 1))
            w_cast_at[j] = (cs_, P.cnt[cs_])

        acc_evac_at = {}
        scal_free_at = {}
        o_done_at = {}
        xnT_spill_at = {}
        acc_dma_at = {}
        ebc_done_at = {}
        chunk_end = {}
        xn_ready_at = {}
        trsp_done_at = {}
        rsq_act_at = {}
        ln_rsq_at = {}
        s_upd_at = {}
        tail_og3_at = None

        x_loaded_at = {}

        def emit_xload(cc):
            xb = xbuf[cc % 4]
            if cc >= 4 and (cc - 4) in acc_evac_at:
                P.wait("sync", "dve", acc_evac_at[cc - 4])
            h = P.inc("dS", 16)
            E("sync", lambda S, cc=cc, xb=xb, h=h: S.dma_start(
                out=xb, in_=x_d.rearrange("(c p) d -> c p d", p=128)[cc]).then_inc(h, 16))
            x_loaded_at[cc] = P.cnt["dS"]

        def emit_ln(cc):
            par = cc % 2
            xb = xbuf[cc % 4]
            bb = 48 + par * 16
            x_loaded = x_loaded_at[cc]
            P.wait("vector", "dS", x_loaded)
            if (cc - 2) in ln_rsq_at:
                P.wait("vector", "act", ln_rsq_at[cc - 2])
            for half in range(2):
                h = P.inc("dve")
                E("vector", lambda V, xb=xb, half=half, bb=bb, h=h: V.bn_stats(
                    bnst[:, bb + half*6: bb + half*6 + 6],
                    xb[:, half*512:(half+1)*512]).then_inc(h, 1))
            P.wait("vector", "dve")
            h = P.inc("dve")
            E("vector", lambda V, bb=bb, h=h: V.bn_aggr(
                bnst[:, bb+12:bb+14], bnst[:, bb:bb+12]).then_inc(h, 1))
            P.wait("scalar", "dve")
            h = P.inc("act")
            E("scalar", lambda S, bb=bb, h=h: S.activation(
                bnst[:, bb+15:bb+16], bnst[:, bb+13:bb+14],
                AF.Ln, bias=c_cols[:, 1:2]).then_inc(h, 1))
            P.wait("scalar", "act")
            h = P.inc("act")
            E("scalar", lambda S, bb=bb, h=h: S.activation(
                bnst[:, bb+14:bb+15], bnst[:, bb+15:bb+16],
                AF.Exp, scale=-0.5).then_inc(h, 1))
            ln_rsq_at[cc] = P.cnt["act"]
            P.wait("vector", "act")
            if (cc - 2) in trsp_done_at:
                P.wait("vector", "pe", trsp_done_at[cc - 2])
            h = P.inc("dve")
            E("vector", lambda V, xb=xb, par=par, bb=bb, h=h: V.tensor_scalar(
                xn_bf[par][:], xb, bnst[:, bb+12:bb+13], bnst[:, bb+14:bb+15],
                ALU.subtract, ALU.mult).then_inc(h, 1))
            xn_ready_at[cc] = P.cnt["dve"]
            h = P.inc("dve")
            E("vector", lambda V, par=par, cc=cc, h=h: V.tensor_reduce(
                xamaxc[:, cc:cc+1], xn_bf[par][:], AX.X, ALU.max, apply_absolute_value=True).then_inc(h, 1))

        xnT_evac_at = {}

        def emit_xnT(cc):
            par = cc % 2
            pe_wait_bank(7)
            P.wait("tensor", "dve", xn_ready_at[cc])
            pe_mm_group([
                (lambda T, f=f, par=par: T.transpose(pT[:, f*128:(f+1)*128],
                                                     xn_bf[par][:, f*128:(f+1)*128], c_id[:]))
                for f in range(NBK)])
            trsp_done_at[cc] = P.cnt["pe"]
            P.wait("scalar", "pe")
            if (cc - 2) in xnT_spill_at:
                P.wait("scalar", "dS", xnT_spill_at[cc - 2])
            h = P.inc("act")
            E("scalar", lambda S, par=par, h=h: S.copy(xnT_s[par][:], pT[:]).then_inc(h, 1))
            xnT_evac_at[cc] = P.cnt["act"]
            bank_free[7] = ("act", xnT_evac_at[cc])
            P.wait("sync", "act", xnT_evac_at[cc])
            h = P.inc("dS", 16)
            E("sync", lambda S, cc=cc, par=par, h=h: S.dma_start(
                out=xnT_d.rearrange("(b p) T -> p b T", p=128)[:, :, cc*128:(cc+1)*128],
                in_=xnT_s[par].rearrange("p (b t) -> p b t", t=128)).then_inc(h, 16))
            xnT_spill_at[cc] = P.cnt["dS"]

        emit_xload(0)
        emit_ln(0)
        emit_xnT(0)

        for c in range(NCH):
            par = c % 2
            xb = xbuf[c % 4]
            mc = c % 4
            if (c - 2) in chunk_end:
                pe2 = chunk_end[c - 2]
                for eng in ("tensor", "vector", "scalar", "gpsimd"):
                    for semn in ("pe", "act", "dve", "pl"):
                        P.wait(eng, semn, pe2[semn])
            if mc < 3 and c + 1 < NCH:
                emit_xload(c + 1)

            pe_wait_bank(4)
            P.wait("tensor", "act", xnT_evac_at[c])
            z_done = pe_mm_group([
                (lambda T, k=k, par=par: T.matmul(pb[4][:, 0:16], xnT_s[par][:, k*128:(k+1)*128],
                                                  WGT[:, k*16:(k+1)*16],
                                                  start=(k == 0), stop=(k == NBK - 1)))
                for k in range(NBK)])

            def proj_mms(W_s, banks):
                return pe_mm_group([
                    (lambda T, k=k, nh_=nh_, W_s=W_s, par=par, b=banks[nh_]: T.matmul(
                        pb[b][:], xnT_s[par][:, k*128:(k+1)*128],
                        W_s[:, k*D + nh_*512: k*D + nh_*512 + 512],
                        start=(k == 0), stop=(k == NBK - 1)))
                    for k in range(NBK) for nh_ in range(2)])

            gg = g16[par]
            P.wait("vector", "pe", z_done)
            h = P.inc("dve")
            E("vector", lambda V, gg=gg, h=h: V.tensor_single_scalar(
                gg[:, 0:16], pb[4][:, 0:16], 0.0, ALU.min).then_inc(h, 1))
            P.wait("scalar", "pe", z_done)
            h = P.inc("act")
            E("scalar", lambda S, gg=gg, h=h: S.activation(gg[:, 16:32], pb[4][:, 0:16], AF.Abs).then_inc(h, 1))
            P.wait("scalar", "act")
            h = P.inc("act")
            E("scalar", lambda S, gg=gg, h=h: S.activation(gg[:, 32:48], gg[:, 16:32], AF.Exp, scale=-1.0).then_inc(h, 1))
            P.wait("scalar", "act")
            h = P.inc("act")
            E("scalar", lambda S, gg=gg, h=h: S.activation(gg[:, 48:64], gg[:, 32:48], AF.Ln, bias=c_cols[:, 0:1]).then_inc(h, 1))
            P.wait("vector", "act")
            P.wait("vector", "dve")
            h = P.inc("dve")
            E("vector", lambda V, gg=gg, h=h: V.tensor_tensor(gg[:, 64:80], gg[:, 0:16], gg[:, 48:64], ALU.subtract).then_inc(h, 1))
            gt_ready = P.cnt["dve"]

            projd = {}
            pe_wait_bank(0); pe_wait_bank(1)
            projd[0] = proj_mms(WQ, (0, 1))

            P.wait("tensor", "dve", gt_ready)
            gmm_done = pe_mm_group([
                (lambda T, gg=gg: T.matmul(pb[4][:, 16:32], c_triu[:], gg[:, 64:80], start=True, stop=True)),
                (lambda T, gg=gg: T.matmul(pb[4][:, 32:48], c_slo[:], gg[:, 64:80], start=True, stop=True)),
                (lambda T, gg=gg: T.matmul(pb[4][0:1, 48:64], c_ones16[:], gg[:, 64:80], start=True, stop=True)),
            ])
            P.wait("scalar", "pe", gmm_done)
            if (c - 2) in ebc_done_at:
                P.wait("scalar", "pe", ebc_done_at[c - 2])
            h = P.inc("act")
            E("scalar", lambda S, gg=gg, h=h: S.activation(gg[:, 80:96], pb[4][:, 32:48], AF.Exp, bias=c_cols[:, 3:4]).then_inc(h, 1))
            h = P.inc("act")
            E("scalar", lambda S, gg=gg, h=h: S.activation(gg[:, 96:112], pb[4][:, 32:48], AF.Exp, scale=-1.0).then_inc(h, 1))
            h = P.inc("act")
            E("scalar", lambda S, par=par, h=h: S.activation(expBrow[par][:], pb[4][0:1, 48:64], AF.Exp).then_inc(h, 1))
            exps_done = P.cnt["act"]

            pe_wait_bank(2); pe_wait_bank(3)
            projd[1] = proj_mms(WK, (2, 3))

            P.wait("tensor", "act", exps_done)
            ebc = pe_mm_group([
                (lambda T, par=par: T.matmul(pb[4][0:128, 64:80], c_ones1[0:1, 0:128],
                                             expBrow[par][:], start=True, stop=True))])
            ebc_done_at[c] = ebc
            P.wait("scalar", "pe", ebc)
            h = P.inc("act")
            E("scalar", lambda S, par=par, h=h: S.copy(expBc[par][:], pb[4][0:128, 64:80]).then_inc(h, 1))
            ebc_copy_at = P.cnt["act"]
            bank_free[4] = ("act", ebc_copy_at)

            pe_wait_bank(5); pe_wait_bank(6)
            vd = pe_mm_group([
                (lambda T, k=k, nh_=nh_, par=par, b=(5, 6)[nh_]: T.matmul(
                    pb[b][:], xnT_s[par][:, k*128:(k+1)*128],
                    WV[:, k*D + nh_*512: k*D + nh_*512 + 512],
                    start=(k == 0), stop=(k == NBK - 1)))
                for k in range(NBK) for nh_ in range(2)])

            proj_evac = {}
            rq_slots = {0: 9, 1: 10}
            for pi, banks in [(0, (0, 1)), (1, (2, 3))]:
                bb = pi * 16
                P.wait("vector", "pe", projd[pi])
                if (pi, c - 1) in rsq_act_at:
                    P.wait("vector", "act", rsq_act_at[(pi, c - 1)])
                for half in range(2):
                    h = P.inc("dve")
                    E("vector", lambda V, b=banks[half], half=half, bb=bb, h=h: V.bn_stats(
                        bnst[:, bb + half*6: bb + half*6 + 6], pb[b][:]).then_inc(h, 1))
                P.wait("vector", "dve")
                h = P.inc("dve")
                E("vector", lambda V, bb=bb, h=h: V.bn_aggr(
                    bnst[:, bb+12:bb+14], bnst[:, bb:bb+12]).then_inc(h, 1))
                P.wait("vector", "dve")
                h = P.inc("dve")
                E("vector", lambda V, bb=bb, h=h: V.tensor_scalar(
                    bnst[:, bb+14:bb+15], bnst[:, bb+12:bb+13],
                    bnst[:, bb+12:bb+13], bnst[:, bb+13:bb+14],
                    ALU.mult, ALU.add).then_inc(h, 1))
                P.wait("scalar", "dve")
                h = P.inc("act")
                E("scalar", lambda S, bb=bb, h=h: S.activation(
                    bnst[:, bb+15:bb+16], bnst[:, bb+14:bb+15],
                    AF.Ln, bias=c_cols[:, 2:3]).then_inc(h, 1))
                P.wait("scalar", "act")
                h = P.inc("act")
                E("scalar", lambda S, pi=pi, par=par, bb=bb, h=h: S.activation(
                    scal_t[:, par*16+rq_slots[pi]:par*16+rq_slots[pi]+1],
                    bnst[:, bb+15:bb+16], AF.Exp, scale=-0.5).then_inc(h, 1))
                rstd_done = P.cnt["act"]
                rsq_act_at[(pi, c)] = rstd_done
                P.wait("vector", "act", max(rstd_done, exps_done))
                src_off = 80 if pi == 0 else 96
                dst_off = 112 if pi == 0 else 128
                h = P.inc("dve")
                E("vector", lambda V, gg=gg, src_off=src_off, dst_off=dst_off, pi=pi, par=par, h=h: V.tensor_scalar(
                    gg[:, dst_off:dst_off+16], gg[:, src_off:src_off+16],
                    scal_t[:, par*16+rq_slots[pi]:par*16+rq_slots[pi]+1], None, ALU.mult).then_inc(h, 1))
                P.wait("vector", "dve")
                dst = qe_bf if pi == 0 else ke_bf
                for half in range(2):
                    h = P.inc("dve")
                    def tt(V, dst=dst, b=banks[half], half=half, par=par, gg=gg, dst_off=dst_off, h=h):
                        p3 = pb[b][:].rearrange("q (g e) -> q g e", e=64)
                        s3 = gg[:, dst_off + half*8: dst_off + half*8 + 8].rearrange("q (g e) -> q g e", e=1)
                        a3, s3b = bass.broadcast_tensor_aps(p3, s3)
                        d3 = dst[par][:, half*512:(half+1)*512].rearrange("q (g e) -> q g e", e=64)
                        return V.tensor_tensor(d3, a3, s3b, ALU.mult).then_inc(h, 1)
                    E("vector", tt)
                bank_free[banks[0]] = ("dve", P.cnt["dve"] - 1)
                bank_free[banks[1]] = ("dve", P.cnt["dve"])
                proj_evac[pi] = P.cnt["dve"]
            qk_evac = P.cnt["dve"]

            P.wait("vector", "pe", vd)
            if ("v", c - 1) in rsq_act_at:
                P.wait("vector", "act", rsq_act_at[("v", c - 1)])
            for half in range(2):
                h = P.inc("dve")
                E("vector", lambda V, half=half, b=(5, 6)[half], h=h: V.bn_stats(
                    bnst[:, 32 + half*6: 32 + half*6 + 6], pb[b][:]).then_inc(h, 1))
            P.wait("vector", "dve")
            h = P.inc("dve")
            E("vector", lambda V, h=h: V.bn_aggr(bnst[:, 44:46], bnst[:, 32:44]).then_inc(h, 1))
            P.wait("vector", "dve")
            h = P.inc("dve")
            E("vector", lambda V, h=h: V.tensor_scalar(
                bnst[:, 46:47], bnst[:, 44:45], bnst[:, 44:45], bnst[:, 45:46],
                ALU.mult, ALU.add).then_inc(h, 1))
            P.wait("scalar", "dve")
            h = P.inc("act")
            E("scalar", lambda S, h=h: S.activation(
                bnst[:, 47:48], bnst[:, 46:47],
                AF.Ln, bias=c_cols[:, 2:3]).then_inc(h, 1))
            P.wait("scalar", "act")
            h = P.inc("act")
            E("scalar", lambda S, par=par, h=h: S.activation(
                scal_t[:, par*16+11:par*16+12], bnst[:, 47:48],
                AF.Exp, scale=-0.5).then_inc(h, 1))
            rsq_act_at[("v", c)] = P.cnt["act"]
            P.wait("scalar", "act")
            for half in range(2):
                h = P.inc("act")
                E("scalar", lambda S, half=half, b=(5, 6)[half], par=par, h=h: S.activation(
                    v_bf[par][:, half*512:(half+1)*512], pb[b][:], AF.Copy,
                    scale=scal_t[:, par*16+11:par*16+12]).then_inc(h, 1))
            bank_free[5] = ("act", P.cnt["act"] - 1)
            bank_free[6] = ("act", P.cnt["act"])
            v_evac = P.cnt["act"]
            scal_free_at[c] = ("act", v_evac)


            pe_wait_bank(2); pe_wait_bank(3)
            gd = pe_mm_group([
                (lambda T, k=k, nh_=nh_, par=par, b=(2, 3)[nh_]: T.matmul(
                    pb[b][:], xnT_s[par][:, k*128:(k+1)*128],
                    WG[:, k*D + nh_*512: k*D + nh_*512 + 512],
                    start=(k == 0), stop=(k == NBK - 1)))
                for k in range(NBK) for nh_ in range(2)])
            P.wait("scalar", "pe", gd)
            for half in range(2):
                h = P.inc("act")
                E("scalar", lambda S, half=half, b=(2, 3)[half], mc=mc, h=h: S.copy(
                    gmac[:, mc*1024 + half*512: mc*1024 + (half+1)*512], pb[b][:]).then_inc(h, 1))
            bank_free[2] = ("act", P.cnt["act"] - 1)
            bank_free[3] = ("act", P.cnt["act"])

            for (src, dstT) in [(qe_bf, qT_s), (ke_bf, kT_s)]:
                for half in range(2):
                    pe_wait_bank(7)
                    P.wait("tensor", "dve", proj_evac[0] if src is qe_bf else proj_evac[1])
                    pe_mm_group([
                        (lambda T, hh=hh, half=half, src=src, par=par: T.transpose(
                            pT[0:64, hh*128:(hh+1)*128],
                            src[par][:, (half*8+hh)*64:(half*8+hh)*64+64], c_id[:]))
                        for hh in range(8)])
                    P.wait("scalar", "pe")
                    h = P.inc("act")
                    E("scalar", lambda S, dstT=dstT, half=half, par=par, h=h: S.copy(
                        dstT[par][0:64, half*1024:(half+1)*1024], pT[0:64, :]).then_inc(h, 1))
                    bank_free[7] = ("act", P.cnt["act"])
            kT_evac = P.cnt["act"]

            if (c - 1) in o_done_at:
                P.wait("gpsimd", "pe", o_done_at[c - 1])
            P.wait("gpsimd", "act", ebc_copy_at)
            if (c - 1) in s_upd_at:
                P.wait("gpsimd", "dve", s_upd_at[c - 1])
            for hd in range(NH):
                if hd < NH - 1:
                    E("gpsimd", lambda G, hd=hd, par=par: G.tensor_scalar(
                        Sdec[:, hd*64:(hd+1)*64], S_f[:, hd*64:(hd+1)*64],
                        expBc[par][0:64, hd:hd+1], None, ALU.mult))
                else:
                    h = P.inc("pl")
                    E("gpsimd", lambda G, hd=hd, par=par, h=h: G.tensor_scalar(
                        Sdec[:, hd*64:(hd+1)*64], S_f[:, hd*64:(hd+1)*64],
                        expBc[par][0:64, hd:hd+1], None, ALU.mult).then_inc(h, 1))
            P.wait("gpsimd", "pl")
            h = P.inc("pl")
            E("gpsimd", lambda G, h=h: G.tensor_copy(Sdec_bf[0:64, :], Sdec[:]).then_inc(h, 1))
            sdec_done = P.cnt["pl"]

            P.wait("tensor", "act", kT_evac)
            at_last = 0
            for gidx in range(4):
                ab = 5 + (gidx % 2)
                pe_wait_bank(ab)
                agrp = pe_mm_group([
                    (lambda T, hd=gidx*4+hh, hh=hh, ab=ab, par=par: T.matmul(
                        pb[ab][:, hh*128:(hh+1)*128],
                        kT_s[par][:, hd*128:(hd+1)*128], qT_s[par][:, hd*128:(hd+1)*128],
                        start=True, stop=True))
                    for hh in range(4)])
                P.wait("vector", "pe", agrp)
                h = P.inc("dve")
                E("vector", lambda V, gidx=gidx, ab=ab, par=par, h=h: V.tensor_tensor(
                    AT_bf[par][:, gidx*512:(gidx+1)*512], pb[ab][:], c_mask4[:], ALU.mult).then_inc(h, 1))
                bank_free[ab] = ("dve", P.cnt["dve"])
                at_last = P.cnt["dve"]

            for wave in range(2):
                pe_wait_bank(4)
                P.wait("tensor", "dve", qk_evac)
                P.wait("tensor", "act", v_evac)
                kvd = pe_mm_group([
                    (lambda T, hd=wave*8+hh, hh=hh, par=par: T.matmul(
                        pb[4][0:64, hh*64:(hh+1)*64],
                        ke_bf[par][:, hd*64:(hd+1)*64], v_bf[par][:, hd*64:(hd+1)*64],
                        start=True, stop=True))
                    for hh in range(8)])
                P.wait("vector", "pe", kvd)
                P.wait("vector", "pl", sdec_done)
                h = P.inc("dve")
                E("vector", lambda V, wave=wave, h=h: V.tensor_tensor(
                    S_f[:, wave*512:(wave+1)*512], Sdec[:, wave*512:(wave+1)*512],
                    pb[4][0:64, 0:512], ALU.add).then_inc(h, 1))
                bank_free[4] = ("dve", P.cnt["dve"])

            s_upd_at[c] = P.cnt["dve"]

            if mc < 3 and c + 1 < NCH:
                emit_ln(c + 1)

            pe_wait_bank(0); pe_wait_bank(1)
            P.wait("tensor", "dve", at_last)
            P.wait("tensor", "pl", sdec_done)
            o_half = {}
            for ob in range(2):
                omms = []
                for hd in range(ob*8, ob*8 + 8):
                    off = (hd % 8) * 64
                    omms.append(lambda T, hd=hd, ob=ob, off=off, par=par: T.matmul(
                        pb[ob][:, off:off+64], AT_bf[par][:, hd*128:(hd+1)*128],
                        v_bf[par][:, hd*64:(hd+1)*64], start=True, stop=False))
                    omms.append(lambda T, hd=hd, ob=ob, off=off, par=par: T.matmul(
                        pb[ob][:, off:off+64], qT_s[par][:, hd*128:(hd+1)*128],
                        Sdec_bf[:, hd*64:(hd+1)*64], start=False, stop=True))
                o_half[ob] = pe_mm_group(omms)
            o_done = o_half[1]
            o_done_at[c] = o_done

            if mc == 0 and tail_og3_at is not None:
                P.wait("scalar", "dve", tail_og3_at)
            for half in range(2):
                P.wait("scalar", "pe", o_half[half])
                h = P.inc("act")
                E("scalar", lambda S, half=half, h=h: S.activation(
                    osq[:, half*512:(half+1)*512], pb[half][:], AF.Square).then_inc(h, 1))
            P.wait("vector", "act")
            h = P.inc("dve")
            E("vector", lambda V, par=par, h=h: V.tensor_reduce(
                rst16[par][:, 16:32].rearrange("q (g o) -> q g o", o=1),
                osq[:].rearrange("q (g e) -> q g e", e=64), AX.X, ALU.add).then_inc(h, 1))
            P.wait("scalar", "dve")
            h = P.inc("act")
            E("scalar", lambda S, par=par, h=h: S.activation(
                rst16[par][:, 0:16], rst16[par][:, 16:32], AF.Ln,
                scale=1.0 / DH, bias=c_cols[:, 2:3]).then_inc(h, 1))
            P.wait("scalar", "act")
            h = P.inc("act")
            E("scalar", lambda S, par=par, h=h: S.activation(
                rst16[par][:, 0:16], rst16[par][:, 0:16], AF.Exp, scale=-0.5).then_inc(h, 1))
            rstd_done = P.cnt["act"]
            P.wait("vector", "act", rstd_done)
            for half in range(2):
                h = P.inc("dve")
                def tt2(V, half=half, par=par, mc=mc, h=h):
                    p3 = pb[half][:].rearrange("q (g e) -> q g e", e=64)
                    s3 = rst16[par][:, half*8:half*8+8].rearrange("q (g e) -> q g e", e=1)
                    a3, s3b = bass.broadcast_tensor_aps(p3, s3)
                    d3 = orms[:, mc*1024 + half*512: mc*1024 + (half+1)*512].rearrange("q (g e) -> q g e", e=64)
                    return V.tensor_tensor(d3, a3, s3b, ALU.mult).then_inc(h, 1)
                E("vector", tt2)
            bank_free[0] = ("dve", P.cnt["dve"] - 1)
            bank_free[1] = ("dve", P.cnt["dve"])

            if mc < 3 and c + 1 < NCH:
                emit_xnT(c + 1)

            chunk_end[c] = {k: P.cnt[k] for k in ("pe", "act", "dve", "pl")}

            if mc == 3:
                m0 = c - 3
                gbuf = [FA[:, 4096:5120], FA[:, 10240:11264],
                        FA[:, 11264:12288], osq]
                ogbs = [ogb, BA[:, 71808:72832]]
                ogTs = [ogT_s, BA[:, 69632:70656]]
                P.wait("scalar", "dve")
                silu_at = {}
                for cc2 in range(4):
                    h = P.inc("act")
                    E("scalar", lambda S, cc2=cc2, h=h: S.activation(
                        gbuf[cc2][:], gmac[:, cc2*1024:(cc2+1)*1024], AF.Silu).then_inc(h, 1))
                    silu_at[cc2] = P.cnt["act"]
                trsp_tail = {}
                wo_tail = {}
                for cc2 in range(4):
                    cglob = m0 + cc2
                    P.wait("vector", "act", silu_at[cc2])
                    if (cc2 - 2) in trsp_tail:
                        P.wait("vector", "pe", trsp_tail[cc2 - 2])
                    h = P.inc("dve")
                    E("vector", lambda V, cc2=cc2, h=h: V.tensor_tensor(
                        ogbs[cc2 % 2][:], orms[:, cc2*1024:(cc2+1)*1024],
                        gbuf[cc2][:], ALU.mult).then_inc(h, 1))
                    og_done = P.cnt["dve"]
                    if cc2 == 3:
                        tail_og3_at = og_done
                    pe_wait_bank(7)
                    P.wait("tensor", "dve", og_done)
                    pe_mm_group([
                        (lambda T, f=f, cc2=cc2: T.transpose(pT[:, f*128:(f+1)*128],
                                                             ogbs[cc2 % 2][:, f*128:(f+1)*128], c_id[:]))
                        for f in range(NBK)])
                    trsp_tail[cc2] = P.cnt["pe"]
                    P.wait("scalar", "pe", trsp_tail[cc2])
                    h = P.inc("act")
                    E("scalar", lambda S, cc2=cc2, h=h: S.copy(ogTs[cc2 % 2][:], pT[:]).then_inc(h, 1))
                    ogT_evac = P.cnt["act"]
                    bank_free[7] = ("act", ogT_evac)
                    wbk = (2, 3) if cc2 % 2 == 0 else (5, 6)
                    pe_wait_bank(wbk[0]); pe_wait_bank(wbk[1])
                    P.wait("tensor", "act", ogT_evac)
                    wo_done = pe_mm_group([
                        (lambda T, k=k, nh_=nh_, cc2=cc2, wbk=wbk: T.matmul(
                            pb[wbk[nh_]][:], ogTs[cc2 % 2][:, k*128:(k+1)*128],
                            WO[:, k*D + nh_*512: k*D + nh_*512 + 512],
                            start=(k == 0), stop=(k == NBK - 1)))
                        for k in range(NBK) for nh_ in range(2)])
                    wo_tail[cc2] = wo_done
                    P.wait("vector", "pe", wo_done)
                    if (cglob - 2) in acc_dma_at:
                        P.wait("vector", "dS", acc_dma_at[cglob - 2])
                    for nh_ in range(2):
                        h = P.inc("dve")
                        E("vector", lambda V, nh_=nh_, cglob=cglob, wbk=wbk, h=h: V.scalar_tensor_tensor(
                            accst[cglob % 2][:, nh_*512:(nh_+1)*512], pb[wbk[nh_]][:], 1.0,
                            xbuf[cglob % 4][:, nh_*512:(nh_+1)*512], ALU.mult, ALU.add).then_inc(h, 1))
                    acc_done = P.cnt["dve"]
                    bank_free[wbk[0]] = ("dve", acc_done - 1)
                    bank_free[wbk[1]] = ("dve", acc_done)
                    acc_evac_at[cglob] = acc_done
                    P.wait("sync", "dve", acc_done)
                    h = P.inc("dS", 16)
                    E("sync", lambda S, cglob=cglob, h=h: S.dma_start(
                        out=acc_d.rearrange("(c p) d -> c p d", p=128)[cglob],
                        in_=accst[cglob % 2][:]).then_inc(h, 16))
                    acc_dma_at[cglob] = P.cnt["dS"]
                chunk_end[c] = {k: P.cnt[k] for k in ("pe", "act", "dve", "pl")}
                if c + 1 < NCH:
                    emit_xload(c + 1)
                    emit_ln(c + 1)
                    emit_xnT(c + 1)

        for eng in ("tensor", "vector", "scalar", "sync", "gpsimd"):
            for semn in ("pe", "act", "dve", "pl"):
                P.wait(eng, semn)
        P.wait("vector", "dS")
        P.wait("tensor", "dS")
        P.wait("scalar", "dS")

        h = P.inc("dve")
        E("vector", lambda V, h=h: V.tensor_reduce(xcol[:], xamaxc[:], AX.X, ALU.max).then_inc(h, 1))
        P.wait("sync", "dve")
        h = P.inc("dS", 16)
        E("sync", lambda S, h=h: S.dma_start(out=fold_bounce.rearrange("o p -> p o"), in_=xcol[:]).then_inc(h, 16))
        P.wait("sync", "dS")
        h = P.inc("dS", 16)
        E("sync", lambda S, h=h: S.dma_start(out=xrow, in_=fold_bounce).then_inc(h, 16))
        P.wait("vector", "dS")
        h = P.inc("dve")
        E("vector", lambda V, h=h: V.memset(ccs, 0.0).then_inc(h, 1))
        P.wait("vector", "dve")
        h = P.inc("dve")
        E("vector", lambda V, h=h: V.tensor_reduce(ccs[0:1, 0:1], xrow, AX.X, ALU.max).then_inc(h, 1))
        P.wait("gpsimd", "dve")
        h = P.inc("dG", 16)
        E("gpsimd", lambda G, h=h: G.dma_start(out=cc1_in.ap(), in_=ccs).then_inc(h, 16))
        P.wait("gpsimd", "dG")
        h = P.inc("cc")
        E("gpsimd", lambda G, h=h: G.collective_compute(
            "AllReduce", ALU.max, replica_groups=[list(range(8))],
            ins=[cc1_in.ap().opt()], outs=[cc1_out.ap().opt()]).then_inc(h, 1))
        P.wait("gpsimd", "cc")
        h = P.inc("dG", 16)
        E("gpsimd", lambda G, h=h: G.dma_start(out=gs1, in_=cc1_out.ap()).then_inc(h, 16))
        P.wait("vector", "dG")
        h = P.inc("dve")
        E("vector", lambda V, h=h: V.tensor_scalar_mul(sc_small[0:1, 0:1], gs1[0:1, 0:1], 1.0 / 32767.0).then_inc(h, 1))
        P.wait("tensor", "dve")
        pe_wait_bank(4)
        rsxmm = pe_mm_group([
            (lambda T: T.matmul(pb[4][:, 0:1], c_ones1[:], sc_small[0:1, 0:1], start=True, stop=True))])
        P.wait("scalar", "pe", rsxmm)
        h = P.inc("act")
        E("scalar", lambda S, h=h: S.copy(r_sx_col[:], pb[4][:, 0:1]).then_inc(h, 1))
        bank_free[4] = ("act", P.cnt["act"])
        P.wait("vector", "act")
        h = P.inc("dve")
        E("vector", lambda V, h=h: V.reciprocal(s_x_col[:], r_sx_col[:]).then_inc(h, 1))
        P.wait("vector", "dve")

        qdone = {}
        ldone = {}
        for i in range(2 * NBK):
            k, hf = i // 2, i % 2
            qe_, qs_ = (("sync", "dS") if i % 2 == 0 else ("scalar", "dA"))
            if (i - 2) in qdone:
                P.wait(qe_, "dve", qdone[i - 2])
            h = P.inc(qs_, 16)
            E(qe_, lambda S, k=k, hf=hf, i=i, h=h: S.dma_start(
                out=wstage[:, (i % 2)*2048:(i % 2)*2048+2048],
                in_=Wv_(W_in, k)[:, hf*2048:(hf+1)*2048]).then_inc(h, 16))
            ldone[i] = P.cnt[qs_]
            P.wait("vector", qs_, ldone[i])
            if k == 0:
                h = P.inc("dve")
                E("vector", lambda V, i=i, hf=hf, h=h: V.tensor_tensor(
                    run[:, hf*2048:(hf+1)*2048], wstage[:, (i % 2)*2048:(i % 2)*2048+2048],
                    wstage[:, (i % 2)*2048:(i % 2)*2048+2048], ALU.mult).then_inc(h, 1))
            else:
                E("vector", lambda V, i=i, hf=hf: V.tensor_tensor(
                    swb[:, hf*2048:(hf+1)*2048], wstage[:, (i % 2)*2048:(i % 2)*2048+2048],
                    wstage[:, (i % 2)*2048:(i % 2)*2048+2048], ALU.mult))
                h = P.inc("dve")
                E("vector", lambda V, hf=hf, h=h: V.tensor_tensor(
                    run[:, hf*2048:(hf+1)*2048], run[:, hf*2048:(hf+1)*2048],
                    swb[:, hf*2048:(hf+1)*2048], ALU.max).then_inc(h, 1))
            qdone[i] = P.cnt["dve"]
        for L in (64, 32, 16, 8, 4, 2, 1):
            P.wait("sync", "dve")
            h = P.inc("dS", 16)
            E("sync", lambda S, L=L, h=h: S.dma_start(out=wstage[0:L, :], in_=run[L:2*L, :]).then_inc(h, 16))
            P.wait("vector", "dS")
            h = P.inc("dve")
            E("vector", lambda V, L=L, h=h: V.tensor_tensor(run[0:L, :], run[0:L, :], wstage[0:L, :], ALU.max).then_inc(h, 1))
        P.wait("vector", "dve")
        h = P.inc("dve")
        E("vector", lambda V, h=h: V.tensor_scalar_max(run[0:1, :], run[0:1, :], 1e-16).then_inc(h, 1))
        P.wait("scalar", "dve")
        h = P.inc("act")
        E("scalar", lambda S, h=h: S.activation(run[32:33, :], run[0:1, :], AF.Ln).then_inc(h, 1))
        P.wait("scalar", "act")
        h = P.inc("act")
        E("scalar", lambda S, h=h: S.activation(run[0:1, :], run[32:33, :], AF.Exp, scale=0.5).then_inc(h, 1))
        h = P.inc("act")
        E("scalar", lambda S, h=h: S.activation(wstage[0:1, :], run[32:33, :], AF.Exp, scale=-0.5).then_inc(h, 1))
        P.wait("sync", "act")
        P.wait("tensor", "act")
        h = P.inc("dS", 16)
        E("sync", lambda S, h=h: S.dma_start(out=wq_bounce, in_=run[0:1, :]).then_inc(h, 16))
        P.wait("sync", "dS")
        h = P.inc("dS", 16)
        E("sync", lambda S, h=h: S.dma_start(out=w32[:], in_=wq_bounce.rearrange("o (a p) -> (o a) p", p=128)).then_inc(h, 16))
        P.wait("tensor", "dS")
        pe_wait_bank(3)
        t32 = pe_mm_group([(lambda T: T.transpose(pb[3][:, 0:32], w32[:], c_idf[0:32, 0:32]))])
        P.wait("scalar", "pe", t32)
        h = P.inc("act")
        E("scalar", lambda S, h=h: S.copy(sc32amax[:], pb[3][:, 0:32]).then_inc(h, 1))
        bank_free[3] = ("act", P.cnt["act"])
        P.wait("vector", "act")
        h = P.inc("dve")
        E("vector", lambda V, h=h: V.tensor_scalar(sc32[:], sc32amax[:], r_sx_col[:], 1.0 / 127.0, ALU.mult, ALU.mult).then_inc(h, 1))
        for cq in range(8):
            bk = cq % 2
            pe_wait_bank(bk)
            h = P.inc("pe")
            E("tensor", lambda T, cq=cq, bk=bk, h=h: T.matmul(
                pb[bk][:], c_c127[:], wstage[0:1, cq*512:(cq+1)*512], start=True, stop=True).then_inc(h, 1))
            P.wait("scalar", "pe")
            h = P.inc("act")
            E("scalar", lambda S, cq=cq, bk=bk, h=h: S.copy(swb[:, cq*512:(cq+1)*512], pb[bk][:]).then_inc(h, 1))
            bank_free[bk] = ("act", P.cnt["act"])
        qdone = {}
        for i in range(2 * NBK):
            k, hf = i // 2, i % 2
            qe_, qs_ = (("sync", "dS") if i % 2 == 0 else ("scalar", "dA"))
            if i <= 1:
                P.wait(qe_, "act")
            if (i - 2) in qdone:
                P.wait(qe_, "dve", qdone[i - 2])
            h = P.inc(qs_, 16)
            E(qe_, lambda S, k=k, hf=hf, i=i, h=h: S.dma_start(
                out=wstage[:, (i % 2)*2048:(i % 2)*2048+2048],
                in_=Wv_(W_in, k)[:, hf*2048:(hf+1)*2048]).then_inc(h, 16))
            ld = P.cnt[qs_]
            P.wait("vector", qs_, ld)
            if i == 0:
                P.wait("vector", "act")
            E("vector", lambda V, i=i, hf=hf: V.tensor_tensor(
                run[:, hf*2048:(hf+1)*2048], wstage[:, (i % 2)*2048:(i % 2)*2048+2048],
                swb[:, hf*2048:(hf+1)*2048], ALU.mult))
            h = P.inc("dve")
            E("vector", lambda V, k=k, hf=hf, h=h: V.tensor_scalar(
                arenaA[:, k*DFF + hf*2048: k*DFF + hf*2048 + 2048], run[:, hf*2048:(hf+1)*2048],
                MAGIC, MAGIC, ALU.add, ALU.subtract).then_inc(h, 1))
            qdone[i] = P.cnt["dve"]

        for b in range(NBK):
            h = P.inc("dS", 16)
            E("sync", lambda S, b=b, h=h: S.dma_start(
                out=arenaB[:, b*NT:(b+1)*NT],
                in_=xnT_d.rearrange("(b p) T -> b p T", p=128)[b]).then_inc(h, 16))
        P.wait("vector", "dS")
        for b in range(NBK):
            E("vector", lambda V, b=b: V.tensor_scalar(
                run[:], arenaB[:, b*NT:(b+1)*NT], s_x_col[:], MAGIC, ALU.mult, ALU.add))
            h = P.inc("dve")
            E("vector", lambda V, b=b, h=h: V.tensor_single_scalar(
                arenaB[:, b*NT:(b+1)*NT], run[:], MAGIC, ALU.subtract).then_inc(h, 1))
        xq_ready = P.cnt["dve"]
        P.wait("tensor", "dve", xq_ready)
        P.wait("tensor", "act")
        E("vector", lambda V: V.memset(hqc[:], 0.0))
        evac_done_at = {}
        dve_red_at = {}
        spill_at = {}
        for quarter in range(4):
            tok0 = quarter * 1024
            for m in range(NBF):
                g = m % 2
                gm = quarter * NBF + m
                if gm >= 2:
                    P.wait("tensor", "act", evac_done_at[gm - 2])
                mml = []
                for k in range(NBK):
                    for t in range(2):
                        mml.append(lambda T, k=k, t=t, m=m, g=g, tok0=tok0: T.matmul(
                            pb[2*g + t][:],
                            arenaA[:, k*DFF + m*128: k*DFF + m*128 + 128],
                            arenaB[:, k*NT + tok0 + t*512: k*NT + tok0 + t*512 + 512],
                            start=(k == 0), stop=(k == NBK - 1)))
                mmd = pe_mm_group(mml)
                P.wait("scalar", "pe", mmd)
                if gm >= 2:
                    P.wait("scalar", "dve", dve_red_at[gm - 2])
                    P.wait("scalar", "dS", spill_at[gm - 2])
                for t in range(2):
                    h = P.inc("act")
                    E("scalar", lambda S, t=t, m=m, g=g, h=h: S.activation(
                        hstage[g][:, t*512:(t+1)*512], pb[2*g + t][:], AF.Gelu,
                        bias=s_bic[:, m:m+1], scale=sc32[:, m:m+1]).then_inc(h, 1))
                evac_done_at[gm] = P.cnt["act"]
                P.wait("vector", "act", evac_done_at[gm])
                h = P.inc("dve")
                E("vector", lambda V, g=g, m=m, h=h: V.tensor_reduce(
                    run[:, m:m+1], hstage[g][:], AX.X, ALU.max,
                    apply_absolute_value=True).then_inc(h, 1))
                P.wait("vector", "dve")
                h = P.inc("dve")
                E("vector", lambda V, m=m, h=h: V.scalar_tensor_tensor(
                    hqc[:, m:m+1], run[:, m:m+1], 1.0, hqc[:, m:m+1], ALU.mult, ALU.max).then_inc(h, 1))
                dve_red_at[gm] = P.cnt["dve"]
                P.wait("sync", "act", evac_done_at[gm])
                h = P.inc("dS", 16)
                E("sync", lambda S, m=m, g=g, tok0=tok0, h=h: S.dma_start(
                    out=hidT.rearrange("(b p) n -> b p n", p=128)[m][:, tok0:tok0+1024],
                    in_=hstage[g][:]).then_inc(h, 16))
                spill_at[gm] = P.cnt["dS"]

        P.wait("vector", "dve")
        h = P.inc("dve")
        E("vector", lambda V, h=h: V.tensor_reduce(xcol[:], hqc[:], AX.X, ALU.max).then_inc(h, 1))
        P.wait("sync", "dve")
        h = P.inc("dS", 16)
        E("sync", lambda S, h=h: S.dma_start(out=fold_bounce.rearrange("o p -> p o"), in_=xcol[:]).then_inc(h, 16))
        P.wait("sync", "dS")
        h = P.inc("dS", 16)
        E("sync", lambda S, h=h: S.dma_start(out=xrow, in_=fold_bounce).then_inc(h, 16))
        P.wait("vector", "dS")
        h = P.inc("dve")
        E("vector", lambda V, h=h: V.memset(ccs, 0.0).then_inc(h, 1))
        P.wait("vector", "dve")
        h = P.inc("dve")
        E("vector", lambda V, h=h: V.tensor_reduce(ccs[0:1, 0:1], xrow, AX.X, ALU.max).then_inc(h, 1))
        P.wait("gpsimd", "dve")
        h = P.inc("dG", 16)
        E("gpsimd", lambda G, h=h: G.dma_start(out=cc2_in.ap(), in_=ccs).then_inc(h, 16))
        P.wait("gpsimd", "dG")
        h = P.inc("cc")
        E("gpsimd", lambda G, h=h: G.collective_compute(
            "AllReduce", ALU.max, replica_groups=[list(range(8))],
            ins=[cc2_in.ap().opt()], outs=[cc2_out.ap().opt()]).then_inc(h, 1))
        P.wait("gpsimd", "cc")
        h = P.inc("dG", 16)
        E("gpsimd", lambda G, h=h: G.dma_start(out=gs2, in_=cc2_out.ap()).then_inc(h, 16))
        P.wait("vector", "dG")
        h = P.inc("dve")
        E("vector", lambda V, h=h: V.tensor_scalar_mul(sc_small[0:1, 1:2], gs2[0:1, 0:1], 1.0 / 32767.0).then_inc(h, 1))
        P.wait("tensor", "dve")
        pe_wait_bank(4)
        rshmm = pe_mm_group([
            (lambda T: T.matmul(pb[4][:, 0:1], c_ones1[:], sc_small[0:1, 1:2], start=True, stop=True))])
        P.wait("scalar", "pe", rshmm)
        h = P.inc("act")
        E("scalar", lambda S, h=h: S.copy(r_sh_col[:], pb[4][:, 0:1]).then_inc(h, 1))
        bank_free[4] = ("act", P.cnt["act"])
        P.wait("vector", "act")
        h = P.inc("dve")
        E("vector", lambda V, h=h: V.reciprocal(s_h_col[:], r_sh_col[:]).then_inc(h, 1))
        P.wait("vector", "dve")

        P.wait("vector", "pe")
        qdone = {}
        for k in range(NBF):
            qe_, qs_ = (("sync", "dS") if k % 2 == 0 else ("scalar", "dA"))
            if (k - 2) in qdone:
                P.wait(qe_, "dve", qdone[k - 2])
            h = P.inc(qs_, 16)
            E(qe_, lambda S, k=k, h=h: S.dma_start(
                out=wstage[:, (k % 2)*1024:(k % 2)*1024+1024], in_=Wv_(W_out, k)).then_inc(h, 16))
            ld = P.cnt[qs_]
            P.wait("vector", qs_, ld)
            h = P.inc("dve")
            E("vector", lambda V, k=k, h=h: V.tensor_reduce(
                amax2c[:, k:k+1], wstage[:, (k % 2)*1024:(k % 2)*1024+1024],
                AX.X, ALU.max, apply_absolute_value=True).then_inc(h, 1))
            P.wait("vector", "dve")
            h = P.inc("dve")
            E("vector", lambda V, k=k, h=h: V.tensor_scalar_max(amax2c[:, k:k+1], amax2c[:, k:k+1], 1e-8).then_inc(h, 1))
            P.wait("vector", "dve")
            h = P.inc("dve")
            E("vector", lambda V, k=k, h=h: V.reciprocal(rcol[:], amax2c[:, k:k+1]).then_inc(h, 1))
            P.wait("vector", "dve")
            E("vector", lambda V, k=k: V.tensor_scalar(
                run[:, (k % 2)*1024:(k % 2)*1024+1024], wstage[:, (k % 2)*1024:(k % 2)*1024+1024],
                rcol[:], 127.0, ALU.mult, ALU.mult))
            h = P.inc("dve")
            E("vector", lambda V, k=k, h=h: V.tensor_scalar(
                arenaA[:, k*1024:(k+1)*1024], run[:, (k % 2)*1024:(k % 2)*1024+1024],
                MAGIC, MAGIC, ALU.add, ALU.subtract).then_inc(h, 1))
            qdone[k] = P.cnt["dve"]
        h = P.inc("dve")
        E("vector", lambda V, h=h: V.tensor_scalar(sc2[:], amax2c[:], r_sh_col[:], 1.0 / 127.0, ALU.mult, ALU.mult).then_inc(h, 1))
        P.wait("vector", "dve")
        P.wait("sync", "dve")
        h = P.inc("dS", 16)
        E("sync", lambda S, h=h: S.dma_start(out=brow[0:1, :], in_=b_out_d).then_inc(h, 16))
        P.wait("tensor", "dS")
        P.wait("tensor", "dve")
        pe_wait_bank(4); pe_wait_bank(5)
        bmm = pe_mm_group([
            (lambda T, n=n: T.matmul(pb[4 + n][:], c_ones1[:], brow[0:1, n*512:(n+1)*512],
                                     start=True, stop=True)) for n in range(2)])
        P.wait("scalar", "pe", bmm)
        for n in range(2):
            h = P.inc("act")
            E("scalar", lambda S, n=n, h=h: S.copy(boutb[:, n*512:(n+1)*512], pb[4 + n][:]).then_inc(h, 1))
        bank_free[4] = ("act", P.cnt["act"] - 1)
        bank_free[5] = ("act", P.cnt["act"])
        P.wait("vector", "act")

        ld_at = {}
        ev_at = {}
        pe_after_sub = 0
        hdbuf4 = [BA[:, 65536:69632], BA[:, 69632:73728]]
        p3scr2 = FA[:, 1024:3072]
        q_done = {}
        l_done = {}
        p2sp = P.cnt["dS"]
        P.wait("scalar", "dS", p2sp)
        for sub in range(4):
            tok0 = sub * 1024
            if sub > 0:
                P.wait("vector", "pe", pe_after_sub)
            for kb0 in range(8):
                kb = kb0
                gi = sub * 8 + kb
                qe_, qs_ = (("sync", "dS") if gi % 2 == 0 else ("scalar", "dA"))
                if (gi - 2) in q_done:
                    P.wait(qe_, "dve", q_done[gi - 2])
                h = P.inc(qs_, 16)
                E(qe_, lambda S, kb=kb, tok0=tok0, h=h: S.dma_start(
                    out=hdbuf4[kb % 2].rearrange("p (b n) -> p b n", n=1024),
                    in_=hidT.rearrange("(a b p) n -> a p b n", b=4, p=128)[kb][:, :, tok0:tok0+1024]).then_inc(h, 16))
                l_done[gi] = P.cnt[qs_]
                P.wait("vector", qs_, l_done[gi])
                for half in range(2):
                    E("vector", lambda V, kb=kb, half=half: V.tensor_scalar(
                        p3scr2[:], hdbuf4[kb % 2][:, half*2048:(half+1)*2048],
                        s_h_col[:], MAGIC, ALU.mult, ALU.add))
                    for j in range(2):
                        k0 = kb * 4 + half * 2 + j
                        h = P.inc("dve")
                        E("vector", lambda V, k0=k0, j=j, h=h: V.tensor_scalar(
                            arenaB[:, k0*1024:(k0+1)*1024], p3scr2[:, j*1024:(j+1)*1024],
                            MAGIC, sc2[:, k0:k0+1], ALU.subtract, ALU.mult).then_inc(h, 1))
                q_done[gi] = P.cnt["dve"]
            hs_ready = P.cnt["dve"]
            P.wait("tensor", "dve", hs_ready)
            P.wait("tensor", "act")
            for mt in range(8):
                g = mt % 2
                gm = sub * 8 + mt
                cglob = sub * 8 + mt
                if gm >= 2:
                    P.wait("sync", "dve", ev_at[gm - 2])
                h = P.inc("dS", 16)
                E("sync", lambda S, cglob=cglob, g=g, h=h: S.dma_start(
                    out=accbuf[g][:],
                    in_=acc_d.rearrange("(c p) d -> c p d", p=128)[cglob]).then_inc(h, 16))
                accl = P.cnt["dS"]
                if gm >= 2:
                    P.wait("tensor", "dve", ev_at[gm - 2])
                mml = []
                for k in range(NBF):
                    for n in range(2):
                        mml.append(lambda T, k=k, n=n, mt=mt, g=g: T.matmul(
                            pb[2*g + n][:],
                            arenaB[:, k*1024 + mt*128: k*1024 + mt*128 + 128],
                            arenaA[:, k*1024 + n*512: k*1024 + n*512 + 512],
                            start=(k == 0), stop=(k == NBF - 1)))
                mmd = pe_mm_group(mml)
                P.wait("vector", "pe", mmd)
                P.wait("vector", "dS", accl)
                if gm >= 2:
                    P.wait("vector", "dS", ld_at[gm - 2])
                for n in range(2):
                    E("vector", lambda V, n=n, g=g: V.scalar_tensor_tensor(
                        ostage[g][:, n*512:(n+1)*512], pb[2*g + n][:], 1.0,
                        accbuf[g][:, n*512:(n+1)*512], ALU.mult, ALU.add))
                h = P.inc("dve")
                E("vector", lambda V, g=g, h=h: V.tensor_tensor(
                    ostage[g][:], ostage[g][:], boutb[:], ALU.add).then_inc(h, 1))
                ev_at[gm] = P.cnt["dve"]
                P.wait("sync", "dve", ev_at[gm])
                h = P.inc("dS", 16)
                E("sync", lambda S, cglob=cglob, g=g, h=h: S.dma_start(
                    out=out_d.rearrange("(c p) d -> c p d", p=128)[cglob],
                    in_=ostage[g][:]).then_inc(h, 16))
                ld_at[gm] = P.cnt["dS"]
            pe_after_sub = P.cnt["pe"]

        P.wait("sync", "dS")
        P.wait("sync", "dG")
        P.wait("sync", "dA")
        P.wait("gpsimd", "dG")
        P.wait("gpsimd", "dS")
        P.wait("scalar", "dA")

        with nc.Block() as block:
            P.run(block)
    return nc


def _make_consts():
    j = np.arange(128)
    return dict(
        ident_bf=np.eye(128).astype(ml_dtypes.bfloat16),
        ident_f32=np.eye(128).astype(np.float32),
        triu16=(np.where(j[:, None] <= j[None, :], 1.0, 0.0) / 16.0).astype(np.float32),
        slo16=(np.where(j[:, None] > j[None, :], -1.0, 0.0) / 16.0).astype(np.float32),
        onescol16=np.full((128, 1), 1.0 / 16.0, dtype=np.float32),
        ones_1x128=np.ones((1, 128), dtype=np.float32),
        c127_1x128=np.full((1, 128), 127.0, dtype=np.float32),
        mask4=np.tile(np.where(j[:, None] <= j[None, :], 1.0, 0.0).astype(np.float32), (1, 4)),
        c_cols=np.tile(np.array([[1.0, 1e-5, 1e-6, LN8, 0, 0, 0, 0]], np.float32), (128, 1)),
    )


_CACHED = {}


def kernel(L0_attn_x, L0_attn_Wq, L0_attn_Wk, L0_attn_Wv, L0_attn_Wg, L0_attn_Wgt,
           L0_attn_Wo, L0_ffn_W_in, L0_ffn_b_in, L0_ffn_W_out, L0_ffn_b_out):
    C = _make_consts()
    if "nc" not in _CACHED:
        _CACHED["nc"] = build(bass.Bass("TRN2", target_bir_lowering=False))
    nc = _CACHED["nc"]
    in_maps = []
    for c in range(8):
        m = {
            "x": np.ascontiguousarray(np.asarray(L0_attn_x)[c]),
            "Wq": np.asarray(L0_attn_Wq), "Wk": np.asarray(L0_attn_Wk),
            "Wv": np.asarray(L0_attn_Wv), "Wg": np.asarray(L0_attn_Wg),
            "Wgt": np.asarray(L0_attn_Wgt), "Wo": np.asarray(L0_attn_Wo),
            "W_in": np.asarray(L0_ffn_W_in),
            "b_in_cols": np.ascontiguousarray(np.asarray(L0_ffn_b_in).reshape(NBF, 128).T),
            "W_out": np.asarray(L0_ffn_W_out),
            "b_out": np.ascontiguousarray(np.asarray(L0_ffn_b_out).reshape(1, D)),
        }
        m.update(C)
        in_maps.append(m)
    res = run_bass_kernel_spmd(nc, in_maps, core_ids=list(range(8)))
    return np.stack([np.asarray(res.results[c]["out"]) for c in range(8)], axis=0)


# version 37
# speedup vs baseline: 1.0063x; 1.0063x over previous
import contextlib
import numpy as np
import ml_dtypes
import concourse.bass as bass
import concourse.mybir as mybir
from concourse.bass_utils import run_bass_kernel_spmd

F32 = mybir.dt.float32
BF16 = mybir.dt.bfloat16
AF = mybir.ActivationFunctionType
ALU = mybir.AluOpType
AX = mybir.AxisListType
MAGIC = 12582912.0

NT = 4096
D = 1024
DFF = 4096
NH = 16
DH = 64
CH = 128
NCH = NT // CH
NBK = D // 128
NBF = DFF // 128
LN8 = float(np.log(1.0 / 8.0))

ENGINES = ("sync", "gpsimd", "vector", "scalar", "tensor")


class Prog:
    def __init__(self, nc, stack, sem_names):
        self.nc = nc
        self.ops = {e: [] for e in ENGINES}
        self.sem = {n: stack.enter_context(nc.semaphore(n)) for n in sem_names}
        self.cnt = {n: 0 for n in sem_names}

    def emit(self, eng, fn):
        self.ops[eng].append(fn)

    def inc(self, name, k=1):
        self.cnt[name] += k
        return self.sem[name]

    def wait(self, eng, name, val=None):
        v = self.cnt[name] if val is None else val
        hh = self.sem[name]
        self.emit(eng, lambda E, hh=hh, v=v: E.wait_ge(hh, v))

    def run(self, block):
        for eng in ENGINES:
            fns = self.ops[eng]
            if fns:
                getattr(block, eng)(lambda E, fns=fns: [f(E) for f in fns])


def build(nc):
    x_d = nc.dram_tensor("x", [NT, D], F32, kind="ExternalInput").ap()
    wq_d = nc.dram_tensor("Wq", [D, D], F32, kind="ExternalInput").ap()
    wk_d = nc.dram_tensor("Wk", [D, D], F32, kind="ExternalInput").ap()
    wv_d = nc.dram_tensor("Wv", [D, D], F32, kind="ExternalInput").ap()
    wg_d = nc.dram_tensor("Wg", [D, D], F32, kind="ExternalInput").ap()
    wgt_d = nc.dram_tensor("Wgt", [D, NH], F32, kind="ExternalInput").ap()
    wo_d = nc.dram_tensor("Wo", [D, D], F32, kind="ExternalInput").ap()
    W_in = nc.dram_tensor("W_in", [D, DFF], F32, kind="ExternalInput").ap()
    b_in_cols = nc.dram_tensor("b_in_cols", [128, NBF], F32, kind="ExternalInput").ap()
    W_out = nc.dram_tensor("W_out", [DFF, D], F32, kind="ExternalInput").ap()
    b_out_d = nc.dram_tensor("b_out", [1, D], F32, kind="ExternalInput").ap()
    cid_d = nc.dram_tensor("ident_bf", [128, 128], BF16, kind="ExternalInput").ap()
    cidf_d = nc.dram_tensor("ident_f32", [128, 128], F32, kind="ExternalInput").ap()
    ctriu_d = nc.dram_tensor("triu16", [128, 128], F32, kind="ExternalInput").ap()
    cslo_d = nc.dram_tensor("slo16", [128, 128], F32, kind="ExternalInput").ap()
    cones16_d = nc.dram_tensor("onescol16", [128, 1], F32, kind="ExternalInput").ap()
    cones1_d = nc.dram_tensor("ones_1x128", [1, 128], F32, kind="ExternalInput").ap()
    c127_d = nc.dram_tensor("c127_1x128", [1, 128], F32, kind="ExternalInput").ap()
    cmask4_d = nc.dram_tensor("mask4", [128, 512], F32, kind="ExternalInput").ap()
    ccols_d = nc.dram_tensor("c_cols", [128, 8], F32, kind="ExternalInput").ap()
    out_d = nc.dram_tensor("out", [NT, D], F32, kind="ExternalOutput").ap()

    xnT_d = nc.dram_tensor("xnT_spill", [D, NT], BF16, kind="Internal").ap()
    acc_d = nc.dram_tensor("acc_spill", [NT, D], F32, kind="Internal").ap()
    hidT = nc.dram_tensor("hid_spill", [DFF, NT], BF16, kind="Internal").ap()
    wq_bounce = nc.dram_tensor("wq_bounce", [1, DFF], F32, kind="Internal").ap()
    fold_bounce = nc.dram_tensor("fold_bounce", [1, 128], F32, kind="Internal").ap()
    cc1_in = nc.dram_tensor("cc1_in", [1, 128], F32, kind="Internal")
    cc1_out = nc.dram_tensor("cc1_out", [1, 128], F32, kind="Internal")
    cc2_in = nc.dram_tensor("cc2_in", [1, 128], F32, kind="Internal")
    cc2_out = nc.dram_tensor("cc2_out", [1, 128], F32, kind="Internal")

    stack = contextlib.ExitStack()
    with stack:
        def sb(name, shape, dtype=F32):
            return stack.enter_context(nc.sbuf_tensor(name, shape, dtype))

        BA = sb("BA", [128, 75904], BF16)
        FA = sb("FA", [128, 12288], F32)
        WQ = BA[:, 0:8192]; WK = BA[:, 8192:16384]; WV = BA[:, 16384:24576]
        WG = BA[:, 24576:32768]; WO = BA[:, 32768:40960]; WGT = BA[:, 40960:41088]
        xn_bf = [BA[:, 41088:42112], BA[:, 42112:43136]]
        xnT_s = [BA[:, 43136:44160], BA[:, 44160:45184]]
        qe_bf = [BA[:, 45184:46208], BA[:, 46208:47232]]
        ke_bf = [BA[:, 47232:48256], BA[:, 48256:49280]]
        v_bf = [BA[:, 49280:50304], BA[:, 50304:51328]]
        qT_s = [BA[:, 51328:53376], BA[:, 53376:55424]]
        kT_s = [BA[:, 55424:57472], BA[:, 57472:59520]]
        AT_bf = [BA[:, 59520:61568], BA[:, 61568:63616]]
        orms = BA[:, 63616:67712]
        gmac = BA[:, 67712:71808]
        gtmp = BA[:, 71808:72832]
        ogb = BA[:, 72832:73856]
        ogT_s = BA[:, 73856:74880]
        Sdec_bf = BA[:, 74880:75904]
        arenaA = BA[:, 0:32768]
        arenaB = BA[:, 32768:65536]
        hstage = [BA[:, 65536:66560], BA[:, 66560:67584]]
        hdbuf = [BA[:, 67584:68608], BA[:, 68608:69632]]
        xbuf = [FA[:, i*1024:(i+1)*1024] for i in range(4)]
        sqd = FA[:, 4096:5120]
        osq = FA[:, 5120:6144]
        accst = [FA[:, 6144:7168], FA[:, 7168:8192]]
        S_f = FA[0:64, 8192:9216]
        Sdec = FA[0:64, 9216:10240]
        wld = FA[:, 10240:11264]
        lnsq = FA[:, 10240:11264]
        run = FA[:, 0:4096]
        swb = FA[:, 4096:8192]
        wstage = FA[:, 8192:12288]
        ostage = [FA[:, 4096:5120], FA[:, 5120:6144]]
        accbuf = [FA[:, 6144:7168], FA[:, 7168:8192]]
        p3scr = FA[:, 0:1024]
        boutb = FA[:, 8192:9216]
        brow = FA[:, 9216:10240]

        c_id = sb("c_id", [128, 128], BF16)
        c_idf = sb("c_idf", [128, 128], F32)
        c_triu = sb("c_triu", [128, 128], F32)
        c_slo = sb("c_slo", [128, 128], F32)
        c_ones16 = sb("c_ones16", [128, 1], F32)
        c_ones1 = sb("c_ones1", [1, 128], F32)
        c_c127 = sb("c_c127", [1, 128], F32)
        c_mask4 = sb("c_mask4", [128, 512], F32)
        c_cols = sb("s_ccols", [128, 8], F32)
        scal_t = sb("scal_t", [128, 32], F32)
        g16 = [sb(f"g16_{i}", [128, 160], F32) for i in range(2)]
        rst16 = [sb(f"rst16_{i}", [128, 32], F32) for i in range(2)]
        expBc = [sb(f"expBc{i}", [128, 16], F32) for i in range(2)]
        expBrow = [sb(f"expBrow{i}", [1, 16], F32) for i in range(2)]
        xamaxc = sb("xamaxc", [128, 32], F32)
        hqc = sb("hqc", [128, 32], F32)
        amax2c = sb("amax2c", [128, 32], F32)
        sc32amax = sb("sc32amax", [128, 32], F32)
        sc32 = sb("sc32", [128, 32], F32)
        sc2 = sb("sc2", [128, 32], F32)
        s_bic = sb("s_bic", [128, 32], F32)
        rcol = sb("rcol", [128, 1], F32)
        xcol = sb("xcol", [128, 1], F32)
        rows4 = sb("rows4", [1, 512], F32)
        xrow = rows4[0:1, 0:128]; ccs = rows4[0:1, 128:256]
        gs1 = rows4[0:1, 256:384]; gs2 = rows4[0:1, 384:512]
        sc_small = sb("sc_small", [1, 8], F32)
        r_sx_col = sb("r_sx_col", [128, 1], F32)
        s_x_col = sb("s_x_col", [128, 1], F32)
        r_sh_col = sb("r_sh_col", [128, 1], F32)
        s_h_col = sb("s_h_col", [128, 1], F32)
        w32 = sb("w32", [32, 128], F32)
        bnst = sb("bnst", [128, 96], F32)

        def psum(name, shape, dtype=F32):
            return stack.enter_context(nc.psum_tensor(name, shape, dtype))
        pb = [psum(f"pb{i}", [128, 512], F32) for i in range(7)]
        pT = psum("pT", [128, 1024], BF16)

        P = Prog(nc, stack, ["dS", "dG", "dA", "pe", "act", "dve", "cc"])
        E = P.emit

        def Wv_(w, k):
            return w.rearrange("(b p) n -> b p n", p=128)[k]

        def pe_mm_group(mms):
            for i, mm in enumerate(mms):
                if i == len(mms) - 1:
                    h = P.inc("pe")
                    E("tensor", lambda T, mm=mm, h=h: mm(T).then_inc(h, 1))
                else:
                    E("tensor", lambda T, mm=mm: mm(T))
            return P.cnt["pe"]

        bank_free = {i: ("dve", 0) for i in range(8)}

        def pe_wait_bank(i):
            sname, cnt = bank_free[i]
            if cnt > 0:
                P.wait("tensor", sname, cnt)

        for dst, src in [(c_id, cid_d), (c_idf, cidf_d), (c_triu, ctriu_d),
                         (c_slo, cslo_d), (c_ones16, cones16_d), (c_ones1, cones1_d),
                         (c_c127, c127_d), (c_mask4, cmask4_d), (c_cols, ccols_d),
                         (s_bic, b_in_cols)]:
            h = P.inc("dS", 16)
            E("sync", lambda S, dst=dst, src=src, h=h: S.dma_start(out=dst[:], in_=src).then_inc(h, 16))
        E("vector", lambda V: V.memset(xamaxc[:], 0.0))
        E("vector", lambda V: V.memset(S_f[:], 0.0))
        h = P.inc("dve")
        E("vector", lambda V: V.memset(Sdec_bf[:], 0.0))
        E("vector", lambda V, h=h: V.memset(BA[64:128, 51328:59520], 0.0).then_inc(h, 1))
        wld2 = FA[:, 11264:12288]
        wstg = [wld, wld2]
        wjobs = [(WGT, wgt_d, NH, k) for k in range(NBK)]
        wjobs += [(W, wd, D, k) for (W, wd) in
                  [(WQ, wq_d), (WK, wk_d), (WV, wv_d), (WG, wg_d), (WO, wo_d)]
                  for k in range(NBK)]
        w_cast_at = {}
        w_milestones = {}
        for j, (W_s, w_dd, ncols, k) in enumerate(wjobs):
            qe_, qs_ = (("sync", "dS") if j % 2 == 0 else ("scalar", "dA"))
            ce_, cs_ = (("vector", "dve") if j % 2 == 0 else ("scalar", "act"))
            if (j - 2) in w_cast_at:
                P.wait(qe_, w_cast_at[j - 2][0], w_cast_at[j - 2][1])
            h = P.inc(qs_, 16)
            E(qe_, lambda S, w_dd=w_dd, k=k, ncols=ncols, j=j, h=h: S.dma_start(
                out=wstg[j % 2][:, 0:ncols], in_=Wv_(w_dd, k)).then_inc(h, 16))
            P.wait(ce_, qs_, P.cnt[qs_])
            h = P.inc(cs_)
            if ce_ == "vector":
                E(ce_, lambda V, W_s=W_s, k=k, ncols=ncols, j=j, h=h: V.tensor_copy(
                    W_s[:, k*ncols:(k+1)*ncols], wstg[j % 2][:, 0:ncols]).then_inc(h, 1))
            else:
                E(ce_, lambda S, W_s=W_s, k=k, ncols=ncols, j=j, h=h: S.copy(
                    W_s[:, k*ncols:(k+1)*ncols], wstg[j % 2][:, 0:ncols]).then_inc(h, 1))
            w_cast_at[j] = (cs_, P.cnt[cs_])

        acc_evac_at = {}
        scal_free_at = {}
        o_done_at = {}
        xnT_spill_at = {}
        acc_dma_at = {}
        ebc_done_at = {}
        chunk_end = {}
        xn_ready_at = {}
        trsp_done_at = {}
        rsq_act_at = {}
        ln_rsq_at = {}
        tail_og3_at = None

        x_loaded_at = {}

        def emit_xload(cc):
            xb = xbuf[cc % 4]
            if cc >= 4 and (cc - 4) in acc_evac_at:
                P.wait("sync", "dve", acc_evac_at[cc - 4])
            h = P.inc("dS", 16)
            E("sync", lambda S, cc=cc, xb=xb, h=h: S.dma_start(
                out=xb, in_=x_d.rearrange("(c p) d -> c p d", p=128)[cc]).then_inc(h, 16))
            x_loaded_at[cc] = P.cnt["dS"]

        def emit_ln(cc):
            par = cc % 2
            xb = xbuf[cc % 4]
            bb = 48 + par * 16
            x_loaded = x_loaded_at[cc]
            P.wait("vector", "dS", x_loaded)
            if (cc - 2) in ln_rsq_at:
                P.wait("vector", "act", ln_rsq_at[cc - 2])
            for half in range(2):
                h = P.inc("dve")
                E("vector", lambda V, xb=xb, half=half, bb=bb, h=h: V.bn_stats(
                    bnst[:, bb + half*6: bb + half*6 + 6],
                    xb[:, half*512:(half+1)*512]).then_inc(h, 1))
            P.wait("vector", "dve")
            h = P.inc("dve")
            E("vector", lambda V, bb=bb, h=h: V.bn_aggr(
                bnst[:, bb+12:bb+14], bnst[:, bb:bb+12]).then_inc(h, 1))
            P.wait("scalar", "dve")
            h = P.inc("act")
            E("scalar", lambda S, bb=bb, h=h: S.activation(
                bnst[:, bb+15:bb+16], bnst[:, bb+13:bb+14],
                AF.Ln, bias=c_cols[:, 1:2]).then_inc(h, 1))
            P.wait("scalar", "act")
            h = P.inc("act")
            E("scalar", lambda S, bb=bb, h=h: S.activation(
                bnst[:, bb+14:bb+15], bnst[:, bb+15:bb+16],
                AF.Exp, scale=-0.5).then_inc(h, 1))
            ln_rsq_at[cc] = P.cnt["act"]
            P.wait("vector", "act")
            if (cc - 2) in trsp_done_at:
                P.wait("vector", "pe", trsp_done_at[cc - 2])
            h = P.inc("dve")
            E("vector", lambda V, xb=xb, par=par, bb=bb, h=h: V.tensor_scalar(
                xn_bf[par][:], xb, bnst[:, bb+12:bb+13], bnst[:, bb+14:bb+15],
                ALU.subtract, ALU.mult).then_inc(h, 1))
            xn_ready_at[cc] = P.cnt["dve"]
            h = P.inc("dve")
            E("vector", lambda V, par=par, cc=cc, h=h: V.tensor_reduce(
                xamaxc[:, cc:cc+1], xn_bf[par][:], AX.X, ALU.max, apply_absolute_value=True).then_inc(h, 1))

        xnT_evac_at = {}
        xnT_h1_at = {}

        def emit_xnT(cc):
            par = cc % 2
            pe_wait_bank(7)
            P.wait("tensor", "dve", xn_ready_at[cc])
            pe_mm_group([
                (lambda T, f=f, par=par: T.transpose(pT[:, f*128:(f+1)*128],
                                                     xn_bf[par][:, f*128:(f+1)*128], c_id[:]))
                for f in range(NBK)])
            trsp_done_at[cc] = P.cnt["pe"]
            P.wait("scalar", "pe")
            if (cc - 2) in xnT_spill_at:
                P.wait("scalar", "dS", xnT_spill_at[cc - 2])
            h = P.inc("act")
            E("scalar", lambda S, par=par, h=h: S.copy(
                xnT_s[par][:, 0:512], pT[:, 0:512]).then_inc(h, 1))
            xnT_h1_at[cc] = P.cnt["act"]
            h = P.inc("act")
            E("scalar", lambda S, par=par, h=h: S.copy(
                xnT_s[par][:, 512:1024], pT[:, 512:1024]).then_inc(h, 1))
            xnT_evac_at[cc] = P.cnt["act"]
            bank_free[7] = ("act", xnT_evac_at[cc])
            P.wait("sync", "act", xnT_evac_at[cc])
            h = P.inc("dS", 16)
            E("sync", lambda S, cc=cc, par=par, h=h: S.dma_start(
                out=xnT_d.rearrange("(b p) T -> p b T", p=128)[:, :, cc*128:(cc+1)*128],
                in_=xnT_s[par].rearrange("p (b t) -> p b t", t=128)).then_inc(h, 16))
            xnT_spill_at[cc] = P.cnt["dS"]

        emit_xload(0)
        emit_ln(0)
        emit_xnT(0)

        for c in range(NCH):
            par = c % 2
            xb = xbuf[c % 4]
            mc = c % 4
            if (c - 2) in chunk_end:
                pe2 = chunk_end[c - 2]
                for eng in ("tensor", "vector", "scalar"):
                    for semn in ("pe", "act", "dve"):
                        P.wait(eng, semn, pe2[semn])
            if mc < 3 and c + 1 < NCH:
                emit_xload(c + 1)

            pe_wait_bank(4)
            P.wait("tensor", "act", xnT_h1_at[c])
            pe_mm_group([
                (lambda T, k=k, par=par: T.matmul(pb[4][:, 0:16], xnT_s[par][:, k*128:(k+1)*128],
                                                  WGT[:, k*16:(k+1)*16],
                                                  start=(k == 0), stop=False))
                for k in range(4)])
            P.wait("tensor", "act", xnT_evac_at[c])
            z_done = pe_mm_group([
                (lambda T, k=k, par=par: T.matmul(pb[4][:, 0:16], xnT_s[par][:, k*128:(k+1)*128],
                                                  WGT[:, k*16:(k+1)*16],
                                                  start=False, stop=(k == NBK - 1)))
                for k in range(4, NBK)])

            def proj_mms(W_s, banks):
                return pe_mm_group([
                    (lambda T, k=k, nh_=nh_, W_s=W_s, par=par, b=banks[nh_]: T.matmul(
                        pb[b][:], xnT_s[par][:, k*128:(k+1)*128],
                        W_s[:, k*D + nh_*512: k*D + nh_*512 + 512],
                        start=(k == 0), stop=(k == NBK - 1)))
                    for k in range(NBK) for nh_ in range(2)])

            gg = g16[par]
            P.wait("vector", "pe", z_done)
            h = P.inc("dve")
            E("vector", lambda V, gg=gg, h=h: V.tensor_single_scalar(
                gg[:, 0:16], pb[4][:, 0:16], 0.0, ALU.min).then_inc(h, 1))
            P.wait("scalar", "pe", z_done)
            h = P.inc("act")
            E("scalar", lambda S, gg=gg, h=h: S.activation(gg[:, 16:32], pb[4][:, 0:16], AF.Abs).then_inc(h, 1))
            P.wait("scalar", "act")
            h = P.inc("act")
            E("scalar", lambda S, gg=gg, h=h: S.activation(gg[:, 32:48], gg[:, 16:32], AF.Exp, scale=-1.0).then_inc(h, 1))
            P.wait("scalar", "act")
            h = P.inc("act")
            E("scalar", lambda S, gg=gg, h=h: S.activation(gg[:, 48:64], gg[:, 32:48], AF.Ln, bias=c_cols[:, 0:1]).then_inc(h, 1))
            P.wait("vector", "act")
            P.wait("vector", "dve")
            h = P.inc("dve")
            E("vector", lambda V, gg=gg, h=h: V.tensor_tensor(gg[:, 64:80], gg[:, 0:16], gg[:, 48:64], ALU.subtract).then_inc(h, 1))
            gt_ready = P.cnt["dve"]

            projd = {}
            pe_wait_bank(0); pe_wait_bank(1)
            projd[0] = proj_mms(WQ, (0, 1))

            P.wait("tensor", "dve", gt_ready)
            gmm_done = pe_mm_group([
                (lambda T, gg=gg: T.matmul(pb[4][:, 16:32], c_triu[:], gg[:, 64:80], start=True, stop=True)),
                (lambda T, gg=gg: T.matmul(pb[4][:, 32:48], c_slo[:], gg[:, 64:80], start=True, stop=True)),
                (lambda T, gg=gg: T.matmul(pb[4][0:1, 48:64], c_ones16[:], gg[:, 64:80], start=True, stop=True)),
            ])
            P.wait("scalar", "pe", gmm_done)
            if (c - 2) in ebc_done_at:
                P.wait("scalar", "pe", ebc_done_at[c - 2])
            h = P.inc("act")
            E("scalar", lambda S, gg=gg, h=h: S.activation(gg[:, 80:96], pb[4][:, 32:48], AF.Exp, bias=c_cols[:, 3:4]).then_inc(h, 1))
            h = P.inc("act")
            E("scalar", lambda S, gg=gg, h=h: S.activation(gg[:, 96:112], pb[4][:, 32:48], AF.Exp, scale=-1.0).then_inc(h, 1))
            h = P.inc("act")
            E("scalar", lambda S, par=par, h=h: S.activation(expBrow[par][:], pb[4][0:1, 48:64], AF.Exp).then_inc(h, 1))
            exps_done = P.cnt["act"]

            pe_wait_bank(2); pe_wait_bank(3)
            projd[1] = proj_mms(WK, (2, 3))

            P.wait("tensor", "act", exps_done)
            ebc = pe_mm_group([
                (lambda T, par=par: T.matmul(pb[4][0:128, 64:80], c_ones1[0:1, 0:128],
                                             expBrow[par][:], start=True, stop=True))])
            ebc_done_at[c] = ebc
            P.wait("scalar", "pe", ebc)
            h = P.inc("act")
            E("scalar", lambda S, par=par, h=h: S.copy(expBc[par][:], pb[4][0:128, 64:80]).then_inc(h, 1))
            ebc_copy_at = P.cnt["act"]
            bank_free[4] = ("act", ebc_copy_at)

            pe_wait_bank(5); pe_wait_bank(6)
            vd = pe_mm_group([
                (lambda T, k=k, nh_=nh_, par=par, b=(5, 6)[nh_]: T.matmul(
                    pb[b][:], xnT_s[par][:, k*128:(k+1)*128],
                    WV[:, k*D + nh_*512: k*D + nh_*512 + 512],
                    start=(k == 0), stop=(k == NBK - 1)))
                for k in range(NBK) for nh_ in range(2)])

            proj_evac = {}
            rq_slots = {0: 9, 1: 10}
            for pi, banks in [(0, (0, 1)), (1, (2, 3))]:
                bb = pi * 16
                P.wait("vector", "pe", projd[pi])
                if (pi, c - 1) in rsq_act_at:
                    P.wait("vector", "act", rsq_act_at[(pi, c - 1)])
                for half in range(2):
                    h = P.inc("dve")
                    E("vector", lambda V, b=banks[half], half=half, bb=bb, h=h: V.bn_stats(
                        bnst[:, bb + half*6: bb + half*6 + 6], pb[b][:]).then_inc(h, 1))
                P.wait("vector", "dve")
                h = P.inc("dve")
                E("vector", lambda V, bb=bb, h=h: V.bn_aggr(
                    bnst[:, bb+12:bb+14], bnst[:, bb:bb+12]).then_inc(h, 1))
                P.wait("vector", "dve")
                h = P.inc("dve")
                E("vector", lambda V, bb=bb, h=h: V.tensor_scalar(
                    bnst[:, bb+14:bb+15], bnst[:, bb+12:bb+13],
                    bnst[:, bb+12:bb+13], bnst[:, bb+13:bb+14],
                    ALU.mult, ALU.add).then_inc(h, 1))
                P.wait("scalar", "dve")
                h = P.inc("act")
                E("scalar", lambda S, bb=bb, h=h: S.activation(
                    bnst[:, bb+15:bb+16], bnst[:, bb+14:bb+15],
                    AF.Ln, bias=c_cols[:, 2:3]).then_inc(h, 1))
                P.wait("scalar", "act")
                h = P.inc("act")
                E("scalar", lambda S, pi=pi, par=par, bb=bb, h=h: S.activation(
                    scal_t[:, par*16+rq_slots[pi]:par*16+rq_slots[pi]+1],
                    bnst[:, bb+15:bb+16], AF.Exp, scale=-0.5).then_inc(h, 1))
                rstd_done = P.cnt["act"]
                rsq_act_at[(pi, c)] = rstd_done
                P.wait("vector", "act", max(rstd_done, exps_done))
                src_off = 80 if pi == 0 else 96
                dst_off = 112 if pi == 0 else 128
                h = P.inc("dve")
                E("vector", lambda V, gg=gg, src_off=src_off, dst_off=dst_off, pi=pi, par=par, h=h: V.tensor_scalar(
                    gg[:, dst_off:dst_off+16], gg[:, src_off:src_off+16],
                    scal_t[:, par*16+rq_slots[pi]:par*16+rq_slots[pi]+1], None, ALU.mult).then_inc(h, 1))
                P.wait("vector", "dve")
                dst = qe_bf if pi == 0 else ke_bf
                for half in range(2):
                    h = P.inc("dve")
                    def tt(V, dst=dst, b=banks[half], half=half, par=par, gg=gg, dst_off=dst_off, h=h):
                        p3 = pb[b][:].rearrange("q (g e) -> q g e", e=64)
                        s3 = gg[:, dst_off + half*8: dst_off + half*8 + 8].rearrange("q (g e) -> q g e", e=1)
                        a3, s3b = bass.broadcast_tensor_aps(p3, s3)
                        d3 = dst[par][:, half*512:(half+1)*512].rearrange("q (g e) -> q g e", e=64)
                        return V.tensor_tensor(d3, a3, s3b, ALU.mult).then_inc(h, 1)
                    E("vector", tt)
                bank_free[banks[0]] = ("dve", P.cnt["dve"] - 1)
                bank_free[banks[1]] = ("dve", P.cnt["dve"])
                proj_evac[pi] = P.cnt["dve"]
            qk_evac = P.cnt["dve"]

            P.wait("vector", "pe", vd)
            if ("v", c - 1) in rsq_act_at:
                P.wait("vector", "act", rsq_act_at[("v", c - 1)])
            for half in range(2):
                h = P.inc("dve")
                E("vector", lambda V, half=half, b=(5, 6)[half], h=h: V.bn_stats(
                    bnst[:, 32 + half*6: 32 + half*6 + 6], pb[b][:]).then_inc(h, 1))
            P.wait("vector", "dve")
            h = P.inc("dve")
            E("vector", lambda V, h=h: V.bn_aggr(bnst[:, 44:46], bnst[:, 32:44]).then_inc(h, 1))
            P.wait("vector", "dve")
            h = P.inc("dve")
            E("vector", lambda V, h=h: V.tensor_scalar(
                bnst[:, 46:47], bnst[:, 44:45], bnst[:, 44:45], bnst[:, 45:46],
                ALU.mult, ALU.add).then_inc(h, 1))
            P.wait("scalar", "dve")
            h = P.inc("act")
            E("scalar", lambda S, h=h: S.activation(
                bnst[:, 47:48], bnst[:, 46:47],
                AF.Ln, bias=c_cols[:, 2:3]).then_inc(h, 1))
            P.wait("scalar", "act")
            h = P.inc("act")
            E("scalar", lambda S, par=par, h=h: S.activation(
                scal_t[:, par*16+11:par*16+12], bnst[:, 47:48],
                AF.Exp, scale=-0.5).then_inc(h, 1))
            rsq_act_at[("v", c)] = P.cnt["act"]
            P.wait("scalar", "act")
            for half in range(2):
                h = P.inc("act")
                E("scalar", lambda S, half=half, b=(5, 6)[half], par=par, h=h: S.activation(
                    v_bf[par][:, half*512:(half+1)*512], pb[b][:], AF.Copy,
                    scale=scal_t[:, par*16+11:par*16+12]).then_inc(h, 1))
            bank_free[5] = ("act", P.cnt["act"] - 1)
            bank_free[6] = ("act", P.cnt["act"])
            v_evac = P.cnt["act"]
            scal_free_at[c] = ("act", v_evac)


            pe_wait_bank(2); pe_wait_bank(3)
            gd = pe_mm_group([
                (lambda T, k=k, nh_=nh_, par=par, b=(2, 3)[nh_]: T.matmul(
                    pb[b][:], xnT_s[par][:, k*128:(k+1)*128],
                    WG[:, k*D + nh_*512: k*D + nh_*512 + 512],
                    start=(k == 0), stop=(k == NBK - 1)))
                for k in range(NBK) for nh_ in range(2)])
            P.wait("scalar", "pe", gd)
            for half in range(2):
                h = P.inc("act")
                E("scalar", lambda S, half=half, b=(2, 3)[half], mc=mc, h=h: S.copy(
                    gmac[:, mc*1024 + half*512: mc*1024 + (half+1)*512], pb[b][:]).then_inc(h, 1))
            bank_free[2] = ("act", P.cnt["act"] - 1)
            bank_free[3] = ("act", P.cnt["act"])

            for (src, dstT) in [(qe_bf, qT_s), (ke_bf, kT_s)]:
                for half in range(2):
                    pe_wait_bank(7)
                    P.wait("tensor", "dve", proj_evac[0] if src is qe_bf else proj_evac[1])
                    pe_mm_group([
                        (lambda T, hh=hh, half=half, src=src, par=par: T.transpose(
                            pT[0:64, hh*128:(hh+1)*128],
                            src[par][:, (half*8+hh)*64:(half*8+hh)*64+64], c_id[:]))
                        for hh in range(8)])
                    P.wait("scalar", "pe")
                    h = P.inc("act")
                    E("scalar", lambda S, dstT=dstT, half=half, par=par, h=h: S.copy(
                        dstT[par][0:64, half*1024:(half+1)*1024], pT[0:64, :]).then_inc(h, 1))
                    bank_free[7] = ("act", P.cnt["act"])
            kT_evac = P.cnt["act"]

            if (c - 1) in o_done_at:
                P.wait("vector", "pe", o_done_at[c - 1])
            P.wait("vector", "act", ebc_copy_at)
            for hd in range(NH):
                if hd < NH - 1:
                    E("vector", lambda V, hd=hd, par=par: V.tensor_scalar(
                        Sdec[:, hd*64:(hd+1)*64], S_f[:, hd*64:(hd+1)*64],
                        expBc[par][0:64, hd:hd+1], None, ALU.mult))
                else:
                    h = P.inc("dve")
                    E("vector", lambda V, hd=hd, par=par, h=h: V.tensor_scalar(
                        Sdec[:, hd*64:(hd+1)*64], S_f[:, hd*64:(hd+1)*64],
                        expBc[par][0:64, hd:hd+1], None, ALU.mult).then_inc(h, 1))
            P.wait("vector", "dve")
            h = P.inc("dve")
            E("vector", lambda V, h=h: V.tensor_copy(Sdec_bf[0:64, :], Sdec[:]).then_inc(h, 1))
            sdec_done = P.cnt["dve"]

            P.wait("tensor", "act", kT_evac)
            at_last = 0
            for gidx in range(4):
                ab = 5 + (gidx % 2)
                pe_wait_bank(ab)
                agrp = pe_mm_group([
                    (lambda T, hd=gidx*4+hh, hh=hh, ab=ab, par=par: T.matmul(
                        pb[ab][:, hh*128:(hh+1)*128],
                        kT_s[par][:, hd*128:(hd+1)*128], qT_s[par][:, hd*128:(hd+1)*128],
                        start=True, stop=True))
                    for hh in range(4)])
                P.wait("vector", "pe", agrp)
                h = P.inc("dve")
                E("vector", lambda V, gidx=gidx, ab=ab, par=par, h=h: V.tensor_tensor(
                    AT_bf[par][:, gidx*512:(gidx+1)*512], pb[ab][:], c_mask4[:], ALU.mult).then_inc(h, 1))
                bank_free[ab] = ("dve", P.cnt["dve"])
                at_last = P.cnt["dve"]

            for wave in range(2):
                pe_wait_bank(4)
                P.wait("tensor", "dve", qk_evac)
                P.wait("tensor", "act", v_evac)
                kvd = pe_mm_group([
                    (lambda T, hd=wave*8+hh, hh=hh, par=par: T.matmul(
                        pb[4][0:64, hh*64:(hh+1)*64],
                        ke_bf[par][:, hd*64:(hd+1)*64], v_bf[par][:, hd*64:(hd+1)*64],
                        start=True, stop=True))
                    for hh in range(8)])
                P.wait("vector", "pe", kvd)
                P.wait("vector", "dve", sdec_done)
                h = P.inc("dve")
                E("vector", lambda V, wave=wave, h=h: V.tensor_tensor(
                    S_f[:, wave*512:(wave+1)*512], Sdec[:, wave*512:(wave+1)*512],
                    pb[4][0:64, 0:512], ALU.add).then_inc(h, 1))
                bank_free[4] = ("dve", P.cnt["dve"])

            if mc < 3 and c + 1 < NCH:
                emit_ln(c + 1)

            pe_wait_bank(0); pe_wait_bank(1)
            P.wait("tensor", "dve", max(at_last, sdec_done))
            o_half = {}
            for ob in range(2):
                omms = []
                for hd in range(ob*8, ob*8 + 8):
                    off = (hd % 8) * 64
                    omms.append(lambda T, hd=hd, ob=ob, off=off, par=par: T.matmul(
                        pb[ob][:, off:off+64], AT_bf[par][:, hd*128:(hd+1)*128],
                        v_bf[par][:, hd*64:(hd+1)*64], start=True, stop=False))
                    omms.append(lambda T, hd=hd, ob=ob, off=off, par=par: T.matmul(
                        pb[ob][:, off:off+64], qT_s[par][:, hd*128:(hd+1)*128],
                        Sdec_bf[:, hd*64:(hd+1)*64], start=False, stop=True))
                o_half[ob] = pe_mm_group(omms)
            o_done = o_half[1]
            o_done_at[c] = o_done

            if mc == 0 and tail_og3_at is not None:
                P.wait("scalar", "dve", tail_og3_at)
            for half in range(2):
                P.wait("scalar", "pe", o_half[half])
                h = P.inc("act")
                E("scalar", lambda S, half=half, h=h: S.activation(
                    osq[:, half*512:(half+1)*512], pb[half][:], AF.Square).then_inc(h, 1))
            P.wait("vector", "act")
            h = P.inc("dve")
            E("vector", lambda V, par=par, h=h: V.tensor_reduce(
                rst16[par][:, 16:32].rearrange("q (g o) -> q g o", o=1),
                osq[:].rearrange("q (g e) -> q g e", e=64), AX.X, ALU.add).then_inc(h, 1))
            P.wait("scalar", "dve")
            h = P.inc("act")
            E("scalar", lambda S, par=par, h=h: S.activation(
                rst16[par][:, 0:16], rst16[par][:, 16:32], AF.Ln,
                scale=1.0 / DH, bias=c_cols[:, 2:3]).then_inc(h, 1))
            P.wait("scalar", "act")
            h = P.inc("act")
            E("scalar", lambda S, par=par, h=h: S.activation(
                rst16[par][:, 0:16], rst16[par][:, 0:16], AF.Exp, scale=-0.5).then_inc(h, 1))
            rstd_done = P.cnt["act"]
            P.wait("vector", "act", rstd_done)
            for half in range(2):
                h = P.inc("dve")
                def tt2(V, half=half, par=par, mc=mc, h=h):
                    p3 = pb[half][:].rearrange("q (g e) -> q g e", e=64)
                    s3 = rst16[par][:, half*8:half*8+8].rearrange("q (g e) -> q g e", e=1)
                    a3, s3b = bass.broadcast_tensor_aps(p3, s3)
                    d3 = orms[:, mc*1024 + half*512: mc*1024 + (half+1)*512].rearrange("q (g e) -> q g e", e=64)
                    return V.tensor_tensor(d3, a3, s3b, ALU.mult).then_inc(h, 1)
                E("vector", tt2)
            bank_free[0] = ("dve", P.cnt["dve"] - 1)
            bank_free[1] = ("dve", P.cnt["dve"])

            if mc < 3 and c + 1 < NCH:
                emit_xnT(c + 1)

            chunk_end[c] = {k: P.cnt[k] for k in ("pe", "act", "dve")}

            if mc == 3:
                m0 = c - 3
                gbuf = [FA[:, 4096:5120], FA[:, 10240:11264],
                        FA[:, 11264:12288], osq]
                ogbs = [ogb, BA[:, 71808:72832]]
                ogTs = [ogT_s, BA[:, 69632:70656]]
                P.wait("scalar", "dve")
                silu_at = {}
                for cc2 in range(4):
                    h = P.inc("act")
                    E("scalar", lambda S, cc2=cc2, h=h: S.activation(
                        gbuf[cc2][:], gmac[:, cc2*1024:(cc2+1)*1024], AF.Silu).then_inc(h, 1))
                    silu_at[cc2] = P.cnt["act"]
                trsp_tail = {}
                wo_tail = {}
                for cc2 in range(4):
                    cglob = m0 + cc2
                    P.wait("vector", "act", silu_at[cc2])
                    if (cc2 - 2) in trsp_tail:
                        P.wait("vector", "pe", trsp_tail[cc2 - 2])
                    h = P.inc("dve")
                    E("vector", lambda V, cc2=cc2, h=h: V.tensor_tensor(
                        ogbs[cc2 % 2][:], orms[:, cc2*1024:(cc2+1)*1024],
                        gbuf[cc2][:], ALU.mult).then_inc(h, 1))
                    og_done = P.cnt["dve"]
                    if cc2 == 3:
                        tail_og3_at = og_done
                    pe_wait_bank(7)
                    P.wait("tensor", "dve", og_done)
                    pe_mm_group([
                        (lambda T, f=f, cc2=cc2: T.transpose(pT[:, f*128:(f+1)*128],
                                                             ogbs[cc2 % 2][:, f*128:(f+1)*128], c_id[:]))
                        for f in range(NBK)])
                    trsp_tail[cc2] = P.cnt["pe"]
                    P.wait("scalar", "pe", trsp_tail[cc2])
                    h = P.inc("act")
                    E("scalar", lambda S, cc2=cc2, h=h: S.copy(ogTs[cc2 % 2][:], pT[:]).then_inc(h, 1))
                    ogT_evac = P.cnt["act"]
                    bank_free[7] = ("act", ogT_evac)
                    wbk = (2, 3) if cc2 % 2 == 0 else (5, 6)
                    pe_wait_bank(wbk[0]); pe_wait_bank(wbk[1])
                    P.wait("tensor", "act", ogT_evac)
                    wo_done = pe_mm_group([
                        (lambda T, k=k, nh_=nh_, cc2=cc2, wbk=wbk: T.matmul(
                            pb[wbk[nh_]][:], ogTs[cc2 % 2][:, k*128:(k+1)*128],
                            WO[:, k*D + nh_*512: k*D + nh_*512 + 512],
                            start=(k == 0), stop=(k == NBK - 1)))
                        for k in range(NBK) for nh_ in range(2)])
                    wo_tail[cc2] = wo_done
                    P.wait("vector", "pe", wo_done)
                    if (cglob - 2) in acc_dma_at:
                        P.wait("vector", "dS", acc_dma_at[cglob - 2])
                    for nh_ in range(2):
                        h = P.inc("dve")
                        E("vector", lambda V, nh_=nh_, cglob=cglob, wbk=wbk, h=h: V.scalar_tensor_tensor(
                            accst[cglob % 2][:, nh_*512:(nh_+1)*512], pb[wbk[nh_]][:], 1.0,
                            xbuf[cglob % 4][:, nh_*512:(nh_+1)*512], ALU.mult, ALU.add).then_inc(h, 1))
                    acc_done = P.cnt["dve"]
                    bank_free[wbk[0]] = ("dve", acc_done - 1)
                    bank_free[wbk[1]] = ("dve", acc_done)
                    acc_evac_at[cglob] = acc_done
                    P.wait("sync", "dve", acc_done)
                    h = P.inc("dS", 16)
                    E("sync", lambda S, cglob=cglob, h=h: S.dma_start(
                        out=acc_d.rearrange("(c p) d -> c p d", p=128)[cglob],
                        in_=accst[cglob % 2][:]).then_inc(h, 16))
                    acc_dma_at[cglob] = P.cnt["dS"]
                chunk_end[c] = {k: P.cnt[k] for k in ("pe", "act", "dve")}
                if c + 1 < NCH:
                    emit_xload(c + 1)
                    emit_ln(c + 1)
                    emit_xnT(c + 1)

        for eng in ("tensor", "vector", "scalar", "sync", "gpsimd"):
            for semn in ("pe", "act", "dve"):
                P.wait(eng, semn)
        P.wait("vector", "dS")
        P.wait("tensor", "dS")
        P.wait("scalar", "dS")

        h = P.inc("dve")
        E("vector", lambda V, h=h: V.tensor_reduce(xcol[:], xamaxc[:], AX.X, ALU.max).then_inc(h, 1))
        P.wait("sync", "dve")
        h = P.inc("dS", 16)
        E("sync", lambda S, h=h: S.dma_start(out=fold_bounce.rearrange("o p -> p o"), in_=xcol[:]).then_inc(h, 16))
        P.wait("sync", "dS")
        h = P.inc("dS", 16)
        E("sync", lambda S, h=h: S.dma_start(out=xrow, in_=fold_bounce).then_inc(h, 16))
        P.wait("vector", "dS")
        h = P.inc("dve")
        E("vector", lambda V, h=h: V.memset(ccs, 0.0).then_inc(h, 1))
        P.wait("vector", "dve")
        h = P.inc("dve")
        E("vector", lambda V, h=h: V.tensor_reduce(ccs[0:1, 0:1], xrow, AX.X, ALU.max).then_inc(h, 1))
        P.wait("gpsimd", "dve")
        h = P.inc("dG", 16)
        E("gpsimd", lambda G, h=h: G.dma_start(out=cc1_in.ap(), in_=ccs).then_inc(h, 16))
        P.wait("gpsimd", "dG")
        h = P.inc("cc")
        E("gpsimd", lambda G, h=h: G.collective_compute(
            "AllReduce", ALU.max, replica_groups=[list(range(8))],
            ins=[cc1_in.ap().opt()], outs=[cc1_out.ap().opt()]).then_inc(h, 1))
        P.wait("gpsimd", "cc")
        h = P.inc("dG", 16)
        E("gpsimd", lambda G, h=h: G.dma_start(out=gs1, in_=cc1_out.ap()).then_inc(h, 16))
        P.wait("vector", "dG")
        h = P.inc("dve")
        E("vector", lambda V, h=h: V.tensor_scalar_mul(sc_small[0:1, 0:1], gs1[0:1, 0:1], 1.0 / 32767.0).then_inc(h, 1))
        P.wait("tensor", "dve")
        pe_wait_bank(4)
        rsxmm = pe_mm_group([
            (lambda T: T.matmul(pb[4][:, 0:1], c_ones1[:], sc_small[0:1, 0:1], start=True, stop=True))])
        P.wait("scalar", "pe", rsxmm)
        h = P.inc("act")
        E("scalar", lambda S, h=h: S.copy(r_sx_col[:], pb[4][:, 0:1]).then_inc(h, 1))
        bank_free[4] = ("act", P.cnt["act"])
        P.wait("vector", "act")
        h = P.inc("dve")
        E("vector", lambda V, h=h: V.reciprocal(s_x_col[:], r_sx_col[:]).then_inc(h, 1))
        P.wait("vector", "dve")

        qdone = {}
        ldone = {}
        for i in range(2 * NBK):
            k, hf = i // 2, i % 2
            qe_, qs_ = (("sync", "dS") if i % 2 == 0 else ("scalar", "dA"))
            if (i - 2) in qdone:
                P.wait(qe_, "dve", qdone[i - 2])
            h = P.inc(qs_, 16)
            E(qe_, lambda S, k=k, hf=hf, i=i, h=h: S.dma_start(
                out=wstage[:, (i % 2)*2048:(i % 2)*2048+2048],
                in_=Wv_(W_in, k)[:, hf*2048:(hf+1)*2048]).then_inc(h, 16))
            ldone[i] = P.cnt[qs_]
            P.wait("vector", qs_, ldone[i])
            if k == 0:
                h = P.inc("dve")
                E("vector", lambda V, i=i, hf=hf, h=h: V.tensor_tensor(
                    run[:, hf*2048:(hf+1)*2048], wstage[:, (i % 2)*2048:(i % 2)*2048+2048],
                    wstage[:, (i % 2)*2048:(i % 2)*2048+2048], ALU.mult).then_inc(h, 1))
            else:
                E("vector", lambda V, i=i, hf=hf: V.tensor_tensor(
                    swb[:, hf*2048:(hf+1)*2048], wstage[:, (i % 2)*2048:(i % 2)*2048+2048],
                    wstage[:, (i % 2)*2048:(i % 2)*2048+2048], ALU.mult))
                h = P.inc("dve")
                E("vector", lambda V, hf=hf, h=h: V.tensor_tensor(
                    run[:, hf*2048:(hf+1)*2048], run[:, hf*2048:(hf+1)*2048],
                    swb[:, hf*2048:(hf+1)*2048], ALU.max).then_inc(h, 1))
            qdone[i] = P.cnt["dve"]
        for L in (64, 32, 16, 8, 4, 2, 1):
            P.wait("sync", "dve")
            h = P.inc("dS", 16)
            E("sync", lambda S, L=L, h=h: S.dma_start(out=wstage[0:L, :], in_=run[L:2*L, :]).then_inc(h, 16))
            P.wait("vector", "dS")
            h = P.inc("dve")
            E("vector", lambda V, L=L, h=h: V.tensor_tensor(run[0:L, :], run[0:L, :], wstage[0:L, :], ALU.max).then_inc(h, 1))
        P.wait("vector", "dve")
        h = P.inc("dve")
        E("vector", lambda V, h=h: V.tensor_scalar_max(run[0:1, :], run[0:1, :], 1e-16).then_inc(h, 1))
        P.wait("scalar", "dve")
        h = P.inc("act")
        E("scalar", lambda S, h=h: S.activation(run[32:33, :], run[0:1, :], AF.Ln).then_inc(h, 1))
        P.wait("scalar", "act")
        h = P.inc("act")
        E("scalar", lambda S, h=h: S.activation(run[0:1, :], run[32:33, :], AF.Exp, scale=0.5).then_inc(h, 1))
        h = P.inc("act")
        E("scalar", lambda S, h=h: S.activation(wstage[0:1, :], run[32:33, :], AF.Exp, scale=-0.5).then_inc(h, 1))
        P.wait("sync", "act")
        P.wait("tensor", "act")
        h = P.inc("dS", 16)
        E("sync", lambda S, h=h: S.dma_start(out=wq_bounce, in_=run[0:1, :]).then_inc(h, 16))
        P.wait("sync", "dS")
        h = P.inc("dS", 16)
        E("sync", lambda S, h=h: S.dma_start(out=w32[:], in_=wq_bounce.rearrange("o (a p) -> (o a) p", p=128)).then_inc(h, 16))
        P.wait("tensor", "dS")
        pe_wait_bank(3)
        t32 = pe_mm_group([(lambda T: T.transpose(pb[3][:, 0:32], w32[:], c_idf[0:32, 0:32]))])
        P.wait("scalar", "pe", t32)
        h = P.inc("act")
        E("scalar", lambda S, h=h: S.copy(sc32amax[:], pb[3][:, 0:32]).then_inc(h, 1))
        bank_free[3] = ("act", P.cnt["act"])
        P.wait("vector", "act")
        h = P.inc("dve")
        E("vector", lambda V, h=h: V.tensor_scalar(sc32[:], sc32amax[:], r_sx_col[:], 1.0 / 127.0, ALU.mult, ALU.mult).then_inc(h, 1))
        for cq in range(8):
            bk = cq % 2
            pe_wait_bank(bk)
            h = P.inc("pe")
            E("tensor", lambda T, cq=cq, bk=bk, h=h: T.matmul(
                pb[bk][:], c_c127[:], wstage[0:1, cq*512:(cq+1)*512], start=True, stop=True).then_inc(h, 1))
            P.wait("scalar", "pe")
            h = P.inc("act")
            E("scalar", lambda S, cq=cq, bk=bk, h=h: S.copy(swb[:, cq*512:(cq+1)*512], pb[bk][:]).then_inc(h, 1))
            bank_free[bk] = ("act", P.cnt["act"])
        qdone = {}
        for i in range(2 * NBK):
            k, hf = i // 2, i % 2
            qe_, qs_ = (("sync", "dS") if i % 2 == 0 else ("scalar", "dA"))
            if i <= 1:
                P.wait(qe_, "act")
            if (i - 2) in qdone:
                P.wait(qe_, "dve", qdone[i - 2])
            h = P.inc(qs_, 16)
            E(qe_, lambda S, k=k, hf=hf, i=i, h=h: S.dma_start(
                out=wstage[:, (i % 2)*2048:(i % 2)*2048+2048],
                in_=Wv_(W_in, k)[:, hf*2048:(hf+1)*2048]).then_inc(h, 16))
            ld = P.cnt[qs_]
            P.wait("vector", qs_, ld)
            if i == 0:
                P.wait("vector", "act")
            E("vector", lambda V, i=i, hf=hf: V.tensor_tensor(
                run[:, hf*2048:(hf+1)*2048], wstage[:, (i % 2)*2048:(i % 2)*2048+2048],
                swb[:, hf*2048:(hf+1)*2048], ALU.mult))
            h = P.inc("dve")
            E("vector", lambda V, k=k, hf=hf, h=h: V.tensor_scalar(
                arenaA[:, k*DFF + hf*2048: k*DFF + hf*2048 + 2048], run[:, hf*2048:(hf+1)*2048],
                MAGIC, MAGIC, ALU.add, ALU.subtract).then_inc(h, 1))
            qdone[i] = P.cnt["dve"]

        for b in range(NBK):
            h = P.inc("dS", 16)
            E("sync", lambda S, b=b, h=h: S.dma_start(
                out=arenaB[:, b*NT:(b+1)*NT],
                in_=xnT_d.rearrange("(b p) T -> b p T", p=128)[b]).then_inc(h, 16))
        P.wait("vector", "dS")
        for b in range(NBK):
            E("vector", lambda V, b=b: V.tensor_scalar(
                run[:], arenaB[:, b*NT:(b+1)*NT], s_x_col[:], MAGIC, ALU.mult, ALU.add))
            h = P.inc("dve")
            E("vector", lambda V, b=b, h=h: V.tensor_single_scalar(
                arenaB[:, b*NT:(b+1)*NT], run[:], MAGIC, ALU.subtract).then_inc(h, 1))
        xq_ready = P.cnt["dve"]
        P.wait("tensor", "dve", xq_ready)
        P.wait("tensor", "act")
        E("vector", lambda V: V.memset(hqc[:], 0.0))
        evac_done_at = {}
        dve_red_at = {}
        spill_at = {}
        for quarter in range(4):
            tok0 = quarter * 1024
            for m in range(NBF):
                g = m % 2
                gm = quarter * NBF + m
                if gm >= 2:
                    P.wait("tensor", "act", evac_done_at[gm - 2])
                mml = []
                for k in range(NBK):
                    for t in range(2):
                        mml.append(lambda T, k=k, t=t, m=m, g=g, tok0=tok0: T.matmul(
                            pb[2*g + t][:],
                            arenaA[:, k*DFF + m*128: k*DFF + m*128 + 128],
                            arenaB[:, k*NT + tok0 + t*512: k*NT + tok0 + t*512 + 512],
                            start=(k == 0), stop=(k == NBK - 1)))
                mmd = pe_mm_group(mml)
                P.wait("scalar", "pe", mmd)
                if gm >= 2:
                    P.wait("scalar", "dve", dve_red_at[gm - 2])
                    P.wait("scalar", "dS", spill_at[gm - 2])
                for t in range(2):
                    h = P.inc("act")
                    E("scalar", lambda S, t=t, m=m, g=g, h=h: S.activation(
                        hstage[g][:, t*512:(t+1)*512], pb[2*g + t][:], AF.Gelu,
                        bias=s_bic[:, m:m+1], scale=sc32[:, m:m+1]).then_inc(h, 1))
                evac_done_at[gm] = P.cnt["act"]
                P.wait("vector", "act", evac_done_at[gm])
                h = P.inc("dve")
                E("vector", lambda V, g=g, m=m, h=h: V.tensor_reduce(
                    run[:, m:m+1], hstage[g][:], AX.X, ALU.max,
                    apply_absolute_value=True).then_inc(h, 1))
                P.wait("vector", "dve")
                h = P.inc("dve")
                E("vector", lambda V, m=m, h=h: V.scalar_tensor_tensor(
                    hqc[:, m:m+1], run[:, m:m+1], 1.0, hqc[:, m:m+1], ALU.mult, ALU.max).then_inc(h, 1))
                dve_red_at[gm] = P.cnt["dve"]
                P.wait("sync", "act", evac_done_at[gm])
                h = P.inc("dS", 16)
                E("sync", lambda S, m=m, g=g, tok0=tok0, h=h: S.dma_start(
                    out=hidT.rearrange("(b p) n -> b p n", p=128)[m][:, tok0:tok0+1024],
                    in_=hstage[g][:]).then_inc(h, 16))
                spill_at[gm] = P.cnt["dS"]

        P.wait("vector", "dve")
        h = P.inc("dve")
        E("vector", lambda V, h=h: V.tensor_reduce(xcol[:], hqc[:], AX.X, ALU.max).then_inc(h, 1))
        P.wait("sync", "dve")
        h = P.inc("dS", 16)
        E("sync", lambda S, h=h: S.dma_start(out=fold_bounce.rearrange("o p -> p o"), in_=xcol[:]).then_inc(h, 16))
        P.wait("sync", "dS")
        h = P.inc("dS", 16)
        E("sync", lambda S, h=h: S.dma_start(out=xrow, in_=fold_bounce).then_inc(h, 16))
        P.wait("vector", "dS")
        h = P.inc("dve")
        E("vector", lambda V, h=h: V.memset(ccs, 0.0).then_inc(h, 1))
        P.wait("vector", "dve")
        h = P.inc("dve")
        E("vector", lambda V, h=h: V.tensor_reduce(ccs[0:1, 0:1], xrow, AX.X, ALU.max).then_inc(h, 1))
        P.wait("gpsimd", "dve")
        h = P.inc("dG", 16)
        E("gpsimd", lambda G, h=h: G.dma_start(out=cc2_in.ap(), in_=ccs).then_inc(h, 16))
        P.wait("gpsimd", "dG")
        h = P.inc("cc")
        E("gpsimd", lambda G, h=h: G.collective_compute(
            "AllReduce", ALU.max, replica_groups=[list(range(8))],
            ins=[cc2_in.ap().opt()], outs=[cc2_out.ap().opt()]).then_inc(h, 1))
        P.wait("gpsimd", "cc")
        h = P.inc("dG", 16)
        E("gpsimd", lambda G, h=h: G.dma_start(out=gs2, in_=cc2_out.ap()).then_inc(h, 16))
        P.wait("vector", "dG")
        h = P.inc("dve")
        E("vector", lambda V, h=h: V.tensor_scalar_mul(sc_small[0:1, 1:2], gs2[0:1, 0:1], 1.0 / 32767.0).then_inc(h, 1))
        P.wait("tensor", "dve")
        pe_wait_bank(4)
        rshmm = pe_mm_group([
            (lambda T: T.matmul(pb[4][:, 0:1], c_ones1[:], sc_small[0:1, 1:2], start=True, stop=True))])
        P.wait("scalar", "pe", rshmm)
        h = P.inc("act")
        E("scalar", lambda S, h=h: S.copy(r_sh_col[:], pb[4][:, 0:1]).then_inc(h, 1))
        bank_free[4] = ("act", P.cnt["act"])
        P.wait("vector", "act")
        h = P.inc("dve")
        E("vector", lambda V, h=h: V.reciprocal(s_h_col[:], r_sh_col[:]).then_inc(h, 1))
        P.wait("vector", "dve")

        P.wait("vector", "pe")
        qdone = {}
        for k in range(NBF):
            qe_, qs_ = (("sync", "dS") if k % 2 == 0 else ("scalar", "dA"))
            if (k - 2) in qdone:
                P.wait(qe_, "dve", qdone[k - 2])
            h = P.inc(qs_, 16)
            E(qe_, lambda S, k=k, h=h: S.dma_start(
                out=wstage[:, (k % 2)*1024:(k % 2)*1024+1024], in_=Wv_(W_out, k)).then_inc(h, 16))
            ld = P.cnt[qs_]
            P.wait("vector", qs_, ld)
            h = P.inc("dve")
            E("vector", lambda V, k=k, h=h: V.tensor_reduce(
                amax2c[:, k:k+1], wstage[:, (k % 2)*1024:(k % 2)*1024+1024],
                AX.X, ALU.max, apply_absolute_value=True).then_inc(h, 1))
            P.wait("vector", "dve")
            h = P.inc("dve")
            E("vector", lambda V, k=k, h=h: V.tensor_scalar_max(amax2c[:, k:k+1], amax2c[:, k:k+1], 1e-8).then_inc(h, 1))
            P.wait("vector", "dve")
            h = P.inc("dve")
            E("vector", lambda V, k=k, h=h: V.reciprocal(rcol[:], amax2c[:, k:k+1]).then_inc(h, 1))
            P.wait("vector", "dve")
            E("vector", lambda V, k=k: V.tensor_scalar(
                run[:, (k % 2)*1024:(k % 2)*1024+1024], wstage[:, (k % 2)*1024:(k % 2)*1024+1024],
                rcol[:], 127.0, ALU.mult, ALU.mult))
            h = P.inc("dve")
            E("vector", lambda V, k=k, h=h: V.tensor_scalar(
                arenaA[:, k*1024:(k+1)*1024], run[:, (k % 2)*1024:(k % 2)*1024+1024],
                MAGIC, MAGIC, ALU.add, ALU.subtract).then_inc(h, 1))
            qdone[k] = P.cnt["dve"]
        h = P.inc("dve")
        E("vector", lambda V, h=h: V.tensor_scalar(sc2[:], amax2c[:], r_sh_col[:], 1.0 / 127.0, ALU.mult, ALU.mult).then_inc(h, 1))
        P.wait("vector", "dve")
        P.wait("sync", "dve")
        h = P.inc("dS", 16)
        E("sync", lambda S, h=h: S.dma_start(out=brow[0:1, :], in_=b_out_d).then_inc(h, 16))
        P.wait("tensor", "dS")
        P.wait("tensor", "dve")
        pe_wait_bank(4); pe_wait_bank(5)
        bmm = pe_mm_group([
            (lambda T, n=n: T.matmul(pb[4 + n][:], c_ones1[:], brow[0:1, n*512:(n+1)*512],
                                     start=True, stop=True)) for n in range(2)])
        P.wait("scalar", "pe", bmm)
        for n in range(2):
            h = P.inc("act")
            E("scalar", lambda S, n=n, h=h: S.copy(boutb[:, n*512:(n+1)*512], pb[4 + n][:]).then_inc(h, 1))
        bank_free[4] = ("act", P.cnt["act"] - 1)
        bank_free[5] = ("act", P.cnt["act"])
        P.wait("vector", "act")

        ld_at = {}
        ev_at = {}
        pe_after_sub = 0
        hdbuf4 = [BA[:, 65536:69632], BA[:, 69632:73728]]
        p3scr2 = FA[:, 1024:3072]
        q_done = {}
        l_done = {}
        p2sp = P.cnt["dS"]
        P.wait("scalar", "dS", p2sp)
        for sub in range(4):
            tok0 = sub * 1024
            if sub > 0:
                P.wait("vector", "pe", pe_after_sub)
            for kb0 in range(8):
                kb = kb0
                gi = sub * 8 + kb
                qe_, qs_ = (("sync", "dS") if gi % 2 == 0 else ("scalar", "dA"))
                if (gi - 2) in q_done:
                    P.wait(qe_, "dve", q_done[gi - 2])
                h = P.inc(qs_, 16)
                E(qe_, lambda S, kb=kb, tok0=tok0, h=h: S.dma_start(
                    out=hdbuf4[kb % 2].rearrange("p (b n) -> p b n", n=1024),
                    in_=hidT.rearrange("(a b p) n -> a p b n", b=4, p=128)[kb][:, :, tok0:tok0+1024]).then_inc(h, 16))
                l_done[gi] = P.cnt[qs_]
                P.wait("vector", qs_, l_done[gi])
                for half in range(2):
                    E("vector", lambda V, kb=kb, half=half: V.tensor_scalar(
                        p3scr2[:], hdbuf4[kb % 2][:, half*2048:(half+1)*2048],
                        s_h_col[:], MAGIC, ALU.mult, ALU.add))
                    for j in range(2):
                        k0 = kb * 4 + half * 2 + j
                        h = P.inc("dve")
                        E("vector", lambda V, k0=k0, j=j, h=h: V.tensor_scalar(
                            arenaB[:, k0*1024:(k0+1)*1024], p3scr2[:, j*1024:(j+1)*1024],
                            MAGIC, sc2[:, k0:k0+1], ALU.subtract, ALU.mult).then_inc(h, 1))
                q_done[gi] = P.cnt["dve"]
            hs_ready = P.cnt["dve"]
            P.wait("tensor", "dve", hs_ready)
            P.wait("tensor", "act")
            for mt in range(8):
                g = mt % 2
                gm = sub * 8 + mt
                cglob = sub * 8 + mt
                if gm >= 2:
                    P.wait("sync", "dve", ev_at[gm - 2])
                h = P.inc("dS", 16)
                E("sync", lambda S, cglob=cglob, g=g, h=h: S.dma_start(
                    out=accbuf[g][:],
                    in_=acc_d.rearrange("(c p) d -> c p d", p=128)[cglob]).then_inc(h, 16))
                accl = P.cnt["dS"]
                if gm >= 2:
                    P.wait("tensor", "dve", ev_at[gm - 2])
                mml = []
                for k in range(NBF):
                    for n in range(2):
                        mml.append(lambda T, k=k, n=n, mt=mt, g=g: T.matmul(
                            pb[2*g + n][:],
                            arenaB[:, k*1024 + mt*128: k*1024 + mt*128 + 128],
                            arenaA[:, k*1024 + n*512: k*1024 + n*512 + 512],
                            start=(k == 0), stop=(k == NBF - 1)))
                mmd = pe_mm_group(mml)
                P.wait("vector", "pe", mmd)
                P.wait("vector", "dS", accl)
                if gm >= 2:
                    P.wait("vector", "dS", ld_at[gm - 2])
                for n in range(2):
                    E("vector", lambda V, n=n, g=g: V.scalar_tensor_tensor(
                        ostage[g][:, n*512:(n+1)*512], pb[2*g + n][:], 1.0,
                        accbuf[g][:, n*512:(n+1)*512], ALU.mult, ALU.add))
                h = P.inc("dve")
                E("vector", lambda V, g=g, h=h: V.tensor_tensor(
                    ostage[g][:], ostage[g][:], boutb[:], ALU.add).then_inc(h, 1))
                ev_at[gm] = P.cnt["dve"]
                P.wait("sync", "dve", ev_at[gm])
                h = P.inc("dS", 16)
                E("sync", lambda S, cglob=cglob, g=g, h=h: S.dma_start(
                    out=out_d.rearrange("(c p) d -> c p d", p=128)[cglob],
                    in_=ostage[g][:]).then_inc(h, 16))
                ld_at[gm] = P.cnt["dS"]
            pe_after_sub = P.cnt["pe"]

        P.wait("sync", "dS")
        P.wait("sync", "dG")
        P.wait("sync", "dA")
        P.wait("gpsimd", "dG")
        P.wait("gpsimd", "dS")
        P.wait("scalar", "dA")

        with nc.Block() as block:
            P.run(block)
    return nc


def _make_consts():
    j = np.arange(128)
    return dict(
        ident_bf=np.eye(128).astype(ml_dtypes.bfloat16),
        ident_f32=np.eye(128).astype(np.float32),
        triu16=(np.where(j[:, None] <= j[None, :], 1.0, 0.0) / 16.0).astype(np.float32),
        slo16=(np.where(j[:, None] > j[None, :], -1.0, 0.0) / 16.0).astype(np.float32),
        onescol16=np.full((128, 1), 1.0 / 16.0, dtype=np.float32),
        ones_1x128=np.ones((1, 128), dtype=np.float32),
        c127_1x128=np.full((1, 128), 127.0, dtype=np.float32),
        mask4=np.tile(np.where(j[:, None] <= j[None, :], 1.0, 0.0).astype(np.float32), (1, 4)),
        c_cols=np.tile(np.array([[1.0, 1e-5, 1e-6, LN8, 0, 0, 0, 0]], np.float32), (128, 1)),
    )


_CACHED = {}


def kernel(L0_attn_x, L0_attn_Wq, L0_attn_Wk, L0_attn_Wv, L0_attn_Wg, L0_attn_Wgt,
           L0_attn_Wo, L0_ffn_W_in, L0_ffn_b_in, L0_ffn_W_out, L0_ffn_b_out):
    C = _make_consts()
    if "nc" not in _CACHED:
        _CACHED["nc"] = build(bass.Bass("TRN2", target_bir_lowering=False))
    nc = _CACHED["nc"]
    in_maps = []
    for c in range(8):
        m = {
            "x": np.ascontiguousarray(np.asarray(L0_attn_x)[c]),
            "Wq": np.asarray(L0_attn_Wq), "Wk": np.asarray(L0_attn_Wk),
            "Wv": np.asarray(L0_attn_Wv), "Wg": np.asarray(L0_attn_Wg),
            "Wgt": np.asarray(L0_attn_Wgt), "Wo": np.asarray(L0_attn_Wo),
            "W_in": np.asarray(L0_ffn_W_in),
            "b_in_cols": np.ascontiguousarray(np.asarray(L0_ffn_b_in).reshape(NBF, 128).T),
            "W_out": np.asarray(L0_ffn_W_out),
            "b_out": np.ascontiguousarray(np.asarray(L0_ffn_b_out).reshape(1, D)),
        }
        m.update(C)
        in_maps.append(m)
    res = run_bass_kernel_spmd(nc, in_maps, core_ids=list(range(8)))
    return np.stack([np.asarray(res.results[c]["out"]) for c in range(8)], axis=0)


# version 38
# speedup vs baseline: 1.0104x; 1.0041x over previous
import contextlib
import numpy as np
import ml_dtypes
import concourse.bass as bass
import concourse.mybir as mybir
from concourse.bass_utils import run_bass_kernel_spmd

F32 = mybir.dt.float32
BF16 = mybir.dt.bfloat16
AF = mybir.ActivationFunctionType
ALU = mybir.AluOpType
AX = mybir.AxisListType
MAGIC = 12582912.0

NT = 4096
D = 1024
DFF = 4096
NH = 16
DH = 64
CH = 128
NCH = NT // CH
NBK = D // 128
NBF = DFF // 128
LN8 = float(np.log(1.0 / 8.0))

ENGINES = ("sync", "gpsimd", "vector", "scalar", "tensor")


class Prog:
    def __init__(self, nc, stack, sem_names):
        self.nc = nc
        self.ops = {e: [] for e in ENGINES}
        self.sem = {n: stack.enter_context(nc.semaphore(n)) for n in sem_names}
        self.cnt = {n: 0 for n in sem_names}

    def emit(self, eng, fn):
        self.ops[eng].append(fn)

    def inc(self, name, k=1):
        self.cnt[name] += k
        return self.sem[name]

    def wait(self, eng, name, val=None):
        v = self.cnt[name] if val is None else val
        hh = self.sem[name]
        self.emit(eng, lambda E, hh=hh, v=v: E.wait_ge(hh, v))

    def run(self, block):
        for eng in ENGINES:
            fns = self.ops[eng]
            if fns:
                getattr(block, eng)(lambda E, fns=fns: [f(E) for f in fns])


def build(nc):
    x_d = nc.dram_tensor("x", [NT, D], F32, kind="ExternalInput").ap()
    wq_d = nc.dram_tensor("Wq", [D, D], F32, kind="ExternalInput").ap()
    wk_d = nc.dram_tensor("Wk", [D, D], F32, kind="ExternalInput").ap()
    wv_d = nc.dram_tensor("Wv", [D, D], F32, kind="ExternalInput").ap()
    wg_d = nc.dram_tensor("Wg", [D, D], F32, kind="ExternalInput").ap()
    wgt_d = nc.dram_tensor("Wgt", [D, NH], F32, kind="ExternalInput").ap()
    wo_d = nc.dram_tensor("Wo", [D, D], F32, kind="ExternalInput").ap()
    W_in = nc.dram_tensor("W_in", [D, DFF], F32, kind="ExternalInput").ap()
    b_in_cols = nc.dram_tensor("b_in_cols", [128, NBF], F32, kind="ExternalInput").ap()
    W_out = nc.dram_tensor("W_out", [DFF, D], F32, kind="ExternalInput").ap()
    b_out_d = nc.dram_tensor("b_out", [1, D], F32, kind="ExternalInput").ap()
    cid_d = nc.dram_tensor("ident_bf", [128, 128], BF16, kind="ExternalInput").ap()
    cidf_d = nc.dram_tensor("ident_f32", [128, 128], F32, kind="ExternalInput").ap()
    ctriu_d = nc.dram_tensor("triu16", [128, 128], F32, kind="ExternalInput").ap()
    cslo_d = nc.dram_tensor("slo16", [128, 128], F32, kind="ExternalInput").ap()
    cones16_d = nc.dram_tensor("onescol16", [128, 1], F32, kind="ExternalInput").ap()
    cones1_d = nc.dram_tensor("ones_1x128", [1, 128], F32, kind="ExternalInput").ap()
    c127_d = nc.dram_tensor("c127_1x128", [1, 128], F32, kind="ExternalInput").ap()
    cmask4_d = nc.dram_tensor("mask4", [128, 512], F32, kind="ExternalInput").ap()
    ccols_d = nc.dram_tensor("c_cols", [128, 8], F32, kind="ExternalInput").ap()
    out_d = nc.dram_tensor("out", [NT, D], F32, kind="ExternalOutput").ap()

    xnT_d = nc.dram_tensor("xnT_spill", [D, NT], BF16, kind="Internal").ap()
    acc_d = nc.dram_tensor("acc_spill", [NT, D], F32, kind="Internal").ap()
    hidT = nc.dram_tensor("hid_spill", [DFF, NT], BF16, kind="Internal").ap()
    wq_bounce = nc.dram_tensor("wq_bounce", [1, DFF], F32, kind="Internal").ap()
    fold_bounce = nc.dram_tensor("fold_bounce", [1, 128], F32, kind="Internal").ap()
    cc1_in = nc.dram_tensor("cc1_in", [1, 128], F32, kind="Internal")
    cc1_out = nc.dram_tensor("cc1_out", [1, 128], F32, kind="Internal")
    cc2_in = nc.dram_tensor("cc2_in", [1, 128], F32, kind="Internal")
    cc2_out = nc.dram_tensor("cc2_out", [1, 128], F32, kind="Internal")

    stack = contextlib.ExitStack()
    with stack:
        def sb(name, shape, dtype=F32):
            return stack.enter_context(nc.sbuf_tensor(name, shape, dtype))

        BA = sb("BA", [128, 75904], BF16)
        FA = sb("FA", [128, 12288], F32)
        WQ = BA[:, 0:8192]; WK = BA[:, 8192:16384]; WV = BA[:, 16384:24576]
        WG = BA[:, 24576:32768]; WO = BA[:, 32768:40960]; WGT = BA[:, 40960:41088]
        xn_bf = [BA[:, 41088:42112], BA[:, 42112:43136]]
        xnT_s = [BA[:, 43136:44160], BA[:, 44160:45184]]
        qe_bf = [BA[:, 45184:46208], BA[:, 46208:47232]]
        ke_bf = [BA[:, 47232:48256], BA[:, 48256:49280]]
        v_bf = [BA[:, 49280:50304], BA[:, 50304:51328]]
        qT_s = [BA[:, 51328:53376], BA[:, 53376:55424]]
        kT_s = [BA[:, 55424:57472], BA[:, 57472:59520]]
        AT_bf = [BA[:, 59520:61568], BA[:, 61568:63616]]
        orms = BA[:, 63616:67712]
        gmac = BA[:, 67712:71808]
        gtmp = BA[:, 71808:72832]
        ogb = BA[:, 72832:73856]
        ogT_s = BA[:, 73856:74880]
        Sdec_bf = BA[:, 74880:75904]
        arenaA = BA[:, 0:32768]
        arenaB = BA[:, 32768:65536]
        hstage = [BA[:, 65536:66560], BA[:, 66560:67584]]
        hdbuf = [BA[:, 67584:68608], BA[:, 68608:69632]]
        xbuf = [FA[:, i*1024:(i+1)*1024] for i in range(4)]
        sqd = FA[:, 4096:5120]
        osq = FA[:, 5120:6144]
        accst = [FA[:, 6144:7168], FA[:, 7168:8192]]
        S_f = FA[0:64, 8192:9216]
        Sdec = FA[0:64, 9216:10240]
        wld = FA[:, 10240:11264]
        lnsq = FA[:, 10240:11264]
        run = FA[:, 0:4096]
        swb = FA[:, 4096:8192]
        wstage = FA[:, 8192:12288]
        ostage = [FA[:, 4096:5120], FA[:, 5120:6144]]
        accbuf = [FA[:, 6144:7168], FA[:, 7168:8192]]
        p3scr = FA[:, 0:1024]
        boutb = FA[:, 8192:9216]
        brow = FA[:, 9216:10240]

        c_id = sb("c_id", [128, 128], BF16)
        c_idf = sb("c_idf", [128, 128], F32)
        c_triu = sb("c_triu", [128, 128], F32)
        c_slo = sb("c_slo", [128, 128], F32)
        c_ones16 = sb("c_ones16", [128, 1], F32)
        c_ones1 = sb("c_ones1", [1, 128], F32)
        c_c127 = sb("c_c127", [1, 128], F32)
        c_mask4 = sb("c_mask4", [128, 512], F32)
        c_cols = sb("s_ccols", [128, 8], F32)
        scal_t = sb("scal_t", [128, 32], F32)
        g16 = [sb(f"g16_{i}", [128, 160], F32) for i in range(2)]
        rst16 = [sb(f"rst16_{i}", [128, 32], F32) for i in range(2)]
        expBc = [sb(f"expBc{i}", [128, 16], F32) for i in range(2)]
        expBrow = [sb(f"expBrow{i}", [1, 16], F32) for i in range(2)]
        xamaxc = sb("xamaxc", [128, 32], F32)
        hqc = sb("hqc", [128, 32], F32)
        amax2c = sb("amax2c", [128, 32], F32)
        sc32amax = sb("sc32amax", [128, 32], F32)
        sc32 = sb("sc32", [128, 32], F32)
        sc2 = sb("sc2", [128, 32], F32)
        s_bic = sb("s_bic", [128, 32], F32)
        rcol = sb("rcol", [128, 1], F32)
        xcol = sb("xcol", [128, 1], F32)
        rows4 = sb("rows4", [1, 512], F32)
        xrow = rows4[0:1, 0:128]; ccs = rows4[0:1, 128:256]
        gs1 = rows4[0:1, 256:384]; gs2 = rows4[0:1, 384:512]
        sc_small = sb("sc_small", [1, 8], F32)
        r_sx_col = sb("r_sx_col", [128, 1], F32)
        s_x_col = sb("s_x_col", [128, 1], F32)
        r_sh_col = sb("r_sh_col", [128, 1], F32)
        s_h_col = sb("s_h_col", [128, 1], F32)
        w32 = sb("w32", [32, 128], F32)
        bnst = sb("bnst", [128, 96], F32)

        def psum(name, shape, dtype=F32):
            return stack.enter_context(nc.psum_tensor(name, shape, dtype))
        pb = [psum(f"pb{i}", [128, 512], F32) for i in range(7)]
        pT = psum("pT", [128, 1024], BF16)

        P = Prog(nc, stack, ["dS", "dG", "dA", "pe", "act", "dve", "cc"])
        E = P.emit

        def Wv_(w, k):
            return w.rearrange("(b p) n -> b p n", p=128)[k]

        def pe_mm_group(mms):
            for i, mm in enumerate(mms):
                if i == len(mms) - 1:
                    h = P.inc("pe")
                    E("tensor", lambda T, mm=mm, h=h: mm(T).then_inc(h, 1))
                else:
                    E("tensor", lambda T, mm=mm: mm(T))
            return P.cnt["pe"]

        bank_free = {i: ("dve", 0) for i in range(8)}

        def pe_wait_bank(i):
            sname, cnt = bank_free[i]
            if cnt > 0:
                P.wait("tensor", sname, cnt)

        for dst, src in [(c_id, cid_d), (c_idf, cidf_d), (c_triu, ctriu_d),
                         (c_slo, cslo_d), (c_ones16, cones16_d), (c_ones1, cones1_d),
                         (c_c127, c127_d), (c_mask4, cmask4_d), (c_cols, ccols_d),
                         (s_bic, b_in_cols)]:
            h = P.inc("dS", 16)
            E("sync", lambda S, dst=dst, src=src, h=h: S.dma_start(out=dst[:], in_=src).then_inc(h, 16))
        E("vector", lambda V: V.memset(xamaxc[:], 0.0))
        E("vector", lambda V: V.memset(S_f[:], 0.0))
        h = P.inc("dve")
        E("vector", lambda V: V.memset(Sdec_bf[:], 0.0))
        E("vector", lambda V, h=h: V.memset(BA[64:128, 51328:59520], 0.0).then_inc(h, 1))
        wld2 = FA[:, 11264:12288]
        wstg = [wld, wld2]
        wjobs = [(WGT, wgt_d, NH, k) for k in range(NBK)]
        wjobs += [(W, wd, D, k) for (W, wd) in
                  [(WQ, wq_d), (WK, wk_d), (WV, wv_d), (WG, wg_d), (WO, wo_d)]
                  for k in range(NBK)]
        w_cast_at = {}
        w_milestones = {}
        for j, (W_s, w_dd, ncols, k) in enumerate(wjobs):
            qe_, qs_ = (("sync", "dS") if j % 2 == 0 else ("scalar", "dA"))
            ce_, cs_ = (("vector", "dve") if j % 2 == 0 else ("scalar", "act"))
            if (j - 2) in w_cast_at:
                P.wait(qe_, w_cast_at[j - 2][0], w_cast_at[j - 2][1])
            h = P.inc(qs_, 16)
            E(qe_, lambda S, w_dd=w_dd, k=k, ncols=ncols, j=j, h=h: S.dma_start(
                out=wstg[j % 2][:, 0:ncols], in_=Wv_(w_dd, k)).then_inc(h, 16))
            P.wait(ce_, qs_, P.cnt[qs_])
            h = P.inc(cs_)
            if ce_ == "vector":
                E(ce_, lambda V, W_s=W_s, k=k, ncols=ncols, j=j, h=h: V.tensor_copy(
                    W_s[:, k*ncols:(k+1)*ncols], wstg[j % 2][:, 0:ncols]).then_inc(h, 1))
            else:
                E(ce_, lambda S, W_s=W_s, k=k, ncols=ncols, j=j, h=h: S.copy(
                    W_s[:, k*ncols:(k+1)*ncols], wstg[j % 2][:, 0:ncols]).then_inc(h, 1))
            w_cast_at[j] = (cs_, P.cnt[cs_])

        acc_evac_at = {}
        scal_free_at = {}
        o_done_at = {}
        xnT_spill_at = {}
        acc_dma_at = {}
        ebc_done_at = {}
        chunk_end = {}
        xn_ready_at = {}
        trsp_done_at = {}
        rsq_act_at = {}
        ln_rsq_at = {}
        tail_og3_at = None

        x_loaded_at = {}

        def emit_xload(cc):
            xb = xbuf[cc % 4]
            if cc >= 4 and (cc - 4) in acc_evac_at:
                P.wait("sync", "dve", acc_evac_at[cc - 4])
            h = P.inc("dS", 16)
            E("sync", lambda S, cc=cc, xb=xb, h=h: S.dma_start(
                out=xb, in_=x_d.rearrange("(c p) d -> c p d", p=128)[cc]).then_inc(h, 16))
            x_loaded_at[cc] = P.cnt["dS"]

        def emit_ln(cc):
            par = cc % 2
            xb = xbuf[cc % 4]
            bb = 48 + par * 16
            x_loaded = x_loaded_at[cc]
            P.wait("vector", "dS", x_loaded)
            if (cc - 2) in ln_rsq_at:
                P.wait("vector", "act", ln_rsq_at[cc - 2])
            for half in range(2):
                h = P.inc("dve")
                E("vector", lambda V, xb=xb, half=half, bb=bb, h=h: V.bn_stats(
                    bnst[:, bb + half*6: bb + half*6 + 6],
                    xb[:, half*512:(half+1)*512]).then_inc(h, 1))
            P.wait("vector", "dve")
            h = P.inc("dve")
            E("vector", lambda V, bb=bb, h=h: V.bn_aggr(
                bnst[:, bb+12:bb+14], bnst[:, bb:bb+12]).then_inc(h, 1))
            P.wait("scalar", "dve")
            h = P.inc("act")
            E("scalar", lambda S, bb=bb, h=h: S.activation(
                bnst[:, bb+15:bb+16], bnst[:, bb+13:bb+14],
                AF.Ln, bias=c_cols[:, 1:2]).then_inc(h, 1))
            P.wait("scalar", "act")
            h = P.inc("act")
            E("scalar", lambda S, bb=bb, h=h: S.activation(
                bnst[:, bb+14:bb+15], bnst[:, bb+15:bb+16],
                AF.Exp, scale=-0.5).then_inc(h, 1))
            ln_rsq_at[cc] = P.cnt["act"]
            P.wait("vector", "act")
            if (cc - 2) in trsp_done_at:
                P.wait("vector", "pe", trsp_done_at[cc - 2])
            h = P.inc("dve")
            E("vector", lambda V, xb=xb, par=par, bb=bb, h=h: V.tensor_scalar(
                xn_bf[par][:], xb, bnst[:, bb+12:bb+13], bnst[:, bb+14:bb+15],
                ALU.subtract, ALU.mult).then_inc(h, 1))
            xn_ready_at[cc] = P.cnt["dve"]
            h = P.inc("dve")
            E("vector", lambda V, par=par, cc=cc, h=h: V.tensor_reduce(
                xamaxc[:, cc:cc+1], xn_bf[par][:], AX.X, ALU.max, apply_absolute_value=True).then_inc(h, 1))

        xnT_evac_at = {}

        def emit_xnT(cc):
            par = cc % 2
            pe_wait_bank(7)
            P.wait("tensor", "dve", xn_ready_at[cc])
            pe_mm_group([
                (lambda T, f=f, par=par: T.transpose(pT[:, f*128:(f+1)*128],
                                                     xn_bf[par][:, f*128:(f+1)*128], c_id[:]))
                for f in range(NBK)])
            trsp_done_at[cc] = P.cnt["pe"]
            P.wait("scalar", "pe")
            if (cc - 2) in xnT_spill_at:
                P.wait("scalar", "dS", xnT_spill_at[cc - 2])
            h = P.inc("act")
            E("scalar", lambda S, par=par, h=h: S.copy(xnT_s[par][:], pT[:]).then_inc(h, 1))
            xnT_evac_at[cc] = P.cnt["act"]
            bank_free[7] = ("act", xnT_evac_at[cc])
            P.wait("sync", "act", xnT_evac_at[cc])
            h = P.inc("dS", 16)
            E("sync", lambda S, cc=cc, par=par, h=h: S.dma_start(
                out=xnT_d.rearrange("(b p) T -> p b T", p=128)[:, :, cc*128:(cc+1)*128],
                in_=xnT_s[par].rearrange("p (b t) -> p b t", t=128)).then_inc(h, 16))
            xnT_spill_at[cc] = P.cnt["dS"]

        emit_xload(0)
        emit_ln(0)
        emit_xnT(0)

        for c in range(NCH):
            par = c % 2
            xb = xbuf[c % 4]
            mc = c % 4
            if (c - 2) in chunk_end:
                pe2 = chunk_end[c - 2]
                for eng in ("tensor", "vector", "scalar"):
                    for semn in ("pe", "act", "dve"):
                        P.wait(eng, semn, pe2[semn])
            if mc < 3 and c + 1 < NCH:
                emit_xload(c + 1)

            pe_wait_bank(4)
            P.wait("tensor", "act", xnT_evac_at[c])
            z_done = pe_mm_group([
                (lambda T, k=k, par=par: T.matmul(pb[4][:, 0:16], xnT_s[par][:, k*128:(k+1)*128],
                                                  WGT[:, k*16:(k+1)*16],
                                                  start=(k == 0), stop=(k == NBK - 1)))
                for k in range(NBK)])

            def proj_mms(W_s, banks):
                return pe_mm_group([
                    (lambda T, k=k, nh_=nh_, W_s=W_s, par=par, b=banks[nh_]: T.matmul(
                        pb[b][:], xnT_s[par][:, k*128:(k+1)*128],
                        W_s[:, k*D + nh_*512: k*D + nh_*512 + 512],
                        start=(k == 0), stop=(k == NBK - 1)))
                    for k in range(NBK) for nh_ in range(2)])

            gg = g16[par]
            P.wait("vector", "pe", z_done)
            h = P.inc("dve")
            E("vector", lambda V, gg=gg, h=h: V.tensor_single_scalar(
                gg[:, 0:16], pb[4][:, 0:16], 0.0, ALU.min).then_inc(h, 1))
            P.wait("scalar", "pe", z_done)
            h = P.inc("act")
            E("scalar", lambda S, gg=gg, h=h: S.activation(gg[:, 16:32], pb[4][:, 0:16], AF.Abs).then_inc(h, 1))
            P.wait("scalar", "act")
            h = P.inc("act")
            E("scalar", lambda S, gg=gg, h=h: S.activation(gg[:, 32:48], gg[:, 16:32], AF.Exp, scale=-1.0).then_inc(h, 1))
            P.wait("scalar", "act")
            h = P.inc("act")
            E("scalar", lambda S, gg=gg, h=h: S.activation(gg[:, 48:64], gg[:, 32:48], AF.Ln, bias=c_cols[:, 0:1]).then_inc(h, 1))
            P.wait("vector", "act")
            P.wait("vector", "dve")
            h = P.inc("dve")
            E("vector", lambda V, gg=gg, h=h: V.tensor_tensor(gg[:, 64:80], gg[:, 0:16], gg[:, 48:64], ALU.subtract).then_inc(h, 1))
            gt_ready = P.cnt["dve"]

            projd = {}
            pe_wait_bank(0); pe_wait_bank(1)
            projd[0] = proj_mms(WQ, (0, 1))

            P.wait("tensor", "dve", gt_ready)
            gmm_done = pe_mm_group([
                (lambda T, gg=gg: T.matmul(pb[4][:, 16:32], c_triu[:], gg[:, 64:80], start=True, stop=True)),
                (lambda T, gg=gg: T.matmul(pb[4][:, 32:48], c_slo[:], gg[:, 64:80], start=True, stop=True)),
                (lambda T, gg=gg: T.matmul(pb[4][0:1, 48:64], c_ones16[:], gg[:, 64:80], start=True, stop=True)),
            ])
            P.wait("scalar", "pe", gmm_done)
            if (c - 2) in ebc_done_at:
                P.wait("scalar", "pe", ebc_done_at[c - 2])
            h = P.inc("act")
            E("scalar", lambda S, gg=gg, h=h: S.activation(gg[:, 80:96], pb[4][:, 32:48], AF.Exp, bias=c_cols[:, 3:4]).then_inc(h, 1))
            h = P.inc("act")
            E("scalar", lambda S, gg=gg, h=h: S.activation(gg[:, 96:112], pb[4][:, 32:48], AF.Exp, scale=-1.0).then_inc(h, 1))
            h = P.inc("act")
            E("scalar", lambda S, par=par, h=h: S.activation(expBrow[par][:], pb[4][0:1, 48:64], AF.Exp).then_inc(h, 1))
            exps_done = P.cnt["act"]

            pe_wait_bank(2); pe_wait_bank(3)
            projd[1] = proj_mms(WK, (2, 3))

            P.wait("tensor", "act", exps_done)
            ebc = pe_mm_group([
                (lambda T, par=par: T.matmul(pb[4][0:128, 64:80], c_ones1[0:1, 0:128],
                                             expBrow[par][:], start=True, stop=True))])
            ebc_done_at[c] = ebc
            P.wait("scalar", "pe", ebc)
            h = P.inc("act")
            E("scalar", lambda S, par=par, h=h: S.copy(expBc[par][:], pb[4][0:128, 64:80]).then_inc(h, 1))
            ebc_copy_at = P.cnt["act"]
            bank_free[4] = ("act", ebc_copy_at)

            pe_wait_bank(5); pe_wait_bank(6)
            vd = pe_mm_group([
                (lambda T, k=k, nh_=nh_, par=par, b=(5, 6)[nh_]: T.matmul(
                    pb[b][:], xnT_s[par][:, k*128:(k+1)*128],
                    WV[:, k*D + nh_*512: k*D + nh_*512 + 512],
                    start=(k == 0), stop=(k == NBK - 1)))
                for k in range(NBK) for nh_ in range(2)])

            proj_evac = {}
            rq_slots = {0: 9, 1: 10}
            for pi, banks in [(0, (0, 1)), (1, (2, 3))]:
                bb = pi * 16
                P.wait("vector", "pe", projd[pi])
                if (pi, c - 1) in rsq_act_at:
                    P.wait("vector", "act", rsq_act_at[(pi, c - 1)])
                for half in range(2):
                    h = P.inc("dve")
                    E("vector", lambda V, b=banks[half], half=half, bb=bb, h=h: V.bn_stats(
                        bnst[:, bb + half*6: bb + half*6 + 6], pb[b][:]).then_inc(h, 1))
                P.wait("vector", "dve")
                h = P.inc("dve")
                E("vector", lambda V, bb=bb, h=h: V.bn_aggr(
                    bnst[:, bb+12:bb+14], bnst[:, bb:bb+12]).then_inc(h, 1))
                P.wait("vector", "dve")
                h = P.inc("dve")
                E("vector", lambda V, bb=bb, h=h: V.tensor_scalar(
                    bnst[:, bb+14:bb+15], bnst[:, bb+12:bb+13],
                    bnst[:, bb+12:bb+13], bnst[:, bb+13:bb+14],
                    ALU.mult, ALU.add).then_inc(h, 1))
                P.wait("scalar", "dve")
                h = P.inc("act")
                E("scalar", lambda S, bb=bb, h=h: S.activation(
                    bnst[:, bb+15:bb+16], bnst[:, bb+14:bb+15],
                    AF.Ln, bias=c_cols[:, 2:3]).then_inc(h, 1))
                P.wait("scalar", "act")
                h = P.inc("act")
                E("scalar", lambda S, pi=pi, par=par, bb=bb, h=h: S.activation(
                    scal_t[:, par*16+rq_slots[pi]:par*16+rq_slots[pi]+1],
                    bnst[:, bb+15:bb+16], AF.Exp, scale=-0.5).then_inc(h, 1))
                rstd_done = P.cnt["act"]
                rsq_act_at[(pi, c)] = rstd_done
                P.wait("vector", "act", max(rstd_done, exps_done))
                src_off = 80 if pi == 0 else 96
                dst_off = 112 if pi == 0 else 128
                h = P.inc("dve")
                E("vector", lambda V, gg=gg, src_off=src_off, dst_off=dst_off, pi=pi, par=par, h=h: V.tensor_scalar(
                    gg[:, dst_off:dst_off+16], gg[:, src_off:src_off+16],
                    scal_t[:, par*16+rq_slots[pi]:par*16+rq_slots[pi]+1], None, ALU.mult).then_inc(h, 1))
                P.wait("vector", "dve")
                dst = qe_bf if pi == 0 else ke_bf
                for half in range(2):
                    h = P.inc("dve")
                    def tt(V, dst=dst, b=banks[half], half=half, par=par, gg=gg, dst_off=dst_off, h=h):
                        p3 = pb[b][:].rearrange("q (g e) -> q g e", e=64)
                        s3 = gg[:, dst_off + half*8: dst_off + half*8 + 8].rearrange("q (g e) -> q g e", e=1)
                        a3, s3b = bass.broadcast_tensor_aps(p3, s3)
                        d3 = dst[par][:, half*512:(half+1)*512].rearrange("q (g e) -> q g e", e=64)
                        return V.tensor_tensor(d3, a3, s3b, ALU.mult).then_inc(h, 1)
                    E("vector", tt)
                bank_free[banks[0]] = ("dve", P.cnt["dve"] - 1)
                bank_free[banks[1]] = ("dve", P.cnt["dve"])
                proj_evac[pi] = P.cnt["dve"]
            qk_evac = P.cnt["dve"]

            P.wait("vector", "pe", vd)
            if ("v", c - 1) in rsq_act_at:
                P.wait("vector", "act", rsq_act_at[("v", c - 1)])
            for half in range(2):
                h = P.inc("dve")
                E("vector", lambda V, half=half, b=(5, 6)[half], h=h: V.bn_stats(
                    bnst[:, 32 + half*6: 32 + half*6 + 6], pb[b][:]).then_inc(h, 1))
            P.wait("vector", "dve")
            h = P.inc("dve")
            E("vector", lambda V, h=h: V.bn_aggr(bnst[:, 44:46], bnst[:, 32:44]).then_inc(h, 1))
            P.wait("vector", "dve")
            h = P.inc("dve")
            E("vector", lambda V, h=h: V.tensor_scalar(
                bnst[:, 46:47], bnst[:, 44:45], bnst[:, 44:45], bnst[:, 45:46],
                ALU.mult, ALU.add).then_inc(h, 1))
            P.wait("scalar", "dve")
            h = P.inc("act")
            E("scalar", lambda S, h=h: S.activation(
                bnst[:, 47:48], bnst[:, 46:47],
                AF.Ln, bias=c_cols[:, 2:3]).then_inc(h, 1))
            P.wait("scalar", "act")
            h = P.inc("act")
            E("scalar", lambda S, par=par, h=h: S.activation(
                scal_t[:, par*16+11:par*16+12], bnst[:, 47:48],
                AF.Exp, scale=-0.5).then_inc(h, 1))
            rsq_act_at[("v", c)] = P.cnt["act"]
            P.wait("scalar", "act")
            for half in range(2):
                h = P.inc("act")
                E("scalar", lambda S, half=half, b=(5, 6)[half], par=par, h=h: S.activation(
                    v_bf[par][:, half*512:(half+1)*512], pb[b][:], AF.Copy,
                    scale=scal_t[:, par*16+11:par*16+12]).then_inc(h, 1))
            bank_free[5] = ("act", P.cnt["act"] - 1)
            bank_free[6] = ("act", P.cnt["act"])
            v_evac = P.cnt["act"]
            scal_free_at[c] = ("act", v_evac)


            pe_wait_bank(2); pe_wait_bank(3)
            gd = pe_mm_group([
                (lambda T, k=k, nh_=nh_, par=par, b=(2, 3)[nh_]: T.matmul(
                    pb[b][:], xnT_s[par][:, k*128:(k+1)*128],
                    WG[:, k*D + nh_*512: k*D + nh_*512 + 512],
                    start=(k == 0), stop=(k == NBK - 1)))
                for k in range(NBK) for nh_ in range(2)])
            P.wait("scalar", "pe", gd)
            for half in range(2):
                h = P.inc("act")
                E("scalar", lambda S, half=half, b=(2, 3)[half], mc=mc, h=h: S.copy(
                    gmac[:, mc*1024 + half*512: mc*1024 + (half+1)*512], pb[b][:]).then_inc(h, 1))
            bank_free[2] = ("act", P.cnt["act"] - 1)
            bank_free[3] = ("act", P.cnt["act"])

            for (src, dstT) in [(qe_bf, qT_s), (ke_bf, kT_s)]:
                for half in range(2):
                    pe_wait_bank(7)
                    P.wait("tensor", "dve", proj_evac[0] if src is qe_bf else proj_evac[1])
                    pe_mm_group([
                        (lambda T, hh=hh, half=half, src=src, par=par: T.transpose(
                            pT[0:64, hh*128:(hh+1)*128],
                            src[par][:, (half*8+hh)*64:(half*8+hh)*64+64], c_id[:]))
                        for hh in range(8)])
                    P.wait("scalar", "pe")
                    h = P.inc("act")
                    E("scalar", lambda S, dstT=dstT, half=half, par=par, h=h: S.copy(
                        dstT[par][0:64, half*1024:(half+1)*1024], pT[0:64, :]).then_inc(h, 1))
                    bank_free[7] = ("act", P.cnt["act"])
            kT_evac = P.cnt["act"]

            if (c - 1) in o_done_at:
                P.wait("vector", "pe", o_done_at[c - 1])
            P.wait("vector", "act", ebc_copy_at)
            for hd in range(NH):
                if hd < NH - 1:
                    E("vector", lambda V, hd=hd, par=par: V.tensor_scalar(
                        Sdec[:, hd*64:(hd+1)*64], S_f[:, hd*64:(hd+1)*64],
                        expBc[par][0:64, hd:hd+1], None, ALU.mult))
                else:
                    h = P.inc("dve")
                    E("vector", lambda V, hd=hd, par=par, h=h: V.tensor_scalar(
                        Sdec[:, hd*64:(hd+1)*64], S_f[:, hd*64:(hd+1)*64],
                        expBc[par][0:64, hd:hd+1], None, ALU.mult).then_inc(h, 1))
            P.wait("vector", "dve")
            h = P.inc("dve")
            E("vector", lambda V, h=h: V.tensor_copy(Sdec_bf[0:64, :], Sdec[:]).then_inc(h, 1))
            sdec_done = P.cnt["dve"]

            P.wait("tensor", "act", kT_evac)
            at_last = 0
            for gidx in range(4):
                ab = 5 + (gidx % 2)
                pe_wait_bank(ab)
                agrp = pe_mm_group([
                    (lambda T, hd=gidx*4+hh, hh=hh, ab=ab, par=par: T.matmul(
                        pb[ab][:, hh*128:(hh+1)*128],
                        kT_s[par][:, hd*128:(hd+1)*128], qT_s[par][:, hd*128:(hd+1)*128],
                        start=True, stop=True))
                    for hh in range(4)])
                P.wait("vector", "pe", agrp)
                h = P.inc("dve")
                E("vector", lambda V, gidx=gidx, ab=ab, par=par, h=h: V.tensor_tensor(
                    AT_bf[par][:, gidx*512:(gidx+1)*512], pb[ab][:], c_mask4[:], ALU.mult).then_inc(h, 1))
                bank_free[ab] = ("dve", P.cnt["dve"])
                at_last = P.cnt["dve"]

            for wave in range(2):
                pe_wait_bank(4)
                P.wait("tensor", "dve", qk_evac)
                P.wait("tensor", "act", v_evac - 1 + wave)
                kvd = pe_mm_group([
                    (lambda T, hd=wave*8+hh, hh=hh, par=par: T.matmul(
                        pb[4][0:64, hh*64:(hh+1)*64],
                        ke_bf[par][:, hd*64:(hd+1)*64], v_bf[par][:, hd*64:(hd+1)*64],
                        start=True, stop=True))
                    for hh in range(8)])
                P.wait("vector", "pe", kvd)
                P.wait("vector", "dve", sdec_done)
                h = P.inc("dve")
                E("vector", lambda V, wave=wave, h=h: V.tensor_tensor(
                    S_f[:, wave*512:(wave+1)*512], Sdec[:, wave*512:(wave+1)*512],
                    pb[4][0:64, 0:512], ALU.add).then_inc(h, 1))
                bank_free[4] = ("dve", P.cnt["dve"])

            if mc < 3 and c + 1 < NCH:
                emit_ln(c + 1)

            pe_wait_bank(0); pe_wait_bank(1)
            P.wait("tensor", "dve", max(at_last, sdec_done))
            o_half = {}
            for ob in range(2):
                omms = []
                for hd in range(ob*8, ob*8 + 8):
                    off = (hd % 8) * 64
                    omms.append(lambda T, hd=hd, ob=ob, off=off, par=par: T.matmul(
                        pb[ob][:, off:off+64], AT_bf[par][:, hd*128:(hd+1)*128],
                        v_bf[par][:, hd*64:(hd+1)*64], start=True, stop=False))
                    omms.append(lambda T, hd=hd, ob=ob, off=off, par=par: T.matmul(
                        pb[ob][:, off:off+64], qT_s[par][:, hd*128:(hd+1)*128],
                        Sdec_bf[:, hd*64:(hd+1)*64], start=False, stop=True))
                o_half[ob] = pe_mm_group(omms)
            o_done = o_half[1]
            o_done_at[c] = o_done

            if mc == 0 and tail_og3_at is not None:
                P.wait("scalar", "dve", tail_og3_at)
            sq_at = {}
            for half in range(2):
                P.wait("scalar", "pe", o_half[half])
                h = P.inc("act")
                E("scalar", lambda S, half=half, h=h: S.activation(
                    osq[:, half*512:(half+1)*512], pb[half][:], AF.Square).then_inc(h, 1))
                sq_at[half] = P.cnt["act"]
            for half in range(2):
                P.wait("vector", "act", sq_at[half])
                h = P.inc("dve")
                E("vector", lambda V, par=par, half=half, h=h: V.tensor_reduce(
                    rst16[par][:, 16 + half*8:24 + half*8].rearrange("q (g o) -> q g o", o=1),
                    osq[:, half*512:(half+1)*512].rearrange("q (g e) -> q g e", e=64),
                    AX.X, ALU.add).then_inc(h, 1))
            P.wait("scalar", "dve")
            h = P.inc("act")
            E("scalar", lambda S, par=par, h=h: S.activation(
                rst16[par][:, 0:16], rst16[par][:, 16:32], AF.Ln,
                scale=1.0 / DH, bias=c_cols[:, 2:3]).then_inc(h, 1))
            P.wait("scalar", "act")
            h = P.inc("act")
            E("scalar", lambda S, par=par, h=h: S.activation(
                rst16[par][:, 0:16], rst16[par][:, 0:16], AF.Exp, scale=-0.5).then_inc(h, 1))
            rstd_done = P.cnt["act"]
            P.wait("vector", "act", rstd_done)
            for half in range(2):
                h = P.inc("dve")
                def tt2(V, half=half, par=par, mc=mc, h=h):
                    p3 = pb[half][:].rearrange("q (g e) -> q g e", e=64)
                    s3 = rst16[par][:, half*8:half*8+8].rearrange("q (g e) -> q g e", e=1)
                    a3, s3b = bass.broadcast_tensor_aps(p3, s3)
                    d3 = orms[:, mc*1024 + half*512: mc*1024 + (half+1)*512].rearrange("q (g e) -> q g e", e=64)
                    return V.tensor_tensor(d3, a3, s3b, ALU.mult).then_inc(h, 1)
                E("vector", tt2)
            bank_free[0] = ("dve", P.cnt["dve"] - 1)
            bank_free[1] = ("dve", P.cnt["dve"])

            if mc < 3 and c + 1 < NCH:
                emit_xnT(c + 1)

            chunk_end[c] = {k: P.cnt[k] for k in ("pe", "act", "dve")}

            if mc == 3:
                m0 = c - 3
                gbuf = [FA[:, 4096:5120], FA[:, 10240:11264],
                        FA[:, 11264:12288], osq]
                ogbs = [ogb, BA[:, 71808:72832]]
                ogTs = [ogT_s, BA[:, 69632:70656]]
                P.wait("scalar", "dve")
                silu_at = {}
                for cc2 in range(4):
                    h = P.inc("act")
                    E("scalar", lambda S, cc2=cc2, h=h: S.activation(
                        gbuf[cc2][:], gmac[:, cc2*1024:(cc2+1)*1024], AF.Silu).then_inc(h, 1))
                    silu_at[cc2] = P.cnt["act"]
                trsp_tail = {}
                wo_tail = {}
                for cc2 in range(4):
                    cglob = m0 + cc2
                    P.wait("vector", "act", silu_at[cc2])
                    if (cc2 - 2) in trsp_tail:
                        P.wait("vector", "pe", trsp_tail[cc2 - 2])
                    h = P.inc("dve")
                    E("vector", lambda V, cc2=cc2, h=h: V.tensor_tensor(
                        ogbs[cc2 % 2][:], orms[:, cc2*1024:(cc2+1)*1024],
                        gbuf[cc2][:], ALU.mult).then_inc(h, 1))
                    og_done = P.cnt["dve"]
                    if cc2 == 3:
                        tail_og3_at = og_done
                    pe_wait_bank(7)
                    P.wait("tensor", "dve", og_done)
                    pe_mm_group([
                        (lambda T, f=f, cc2=cc2: T.transpose(pT[:, f*128:(f+1)*128],
                                                             ogbs[cc2 % 2][:, f*128:(f+1)*128], c_id[:]))
                        for f in range(NBK)])
                    trsp_tail[cc2] = P.cnt["pe"]
                    P.wait("scalar", "pe", trsp_tail[cc2])
                    h = P.inc("act")
                    E("scalar", lambda S, cc2=cc2, h=h: S.copy(ogTs[cc2 % 2][:], pT[:]).then_inc(h, 1))
                    ogT_evac = P.cnt["act"]
                    bank_free[7] = ("act", ogT_evac)
                    wbk = (2, 3) if cc2 % 2 == 0 else (5, 6)
                    pe_wait_bank(wbk[0]); pe_wait_bank(wbk[1])
                    P.wait("tensor", "act", ogT_evac)
                    wo_done = pe_mm_group([
                        (lambda T, k=k, nh_=nh_, cc2=cc2, wbk=wbk: T.matmul(
                            pb[wbk[nh_]][:], ogTs[cc2 % 2][:, k*128:(k+1)*128],
                            WO[:, k*D + nh_*512: k*D + nh_*512 + 512],
                            start=(k == 0), stop=(k == NBK - 1)))
                        for k in range(NBK) for nh_ in range(2)])
                    wo_tail[cc2] = wo_done
                    P.wait("vector", "pe", wo_done)
                    if (cglob - 2) in acc_dma_at:
                        P.wait("vector", "dS", acc_dma_at[cglob - 2])
                    for nh_ in range(2):
                        h = P.inc("dve")
                        E("vector", lambda V, nh_=nh_, cglob=cglob, wbk=wbk, h=h: V.scalar_tensor_tensor(
                            accst[cglob % 2][:, nh_*512:(nh_+1)*512], pb[wbk[nh_]][:], 1.0,
                            xbuf[cglob % 4][:, nh_*512:(nh_+1)*512], ALU.mult, ALU.add).then_inc(h, 1))
                    acc_done = P.cnt["dve"]
                    bank_free[wbk[0]] = ("dve", acc_done - 1)
                    bank_free[wbk[1]] = ("dve", acc_done)
                    acc_evac_at[cglob] = acc_done
                    P.wait("sync", "dve", acc_done)
                    h = P.inc("dS", 16)
                    E("sync", lambda S, cglob=cglob, h=h: S.dma_start(
                        out=acc_d.rearrange("(c p) d -> c p d", p=128)[cglob],
                        in_=accst[cglob % 2][:]).then_inc(h, 16))
                    acc_dma_at[cglob] = P.cnt["dS"]
                chunk_end[c] = {k: P.cnt[k] for k in ("pe", "act", "dve")}
                if c + 1 < NCH:
                    emit_xload(c + 1)
                    emit_ln(c + 1)
                    emit_xnT(c + 1)

        for eng in ("tensor", "vector", "scalar", "sync", "gpsimd"):
            for semn in ("pe", "act", "dve"):
                P.wait(eng, semn)
        P.wait("vector", "dS")
        P.wait("tensor", "dS")
        P.wait("scalar", "dS")

        h = P.inc("dve")
        E("vector", lambda V, h=h: V.tensor_reduce(xcol[:], xamaxc[:], AX.X, ALU.max).then_inc(h, 1))
        P.wait("sync", "dve")
        h = P.inc("dS", 16)
        E("sync", lambda S, h=h: S.dma_start(out=fold_bounce.rearrange("o p -> p o"), in_=xcol[:]).then_inc(h, 16))
        P.wait("sync", "dS")
        h = P.inc("dS", 16)
        E("sync", lambda S, h=h: S.dma_start(out=xrow, in_=fold_bounce).then_inc(h, 16))
        P.wait("vector", "dS")
        h = P.inc("dve")
        E("vector", lambda V, h=h: V.memset(ccs, 0.0).then_inc(h, 1))
        P.wait("vector", "dve")
        h = P.inc("dve")
        E("vector", lambda V, h=h: V.tensor_reduce(ccs[0:1, 0:1], xrow, AX.X, ALU.max).then_inc(h, 1))
        P.wait("gpsimd", "dve")
        h = P.inc("dG", 16)
        E("gpsimd", lambda G, h=h: G.dma_start(out=cc1_in.ap(), in_=ccs).then_inc(h, 16))
        P.wait("gpsimd", "dG")
        h = P.inc("cc")
        E("gpsimd", lambda G, h=h: G.collective_compute(
            "AllReduce", ALU.max, replica_groups=[list(range(8))],
            ins=[cc1_in.ap().opt()], outs=[cc1_out.ap().opt()]).then_inc(h, 1))
        P.wait("gpsimd", "cc")
        h = P.inc("dG", 16)
        E("gpsimd", lambda G, h=h: G.dma_start(out=gs1, in_=cc1_out.ap()).then_inc(h, 16))
        P.wait("vector", "dG")
        h = P.inc("dve")
        E("vector", lambda V, h=h: V.tensor_scalar_mul(sc_small[0:1, 0:1], gs1[0:1, 0:1], 1.0 / 32767.0).then_inc(h, 1))
        P.wait("tensor", "dve")
        pe_wait_bank(4)
        rsxmm = pe_mm_group([
            (lambda T: T.matmul(pb[4][:, 0:1], c_ones1[:], sc_small[0:1, 0:1], start=True, stop=True))])
        P.wait("scalar", "pe", rsxmm)
        h = P.inc("act")
        E("scalar", lambda S, h=h: S.copy(r_sx_col[:], pb[4][:, 0:1]).then_inc(h, 1))
        bank_free[4] = ("act", P.cnt["act"])
        P.wait("vector", "act")
        h = P.inc("dve")
        E("vector", lambda V, h=h: V.reciprocal(s_x_col[:], r_sx_col[:]).then_inc(h, 1))
        P.wait("vector", "dve")

        qdone = {}
        ldone = {}
        for i in range(2 * NBK):
            k, hf = i // 2, i % 2
            qe_, qs_ = (("sync", "dS") if i % 2 == 0 else ("scalar", "dA"))
            if (i - 2) in qdone:
                P.wait(qe_, "dve", qdone[i - 2])
            h = P.inc(qs_, 16)
            E(qe_, lambda S, k=k, hf=hf, i=i, h=h: S.dma_start(
                out=wstage[:, (i % 2)*2048:(i % 2)*2048+2048],
                in_=Wv_(W_in, k)[:, hf*2048:(hf+1)*2048]).then_inc(h, 16))
            ldone[i] = P.cnt[qs_]
            P.wait("vector", qs_, ldone[i])
            if k == 0:
                h = P.inc("dve")
                E("vector", lambda V, i=i, hf=hf, h=h: V.tensor_tensor(
                    run[:, hf*2048:(hf+1)*2048], wstage[:, (i % 2)*2048:(i % 2)*2048+2048],
                    wstage[:, (i % 2)*2048:(i % 2)*2048+2048], ALU.mult).then_inc(h, 1))
            else:
                E("vector", lambda V, i=i, hf=hf: V.tensor_tensor(
                    swb[:, hf*2048:(hf+1)*2048], wstage[:, (i % 2)*2048:(i % 2)*2048+2048],
                    wstage[:, (i % 2)*2048:(i % 2)*2048+2048], ALU.mult))
                h = P.inc("dve")
                E("vector", lambda V, hf=hf, h=h: V.tensor_tensor(
                    run[:, hf*2048:(hf+1)*2048], run[:, hf*2048:(hf+1)*2048],
                    swb[:, hf*2048:(hf+1)*2048], ALU.max).then_inc(h, 1))
            qdone[i] = P.cnt["dve"]
        for L in (64, 32, 16, 8, 4, 2, 1):
            P.wait("sync", "dve")
            h = P.inc("dS", 16)
            E("sync", lambda S, L=L, h=h: S.dma_start(out=wstage[0:L, :], in_=run[L:2*L, :]).then_inc(h, 16))
            P.wait("vector", "dS")
            h = P.inc("dve")
            E("vector", lambda V, L=L, h=h: V.tensor_tensor(run[0:L, :], run[0:L, :], wstage[0:L, :], ALU.max).then_inc(h, 1))
        P.wait("vector", "dve")
        h = P.inc("dve")
        E("vector", lambda V, h=h: V.tensor_scalar_max(run[0:1, :], run[0:1, :], 1e-16).then_inc(h, 1))
        P.wait("scalar", "dve")
        h = P.inc("act")
        E("scalar", lambda S, h=h: S.activation(run[32:33, :], run[0:1, :], AF.Ln).then_inc(h, 1))
        P.wait("scalar", "act")
        h = P.inc("act")
        E("scalar", lambda S, h=h: S.activation(run[0:1, :], run[32:33, :], AF.Exp, scale=0.5).then_inc(h, 1))
        h = P.inc("act")
        E("scalar", lambda S, h=h: S.activation(wstage[0:1, :], run[32:33, :], AF.Exp, scale=-0.5).then_inc(h, 1))
        P.wait("sync", "act")
        P.wait("tensor", "act")
        h = P.inc("dS", 16)
        E("sync", lambda S, h=h: S.dma_start(out=wq_bounce, in_=run[0:1, :]).then_inc(h, 16))
        P.wait("sync", "dS")
        h = P.inc("dS", 16)
        E("sync", lambda S, h=h: S.dma_start(out=w32[:], in_=wq_bounce.rearrange("o (a p) -> (o a) p", p=128)).then_inc(h, 16))
        P.wait("tensor", "dS")
        pe_wait_bank(3)
        t32 = pe_mm_group([(lambda T: T.transpose(pb[3][:, 0:32], w32[:], c_idf[0:32, 0:32]))])
        P.wait("scalar", "pe", t32)
        h = P.inc("act")
        E("scalar", lambda S, h=h: S.copy(sc32amax[:], pb[3][:, 0:32]).then_inc(h, 1))
        bank_free[3] = ("act", P.cnt["act"])
        P.wait("vector", "act")
        h = P.inc("dve")
        E("vector", lambda V, h=h: V.tensor_scalar(sc32[:], sc32amax[:], r_sx_col[:], 1.0 / 127.0, ALU.mult, ALU.mult).then_inc(h, 1))
        for cq in range(8):
            bk = cq % 2
            pe_wait_bank(bk)
            h = P.inc("pe")
            E("tensor", lambda T, cq=cq, bk=bk, h=h: T.matmul(
                pb[bk][:], c_c127[:], wstage[0:1, cq*512:(cq+1)*512], start=True, stop=True).then_inc(h, 1))
            P.wait("scalar", "pe")
            h = P.inc("act")
            E("scalar", lambda S, cq=cq, bk=bk, h=h: S.copy(swb[:, cq*512:(cq+1)*512], pb[bk][:]).then_inc(h, 1))
            bank_free[bk] = ("act", P.cnt["act"])
        qdone = {}
        for i in range(2 * NBK):
            k, hf = i // 2, i % 2
            qe_, qs_ = (("sync", "dS") if i % 2 == 0 else ("scalar", "dA"))
            if i <= 1:
                P.wait(qe_, "act")
            if (i - 2) in qdone:
                P.wait(qe_, "dve", qdone[i - 2])
            h = P.inc(qs_, 16)
            E(qe_, lambda S, k=k, hf=hf, i=i, h=h: S.dma_start(
                out=wstage[:, (i % 2)*2048:(i % 2)*2048+2048],
                in_=Wv_(W_in, k)[:, hf*2048:(hf+1)*2048]).then_inc(h, 16))
            ld = P.cnt[qs_]
            P.wait("vector", qs_, ld)
            if i == 0:
                P.wait("vector", "act")
            E("vector", lambda V, i=i, hf=hf: V.tensor_tensor(
                run[:, hf*2048:(hf+1)*2048], wstage[:, (i % 2)*2048:(i % 2)*2048+2048],
                swb[:, hf*2048:(hf+1)*2048], ALU.mult))
            h = P.inc("dve")
            E("vector", lambda V, k=k, hf=hf, h=h: V.tensor_scalar(
                arenaA[:, k*DFF + hf*2048: k*DFF + hf*2048 + 2048], run[:, hf*2048:(hf+1)*2048],
                MAGIC, MAGIC, ALU.add, ALU.subtract).then_inc(h, 1))
            qdone[i] = P.cnt["dve"]

        for b in range(NBK):
            h = P.inc("dS", 16)
            E("sync", lambda S, b=b, h=h: S.dma_start(
                out=arenaB[:, b*NT:(b+1)*NT],
                in_=xnT_d.rearrange("(b p) T -> b p T", p=128)[b]).then_inc(h, 16))
        P.wait("vector", "dS")
        for b in range(NBK):
            E("vector", lambda V, b=b: V.tensor_scalar(
                run[:], arenaB[:, b*NT:(b+1)*NT], s_x_col[:], MAGIC, ALU.mult, ALU.add))
            h = P.inc("dve")
            E("vector", lambda V, b=b, h=h: V.tensor_single_scalar(
                arenaB[:, b*NT:(b+1)*NT], run[:], MAGIC, ALU.subtract).then_inc(h, 1))
        xq_ready = P.cnt["dve"]
        P.wait("tensor", "dve", xq_ready)
        P.wait("tensor", "act")
        E("vector", lambda V: V.memset(hqc[:], 0.0))
        evac_done_at = {}
        dve_red_at = {}
        spill_at = {}
        for quarter in range(4):
            tok0 = quarter * 1024
            for m in range(NBF):
                g = m % 2
                gm = quarter * NBF + m
                if gm >= 2:
                    P.wait("tensor", "act", evac_done_at[gm - 2])
                mml = []
                for k in range(NBK):
                    for t in range(2):
                        mml.append(lambda T, k=k, t=t, m=m, g=g, tok0=tok0: T.matmul(
                            pb[2*g + t][:],
                            arenaA[:, k*DFF + m*128: k*DFF + m*128 + 128],
                            arenaB[:, k*NT + tok0 + t*512: k*NT + tok0 + t*512 + 512],
                            start=(k == 0), stop=(k == NBK - 1)))
                mmd = pe_mm_group(mml)
                P.wait("scalar", "pe", mmd)
                if gm >= 2:
                    P.wait("scalar", "dve", dve_red_at[gm - 2])
                    P.wait("scalar", "dS", spill_at[gm - 2])
                for t in range(2):
                    h = P.inc("act")
                    E("scalar", lambda S, t=t, m=m, g=g, h=h: S.activation(
                        hstage[g][:, t*512:(t+1)*512], pb[2*g + t][:], AF.Gelu,
                        bias=s_bic[:, m:m+1], scale=sc32[:, m:m+1]).then_inc(h, 1))
                evac_done_at[gm] = P.cnt["act"]
                P.wait("vector", "act", evac_done_at[gm])
                h = P.inc("dve")
                E("vector", lambda V, g=g, m=m, h=h: V.tensor_reduce(
                    run[:, m:m+1], hstage[g][:], AX.X, ALU.max,
                    apply_absolute_value=True).then_inc(h, 1))
                P.wait("vector", "dve")
                h = P.inc("dve")
                E("vector", lambda V, m=m, h=h: V.scalar_tensor_tensor(
                    hqc[:, m:m+1], run[:, m:m+1], 1.0, hqc[:, m:m+1], ALU.mult, ALU.max).then_inc(h, 1))
                dve_red_at[gm] = P.cnt["dve"]
                P.wait("sync", "act", evac_done_at[gm])
                h = P.inc("dS", 16)
                E("sync", lambda S, m=m, g=g, tok0=tok0, h=h: S.dma_start(
                    out=hidT.rearrange("(b p) n -> b p n", p=128)[m][:, tok0:tok0+1024],
                    in_=hstage[g][:]).then_inc(h, 16))
                spill_at[gm] = P.cnt["dS"]

        P.wait("vector", "dve")
        h = P.inc("dve")
        E("vector", lambda V, h=h: V.tensor_reduce(xcol[:], hqc[:], AX.X, ALU.max).then_inc(h, 1))
        P.wait("sync", "dve")
        h = P.inc("dS", 16)
        E("sync", lambda S, h=h: S.dma_start(out=fold_bounce.rearrange("o p -> p o"), in_=xcol[:]).then_inc(h, 16))
        P.wait("sync", "dS")
        h = P.inc("dS", 16)
        E("sync", lambda S, h=h: S.dma_start(out=xrow, in_=fold_bounce).then_inc(h, 16))
        P.wait("vector", "dS")
        h = P.inc("dve")
        E("vector", lambda V, h=h: V.memset(ccs, 0.0).then_inc(h, 1))
        P.wait("vector", "dve")
        h = P.inc("dve")
        E("vector", lambda V, h=h: V.tensor_reduce(ccs[0:1, 0:1], xrow, AX.X, ALU.max).then_inc(h, 1))
        P.wait("gpsimd", "dve")
        h = P.inc("dG", 16)
        E("gpsimd", lambda G, h=h: G.dma_start(out=cc2_in.ap(), in_=ccs).then_inc(h, 16))
        P.wait("gpsimd", "dG")
        h = P.inc("cc")
        E("gpsimd", lambda G, h=h: G.collective_compute(
            "AllReduce", ALU.max, replica_groups=[list(range(8))],
            ins=[cc2_in.ap().opt()], outs=[cc2_out.ap().opt()]).then_inc(h, 1))
        P.wait("gpsimd", "cc")
        h = P.inc("dG", 16)
        E("gpsimd", lambda G, h=h: G.dma_start(out=gs2, in_=cc2_out.ap()).then_inc(h, 16))
        P.wait("vector", "dG")
        h = P.inc("dve")
        E("vector", lambda V, h=h: V.tensor_scalar_mul(sc_small[0:1, 1:2], gs2[0:1, 0:1], 1.0 / 32767.0).then_inc(h, 1))
        P.wait("tensor", "dve")
        pe_wait_bank(4)
        rshmm = pe_mm_group([
            (lambda T: T.matmul(pb[4][:, 0:1], c_ones1[:], sc_small[0:1, 1:2], start=True, stop=True))])
        P.wait("scalar", "pe", rshmm)
        h = P.inc("act")
        E("scalar", lambda S, h=h: S.copy(r_sh_col[:], pb[4][:, 0:1]).then_inc(h, 1))
        bank_free[4] = ("act", P.cnt["act"])
        P.wait("vector", "act")
        h = P.inc("dve")
        E("vector", lambda V, h=h: V.reciprocal(s_h_col[:], r_sh_col[:]).then_inc(h, 1))
        P.wait("vector", "dve")

        P.wait("vector", "pe")
        qdone = {}
        for k in range(NBF):
            qe_, qs_ = (("sync", "dS") if k % 2 == 0 else ("scalar", "dA"))
            if (k - 2) in qdone:
                P.wait(qe_, "dve", qdone[k - 2])
            h = P.inc(qs_, 16)
            E(qe_, lambda S, k=k, h=h: S.dma_start(
                out=wstage[:, (k % 2)*1024:(k % 2)*1024+1024], in_=Wv_(W_out, k)).then_inc(h, 16))
            ld = P.cnt[qs_]
            P.wait("vector", qs_, ld)
            h = P.inc("dve")
            E("vector", lambda V, k=k, h=h: V.tensor_reduce(
                amax2c[:, k:k+1], wstage[:, (k % 2)*1024:(k % 2)*1024+1024],
                AX.X, ALU.max, apply_absolute_value=True).then_inc(h, 1))
            P.wait("vector", "dve")
            h = P.inc("dve")
            E("vector", lambda V, k=k, h=h: V.tensor_scalar_max(amax2c[:, k:k+1], amax2c[:, k:k+1], 1e-8).then_inc(h, 1))
            P.wait("vector", "dve")
            h = P.inc("dve")
            E("vector", lambda V, k=k, h=h: V.reciprocal(rcol[:], amax2c[:, k:k+1]).then_inc(h, 1))
            P.wait("vector", "dve")
            E("vector", lambda V, k=k: V.tensor_scalar(
                run[:, (k % 2)*1024:(k % 2)*1024+1024], wstage[:, (k % 2)*1024:(k % 2)*1024+1024],
                rcol[:], 127.0, ALU.mult, ALU.mult))
            h = P.inc("dve")
            E("vector", lambda V, k=k, h=h: V.tensor_scalar(
                arenaA[:, k*1024:(k+1)*1024], run[:, (k % 2)*1024:(k % 2)*1024+1024],
                MAGIC, MAGIC, ALU.add, ALU.subtract).then_inc(h, 1))
            qdone[k] = P.cnt["dve"]
        h = P.inc("dve")
        E("vector", lambda V, h=h: V.tensor_scalar(sc2[:], amax2c[:], r_sh_col[:], 1.0 / 127.0, ALU.mult, ALU.mult).then_inc(h, 1))
        P.wait("vector", "dve")
        P.wait("sync", "dve")
        h = P.inc("dS", 16)
        E("sync", lambda S, h=h: S.dma_start(out=brow[0:1, :], in_=b_out_d).then_inc(h, 16))
        P.wait("tensor", "dS")
        P.wait("tensor", "dve")
        pe_wait_bank(4); pe_wait_bank(5)
        bmm = pe_mm_group([
            (lambda T, n=n: T.matmul(pb[4 + n][:], c_ones1[:], brow[0:1, n*512:(n+1)*512],
                                     start=True, stop=True)) for n in range(2)])
        P.wait("scalar", "pe", bmm)
        for n in range(2):
            h = P.inc("act")
            E("scalar", lambda S, n=n, h=h: S.copy(boutb[:, n*512:(n+1)*512], pb[4 + n][:]).then_inc(h, 1))
        bank_free[4] = ("act", P.cnt["act"] - 1)
        bank_free[5] = ("act", P.cnt["act"])
        P.wait("vector", "act")

        ld_at = {}
        ev_at = {}
        pe_after_sub = 0
        hdbuf4 = [BA[:, 65536:69632], BA[:, 69632:73728]]
        p3scr2 = FA[:, 1024:3072]
        q_done = {}
        l_done = {}
        p2sp = P.cnt["dS"]
        P.wait("scalar", "dS", p2sp)
        for sub in range(4):
            tok0 = sub * 1024
            if sub > 0:
                P.wait("vector", "pe", pe_after_sub)
            for kb0 in range(8):
                kb = kb0
                gi = sub * 8 + kb
                qe_, qs_ = (("sync", "dS") if gi % 2 == 0 else ("scalar", "dA"))
                if (gi - 2) in q_done:
                    P.wait(qe_, "dve", q_done[gi - 2])
                h = P.inc(qs_, 16)
                E(qe_, lambda S, kb=kb, tok0=tok0, h=h: S.dma_start(
                    out=hdbuf4[kb % 2].rearrange("p (b n) -> p b n", n=1024),
                    in_=hidT.rearrange("(a b p) n -> a p b n", b=4, p=128)[kb][:, :, tok0:tok0+1024]).then_inc(h, 16))
                l_done[gi] = P.cnt[qs_]
                P.wait("vector", qs_, l_done[gi])
                for half in range(2):
                    E("vector", lambda V, kb=kb, half=half: V.tensor_scalar(
                        p3scr2[:], hdbuf4[kb % 2][:, half*2048:(half+1)*2048],
                        s_h_col[:], MAGIC, ALU.mult, ALU.add))
                    for j in range(2):
                        k0 = kb * 4 + half * 2 + j
                        h = P.inc("dve")
                        E("vector", lambda V, k0=k0, j=j, h=h: V.tensor_scalar(
                            arenaB[:, k0*1024:(k0+1)*1024], p3scr2[:, j*1024:(j+1)*1024],
                            MAGIC, sc2[:, k0:k0+1], ALU.subtract, ALU.mult).then_inc(h, 1))
                q_done[gi] = P.cnt["dve"]
            hs_ready = P.cnt["dve"]
            P.wait("tensor", "dve", hs_ready)
            P.wait("tensor", "act")
            for mt in range(8):
                g = mt % 2
                gm = sub * 8 + mt
                cglob = sub * 8 + mt
                if gm >= 2:
                    P.wait("sync", "dve", ev_at[gm - 2])
                h = P.inc("dS", 16)
                E("sync", lambda S, cglob=cglob, g=g, h=h: S.dma_start(
                    out=accbuf[g][:],
                    in_=acc_d.rearrange("(c p) d -> c p d", p=128)[cglob]).then_inc(h, 16))
                accl = P.cnt["dS"]
                if gm >= 2:
                    P.wait("tensor", "dve", ev_at[gm - 2])
                mml = []
                for k in range(NBF):
                    for n in range(2):
                        mml.append(lambda T, k=k, n=n, mt=mt, g=g: T.matmul(
                            pb[2*g + n][:],
                            arenaB[:, k*1024 + mt*128: k*1024 + mt*128 + 128],
                            arenaA[:, k*1024 + n*512: k*1024 + n*512 + 512],
                            start=(k == 0), stop=(k == NBF - 1)))
                mmd = pe_mm_group(mml)
                P.wait("vector", "pe", mmd)
                P.wait("vector", "dS", accl)
                if gm >= 2:
                    P.wait("vector", "dS", ld_at[gm - 2])
                for n in range(2):
                    E("vector", lambda V, n=n, g=g: V.scalar_tensor_tensor(
                        ostage[g][:, n*512:(n+1)*512], pb[2*g + n][:], 1.0,
                        accbuf[g][:, n*512:(n+1)*512], ALU.mult, ALU.add))
                h = P.inc("dve")
                E("vector", lambda V, g=g, h=h: V.tensor_tensor(
                    ostage[g][:], ostage[g][:], boutb[:], ALU.add).then_inc(h, 1))
                ev_at[gm] = P.cnt["dve"]
                P.wait("sync", "dve", ev_at[gm])
                h = P.inc("dS", 16)
                E("sync", lambda S, cglob=cglob, g=g, h=h: S.dma_start(
                    out=out_d.rearrange("(c p) d -> c p d", p=128)[cglob],
                    in_=ostage[g][:]).then_inc(h, 16))
                ld_at[gm] = P.cnt["dS"]
            pe_after_sub = P.cnt["pe"]

        P.wait("sync", "dS")
        P.wait("sync", "dG")
        P.wait("sync", "dA")
        P.wait("gpsimd", "dG")
        P.wait("gpsimd", "dS")
        P.wait("scalar", "dA")

        with nc.Block() as block:
            P.run(block)
    return nc


def _make_consts():
    j = np.arange(128)
    return dict(
        ident_bf=np.eye(128).astype(ml_dtypes.bfloat16),
        ident_f32=np.eye(128).astype(np.float32),
        triu16=(np.where(j[:, None] <= j[None, :], 1.0, 0.0) / 16.0).astype(np.float32),
        slo16=(np.where(j[:, None] > j[None, :], -1.0, 0.0) / 16.0).astype(np.float32),
        onescol16=np.full((128, 1), 1.0 / 16.0, dtype=np.float32),
        ones_1x128=np.ones((1, 128), dtype=np.float32),
        c127_1x128=np.full((1, 128), 127.0, dtype=np.float32),
        mask4=np.tile(np.where(j[:, None] <= j[None, :], 1.0, 0.0).astype(np.float32), (1, 4)),
        c_cols=np.tile(np.array([[1.0, 1e-5, 1e-6, LN8, 0, 0, 0, 0]], np.float32), (128, 1)),
    )


_CACHED = {}


def kernel(L0_attn_x, L0_attn_Wq, L0_attn_Wk, L0_attn_Wv, L0_attn_Wg, L0_attn_Wgt,
           L0_attn_Wo, L0_ffn_W_in, L0_ffn_b_in, L0_ffn_W_out, L0_ffn_b_out):
    C = _make_consts()
    if "nc" not in _CACHED:
        _CACHED["nc"] = build(bass.Bass("TRN2", target_bir_lowering=False))
    nc = _CACHED["nc"]
    in_maps = []
    for c in range(8):
        m = {
            "x": np.ascontiguousarray(np.asarray(L0_attn_x)[c]),
            "Wq": np.asarray(L0_attn_Wq), "Wk": np.asarray(L0_attn_Wk),
            "Wv": np.asarray(L0_attn_Wv), "Wg": np.asarray(L0_attn_Wg),
            "Wgt": np.asarray(L0_attn_Wgt), "Wo": np.asarray(L0_attn_Wo),
            "W_in": np.asarray(L0_ffn_W_in),
            "b_in_cols": np.ascontiguousarray(np.asarray(L0_ffn_b_in).reshape(NBF, 128).T),
            "W_out": np.asarray(L0_ffn_W_out),
            "b_out": np.ascontiguousarray(np.asarray(L0_ffn_b_out).reshape(1, D)),
        }
        m.update(C)
        in_maps.append(m)
    res = run_bass_kernel_spmd(nc, in_maps, core_ids=list(range(8)))
    return np.stack([np.asarray(res.results[c]["out"]) for c in range(8)], axis=0)


# version 39
# speedup vs baseline: 1.0128x; 1.0025x over previous
import contextlib
import numpy as np
import ml_dtypes
import concourse.bass as bass
import concourse.mybir as mybir
from concourse.bass_utils import run_bass_kernel_spmd

F32 = mybir.dt.float32
BF16 = mybir.dt.bfloat16
AF = mybir.ActivationFunctionType
ALU = mybir.AluOpType
AX = mybir.AxisListType
MAGIC = 12582912.0

NT = 4096
D = 1024
DFF = 4096
NH = 16
DH = 64
CH = 128
NCH = NT // CH
NBK = D // 128
NBF = DFF // 128
LN8 = float(np.log(1.0 / 8.0))

ENGINES = ("sync", "gpsimd", "vector", "scalar", "tensor")


class Prog:
    def __init__(self, nc, stack, sem_names):
        self.nc = nc
        self.ops = {e: [] for e in ENGINES}
        self.sem = {n: stack.enter_context(nc.semaphore(n)) for n in sem_names}
        self.cnt = {n: 0 for n in sem_names}

    def emit(self, eng, fn):
        self.ops[eng].append(fn)

    def inc(self, name, k=1):
        self.cnt[name] += k
        return self.sem[name]

    def wait(self, eng, name, val=None):
        v = self.cnt[name] if val is None else val
        hh = self.sem[name]
        self.emit(eng, lambda E, hh=hh, v=v: E.wait_ge(hh, v))

    def run(self, block):
        for eng in ENGINES:
            fns = self.ops[eng]
            if fns:
                getattr(block, eng)(lambda E, fns=fns: [f(E) for f in fns])


def build(nc):
    x_d = nc.dram_tensor("x", [NT, D], F32, kind="ExternalInput").ap()
    wq_d = nc.dram_tensor("Wq", [D, D], F32, kind="ExternalInput").ap()
    wk_d = nc.dram_tensor("Wk", [D, D], F32, kind="ExternalInput").ap()
    wv_d = nc.dram_tensor("Wv", [D, D], F32, kind="ExternalInput").ap()
    wg_d = nc.dram_tensor("Wg", [D, D], F32, kind="ExternalInput").ap()
    wgt_d = nc.dram_tensor("Wgt", [D, NH], F32, kind="ExternalInput").ap()
    wo_d = nc.dram_tensor("Wo", [D, D], F32, kind="ExternalInput").ap()
    W_in = nc.dram_tensor("W_in", [D, DFF], F32, kind="ExternalInput").ap()
    b_in_cols = nc.dram_tensor("b_in_cols", [128, NBF], F32, kind="ExternalInput").ap()
    W_out = nc.dram_tensor("W_out", [DFF, D], F32, kind="ExternalInput").ap()
    b_out_d = nc.dram_tensor("b_out", [1, D], F32, kind="ExternalInput").ap()
    cid_d = nc.dram_tensor("ident_bf", [128, 128], BF16, kind="ExternalInput").ap()
    cidf_d = nc.dram_tensor("ident_f32", [128, 128], F32, kind="ExternalInput").ap()
    ctriu_d = nc.dram_tensor("triu16", [128, 128], F32, kind="ExternalInput").ap()
    cslo_d = nc.dram_tensor("slo16", [128, 128], F32, kind="ExternalInput").ap()
    cones16_d = nc.dram_tensor("onescol16", [128, 1], F32, kind="ExternalInput").ap()
    cones1_d = nc.dram_tensor("ones_1x128", [1, 128], F32, kind="ExternalInput").ap()
    c127_d = nc.dram_tensor("c127_1x128", [1, 128], F32, kind="ExternalInput").ap()
    cmask4_d = nc.dram_tensor("mask4", [128, 512], F32, kind="ExternalInput").ap()
    ccols_d = nc.dram_tensor("c_cols", [128, 8], F32, kind="ExternalInput").ap()
    out_d = nc.dram_tensor("out", [NT, D], F32, kind="ExternalOutput").ap()

    xnT_d = nc.dram_tensor("xnT_spill", [D, NT], BF16, kind="Internal").ap()
    acc_d = nc.dram_tensor("acc_spill", [NT, D], F32, kind="Internal").ap()
    hidT = nc.dram_tensor("hid_spill", [DFF, NT], BF16, kind="Internal").ap()
    wq_bounce = nc.dram_tensor("wq_bounce", [1, DFF], F32, kind="Internal").ap()
    fold_bounce = nc.dram_tensor("fold_bounce", [1, 128], F32, kind="Internal").ap()
    cc1_in = nc.dram_tensor("cc1_in", [1, 128], F32, kind="Internal")
    cc1_out = nc.dram_tensor("cc1_out", [1, 128], F32, kind="Internal")
    cc2_in = nc.dram_tensor("cc2_in", [1, 128], F32, kind="Internal")
    cc2_out = nc.dram_tensor("cc2_out", [1, 128], F32, kind="Internal")

    stack = contextlib.ExitStack()
    with stack:
        def sb(name, shape, dtype=F32):
            return stack.enter_context(nc.sbuf_tensor(name, shape, dtype))

        BA = sb("BA", [128, 75904], BF16)
        FA = sb("FA", [128, 12288], F32)
        WQ = BA[:, 0:8192]; WK = BA[:, 8192:16384]; WV = BA[:, 16384:24576]
        WG = BA[:, 24576:32768]; WO = BA[:, 32768:40960]; WGT = BA[:, 40960:41088]
        xn_bf = [BA[:, 41088:42112], BA[:, 42112:43136]]
        xnT_s = [BA[:, 43136:44160], BA[:, 44160:45184]]
        qe_bf = [BA[:, 45184:46208], BA[:, 46208:47232]]
        ke_bf = [BA[:, 47232:48256], BA[:, 48256:49280]]
        v_bf = [BA[:, 49280:50304], BA[:, 50304:51328]]
        qT_s = [BA[:, 51328:53376], BA[:, 53376:55424]]
        kT_s = [BA[:, 55424:57472], BA[:, 57472:59520]]
        AT_bf = [BA[:, 59520:61568], BA[:, 61568:63616]]
        orms = BA[:, 63616:67712]
        gmac = BA[:, 67712:71808]
        gtmp = BA[:, 71808:72832]
        ogb = BA[:, 72832:73856]
        ogT_s = BA[:, 73856:74880]
        Sdec_bf = BA[:, 74880:75904]
        arenaA = BA[:, 0:32768]
        arenaB = BA[:, 32768:65536]
        hstage = [BA[:, 65536:66560], BA[:, 66560:67584]]
        hdbuf = [BA[:, 67584:68608], BA[:, 68608:69632]]
        xbuf = [FA[:, i*1024:(i+1)*1024] for i in range(4)]
        sqd = FA[:, 4096:5120]
        osq = FA[:, 5120:6144]
        accst = [FA[:, 6144:7168], FA[:, 7168:8192]]
        S_f = FA[0:64, 8192:9216]
        Sdec = FA[0:64, 9216:10240]
        wld = FA[:, 10240:11264]
        lnsq = FA[:, 10240:11264]
        run = FA[:, 0:4096]
        swb = FA[:, 4096:8192]
        wstage = FA[:, 8192:12288]
        ostage = [FA[:, 4096:5120], FA[:, 5120:6144]]
        accbuf = [FA[:, 6144:7168], FA[:, 7168:8192]]
        p3scr = FA[:, 0:1024]
        boutb = FA[:, 8192:9216]
        brow = FA[:, 9216:10240]

        c_id = sb("c_id", [128, 128], BF16)
        c_idf = sb("c_idf", [128, 128], F32)
        c_triu = sb("c_triu", [128, 128], F32)
        c_slo = sb("c_slo", [128, 128], F32)
        c_ones16 = sb("c_ones16", [128, 1], F32)
        c_ones1 = sb("c_ones1", [1, 128], F32)
        c_c127 = sb("c_c127", [1, 128], F32)
        c_mask4 = sb("c_mask4", [128, 512], F32)
        c_cols = sb("s_ccols", [128, 8], F32)
        scal_t = sb("scal_t", [128, 32], F32)
        g16 = [sb(f"g16_{i}", [128, 160], F32) for i in range(2)]
        rst16 = [sb(f"rst16_{i}", [128, 32], F32) for i in range(2)]
        expBc = [sb(f"expBc{i}", [128, 16], F32) for i in range(2)]
        expBrow = [sb(f"expBrow{i}", [1, 16], F32) for i in range(2)]
        xamaxc = sb("xamaxc", [128, 32], F32)
        hqc = sb("hqc", [128, 32], F32)
        amax2c = sb("amax2c", [128, 32], F32)
        sc32amax = sb("sc32amax", [128, 32], F32)
        sc32 = sb("sc32", [128, 32], F32)
        sc2 = sb("sc2", [128, 32], F32)
        s_bic = sb("s_bic", [128, 32], F32)
        rcol = sb("rcol", [128, 1], F32)
        xcol = sb("xcol", [128, 1], F32)
        rows4 = sb("rows4", [1, 512], F32)
        xrow = rows4[0:1, 0:128]; ccs = rows4[0:1, 128:256]
        gs1 = rows4[0:1, 256:384]; gs2 = rows4[0:1, 384:512]
        sc_small = sb("sc_small", [1, 8], F32)
        r_sx_col = sb("r_sx_col", [128, 1], F32)
        s_x_col = sb("s_x_col", [128, 1], F32)
        r_sh_col = sb("r_sh_col", [128, 1], F32)
        s_h_col = sb("s_h_col", [128, 1], F32)
        w32 = sb("w32", [32, 128], F32)
        bnst = sb("bnst", [128, 96], F32)

        def psum(name, shape, dtype=F32):
            return stack.enter_context(nc.psum_tensor(name, shape, dtype))
        pb = [psum(f"pb{i}", [128, 512], F32) for i in range(7)]
        pT = psum("pT", [128, 1024], BF16)

        P = Prog(nc, stack, ["dS", "dG", "dA", "pe", "act", "dve", "cc"])
        E = P.emit

        def Wv_(w, k):
            return w.rearrange("(b p) n -> b p n", p=128)[k]

        def pe_mm_group(mms):
            for i, mm in enumerate(mms):
                if i == len(mms) - 1:
                    h = P.inc("pe")
                    E("tensor", lambda T, mm=mm, h=h: mm(T).then_inc(h, 1))
                else:
                    E("tensor", lambda T, mm=mm: mm(T))
            return P.cnt["pe"]

        bank_free = {i: ("dve", 0) for i in range(8)}

        def pe_wait_bank(i):
            sname, cnt = bank_free[i]
            if cnt > 0:
                P.wait("tensor", sname, cnt)

        for dst, src in [(c_id, cid_d), (c_idf, cidf_d), (c_triu, ctriu_d),
                         (c_slo, cslo_d), (c_ones16, cones16_d), (c_ones1, cones1_d),
                         (c_c127, c127_d), (c_mask4, cmask4_d), (c_cols, ccols_d),
                         (s_bic, b_in_cols)]:
            h = P.inc("dS", 16)
            E("sync", lambda S, dst=dst, src=src, h=h: S.dma_start(out=dst[:], in_=src).then_inc(h, 16))
        E("vector", lambda V: V.memset(xamaxc[:], 0.0))
        E("vector", lambda V: V.memset(S_f[:], 0.0))
        h = P.inc("dve")
        E("vector", lambda V: V.memset(Sdec_bf[:], 0.0))
        E("vector", lambda V, h=h: V.memset(BA[64:128, 51328:59520], 0.0).then_inc(h, 1))
        wld2 = FA[:, 11264:12288]
        wstg = [wld, wld2]
        wjobs = [(WGT, wgt_d, NH, k) for k in range(NBK)]
        wjobs += [(W, wd, D, k) for (W, wd) in
                  [(WQ, wq_d), (WK, wk_d), (WV, wv_d), (WG, wg_d), (WO, wo_d)]
                  for k in range(NBK)]
        w_cast_at = {}
        w_milestones = {}
        for j, (W_s, w_dd, ncols, k) in enumerate(wjobs):
            qe_, qs_ = (("sync", "dS") if j % 2 == 0 else ("scalar", "dA"))
            ce_, cs_ = (("vector", "dve") if j % 2 == 0 else ("scalar", "act"))
            if (j - 2) in w_cast_at:
                P.wait(qe_, w_cast_at[j - 2][0], w_cast_at[j - 2][1])
            h = P.inc(qs_, 16)
            E(qe_, lambda S, w_dd=w_dd, k=k, ncols=ncols, j=j, h=h: S.dma_start(
                out=wstg[j % 2][:, 0:ncols], in_=Wv_(w_dd, k)).then_inc(h, 16))
            P.wait(ce_, qs_, P.cnt[qs_])
            h = P.inc(cs_)
            if ce_ == "vector":
                E(ce_, lambda V, W_s=W_s, k=k, ncols=ncols, j=j, h=h: V.tensor_copy(
                    W_s[:, k*ncols:(k+1)*ncols], wstg[j % 2][:, 0:ncols]).then_inc(h, 1))
            else:
                E(ce_, lambda S, W_s=W_s, k=k, ncols=ncols, j=j, h=h: S.copy(
                    W_s[:, k*ncols:(k+1)*ncols], wstg[j % 2][:, 0:ncols]).then_inc(h, 1))
            w_cast_at[j] = (cs_, P.cnt[cs_])

        acc_evac_at = {}
        scal_free_at = {}
        o_done_at = {}
        xnT_spill_at = {}
        acc_dma_at = {}
        ebc_done_at = {}
        chunk_end = {}
        xn_ready_at = {}
        trsp_done_at = {}
        rsq_act_at = {}
        ln_rsq_at = {}
        tail_og3_at = None

        x_loaded_at = {}

        def emit_xload(cc):
            xb = xbuf[cc % 4]
            if cc >= 4 and (cc - 4) in acc_evac_at:
                P.wait("sync", "dve", acc_evac_at[cc - 4])
            h = P.inc("dS", 16)
            E("sync", lambda S, cc=cc, xb=xb, h=h: S.dma_start(
                out=xb, in_=x_d.rearrange("(c p) d -> c p d", p=128)[cc]).then_inc(h, 16))
            x_loaded_at[cc] = P.cnt["dS"]

        def emit_ln(cc):
            par = cc % 2
            xb = xbuf[cc % 4]
            bb = 48 + par * 16
            x_loaded = x_loaded_at[cc]
            P.wait("vector", "dS", x_loaded)
            if (cc - 2) in ln_rsq_at:
                P.wait("vector", "act", ln_rsq_at[cc - 2])
            for half in range(2):
                h = P.inc("dve")
                E("vector", lambda V, xb=xb, half=half, bb=bb, h=h: V.bn_stats(
                    bnst[:, bb + half*6: bb + half*6 + 6],
                    xb[:, half*512:(half+1)*512]).then_inc(h, 1))
            P.wait("vector", "dve")
            h = P.inc("dve")
            E("vector", lambda V, bb=bb, h=h: V.bn_aggr(
                bnst[:, bb+12:bb+14], bnst[:, bb:bb+12]).then_inc(h, 1))
            P.wait("scalar", "dve")
            h = P.inc("act")
            E("scalar", lambda S, bb=bb, h=h: S.activation(
                bnst[:, bb+15:bb+16], bnst[:, bb+13:bb+14],
                AF.Ln, bias=c_cols[:, 1:2]).then_inc(h, 1))
            P.wait("scalar", "act")
            h = P.inc("act")
            E("scalar", lambda S, bb=bb, h=h: S.activation(
                bnst[:, bb+14:bb+15], bnst[:, bb+15:bb+16],
                AF.Exp, scale=-0.5).then_inc(h, 1))
            ln_rsq_at[cc] = P.cnt["act"]
            P.wait("vector", "act")
            if (cc - 2) in trsp_done_at:
                P.wait("vector", "pe", trsp_done_at[cc - 2])
            h = P.inc("dve")
            E("vector", lambda V, xb=xb, par=par, bb=bb, h=h: V.tensor_scalar(
                xn_bf[par][:], xb, bnst[:, bb+12:bb+13], bnst[:, bb+14:bb+15],
                ALU.subtract, ALU.mult).then_inc(h, 1))
            xn_ready_at[cc] = P.cnt["dve"]
            h = P.inc("dve")
            E("vector", lambda V, par=par, cc=cc, h=h: V.tensor_reduce(
                xamaxc[:, cc:cc+1], xn_bf[par][:], AX.X, ALU.max, apply_absolute_value=True).then_inc(h, 1))

        xnT_evac_at = {}

        def emit_xnT(cc):
            par = cc % 2
            pe_wait_bank(7)
            P.wait("tensor", "dve", xn_ready_at[cc])
            pe_mm_group([
                (lambda T, f=f, par=par: T.transpose(pT[:, f*128:(f+1)*128],
                                                     xn_bf[par][:, f*128:(f+1)*128], c_id[:]))
                for f in range(NBK)])
            trsp_done_at[cc] = P.cnt["pe"]
            P.wait("scalar", "pe")
            if (cc - 2) in xnT_spill_at:
                P.wait("scalar", "dS", xnT_spill_at[cc - 2])
            h = P.inc("act")
            E("scalar", lambda S, par=par, h=h: S.copy(xnT_s[par][:], pT[:]).then_inc(h, 1))
            xnT_evac_at[cc] = P.cnt["act"]
            bank_free[7] = ("act", xnT_evac_at[cc])
            P.wait("sync", "act", xnT_evac_at[cc])
            h = P.inc("dS", 16)
            E("sync", lambda S, cc=cc, par=par, h=h: S.dma_start(
                out=xnT_d.rearrange("(b p) T -> p b T", p=128)[:, :, cc*128:(cc+1)*128],
                in_=xnT_s[par].rearrange("p (b t) -> p b t", t=128)).then_inc(h, 16))
            xnT_spill_at[cc] = P.cnt["dS"]

        emit_xload(0)
        emit_ln(0)
        emit_xnT(0)

        for c in range(NCH):
            par = c % 2
            xb = xbuf[c % 4]
            mc = c % 4
            if (c - 2) in chunk_end:
                pe2 = chunk_end[c - 2]
                for eng in ("tensor", "vector", "scalar"):
                    for semn in ("pe", "act", "dve"):
                        P.wait(eng, semn, pe2[semn])
            if mc < 3 and c + 1 < NCH:
                emit_xload(c + 1)

            pe_wait_bank(4)
            P.wait("tensor", "act", xnT_evac_at[c])
            z_done = pe_mm_group([
                (lambda T, k=k, par=par: T.matmul(pb[4][:, 0:16], xnT_s[par][:, k*128:(k+1)*128],
                                                  WGT[:, k*16:(k+1)*16],
                                                  start=(k == 0), stop=(k == NBK - 1)))
                for k in range(NBK)])

            def proj_mms(W_s, banks):
                return pe_mm_group([
                    (lambda T, k=k, nh_=nh_, W_s=W_s, par=par, b=banks[nh_]: T.matmul(
                        pb[b][:], xnT_s[par][:, k*128:(k+1)*128],
                        W_s[:, k*D + nh_*512: k*D + nh_*512 + 512],
                        start=(k == 0), stop=(k == NBK - 1)))
                    for k in range(NBK) for nh_ in range(2)])

            gg = g16[par]
            P.wait("vector", "pe", z_done)
            h = P.inc("dve")
            E("vector", lambda V, gg=gg, h=h: V.tensor_single_scalar(
                gg[:, 0:16], pb[4][:, 0:16], 0.0, ALU.min).then_inc(h, 1))
            P.wait("scalar", "pe", z_done)
            h = P.inc("act")
            E("scalar", lambda S, gg=gg, h=h: S.activation(gg[:, 16:32], pb[4][:, 0:16], AF.Abs).then_inc(h, 1))
            P.wait("scalar", "act")
            h = P.inc("act")
            E("scalar", lambda S, gg=gg, h=h: S.activation(gg[:, 32:48], gg[:, 16:32], AF.Exp, scale=-1.0).then_inc(h, 1))
            P.wait("scalar", "act")
            h = P.inc("act")
            E("scalar", lambda S, gg=gg, h=h: S.activation(gg[:, 48:64], gg[:, 32:48], AF.Ln, bias=c_cols[:, 0:1]).then_inc(h, 1))
            P.wait("vector", "act")
            P.wait("vector", "dve")
            h = P.inc("dve")
            E("vector", lambda V, gg=gg, h=h: V.tensor_tensor(gg[:, 64:80], gg[:, 0:16], gg[:, 48:64], ALU.subtract).then_inc(h, 1))
            gt_ready = P.cnt["dve"]

            projd = {}
            pe_wait_bank(0); pe_wait_bank(1)
            projd[0] = proj_mms(WQ, (0, 1))

            P.wait("tensor", "dve", gt_ready)
            gmm_done = pe_mm_group([
                (lambda T, gg=gg: T.matmul(pb[4][:, 16:32], c_triu[:], gg[:, 64:80], start=True, stop=True)),
                (lambda T, gg=gg: T.matmul(pb[4][:, 32:48], c_slo[:], gg[:, 64:80], start=True, stop=True)),
                (lambda T, gg=gg: T.matmul(pb[4][0:1, 48:64], c_ones16[:], gg[:, 64:80], start=True, stop=True)),
            ])
            P.wait("scalar", "pe", gmm_done)
            if (c - 2) in ebc_done_at:
                P.wait("scalar", "pe", ebc_done_at[c - 2])
            h = P.inc("act")
            E("scalar", lambda S, gg=gg, h=h: S.activation(gg[:, 80:96], pb[4][:, 32:48], AF.Exp, bias=c_cols[:, 3:4]).then_inc(h, 1))
            h = P.inc("act")
            E("scalar", lambda S, gg=gg, h=h: S.activation(gg[:, 96:112], pb[4][:, 32:48], AF.Exp, scale=-1.0).then_inc(h, 1))
            h = P.inc("act")
            E("scalar", lambda S, par=par, h=h: S.activation(expBrow[par][:], pb[4][0:1, 48:64], AF.Exp).then_inc(h, 1))
            exps_done = P.cnt["act"]

            pe_wait_bank(2); pe_wait_bank(3)
            projd[1] = proj_mms(WK, (2, 3))

            P.wait("tensor", "act", exps_done)
            ebc = pe_mm_group([
                (lambda T, par=par: T.matmul(pb[4][0:128, 64:80], c_ones1[0:1, 0:128],
                                             expBrow[par][:], start=True, stop=True))])
            ebc_done_at[c] = ebc
            P.wait("scalar", "pe", ebc)
            h = P.inc("act")
            E("scalar", lambda S, par=par, h=h: S.copy(expBc[par][:], pb[4][0:128, 64:80]).then_inc(h, 1))
            ebc_copy_at = P.cnt["act"]
            bank_free[4] = ("act", ebc_copy_at)

            pe_wait_bank(5); pe_wait_bank(6)
            vd = pe_mm_group([
                (lambda T, k=k, nh_=nh_, par=par, b=(5, 6)[nh_]: T.matmul(
                    pb[b][:], xnT_s[par][:, k*128:(k+1)*128],
                    WV[:, k*D + nh_*512: k*D + nh_*512 + 512],
                    start=(k == 0), stop=(k == NBK - 1)))
                for k in range(NBK) for nh_ in range(2)])

            proj_evac = {}
            rq_slots = {0: 9, 1: 10}
            for pi, banks in [(0, (0, 1)), (1, (2, 3))]:
                bb = pi * 16
                P.wait("vector", "pe", projd[pi])
                if (pi, c - 1) in rsq_act_at:
                    P.wait("vector", "act", rsq_act_at[(pi, c - 1)])
                for half in range(2):
                    h = P.inc("dve")
                    E("vector", lambda V, b=banks[half], half=half, bb=bb, h=h: V.bn_stats(
                        bnst[:, bb + half*6: bb + half*6 + 6], pb[b][:]).then_inc(h, 1))
                P.wait("vector", "dve")
                h = P.inc("dve")
                E("vector", lambda V, bb=bb, h=h: V.bn_aggr(
                    bnst[:, bb+12:bb+14], bnst[:, bb:bb+12]).then_inc(h, 1))
                P.wait("vector", "dve")
                h = P.inc("dve")
                E("vector", lambda V, bb=bb, h=h: V.tensor_scalar(
                    bnst[:, bb+14:bb+15], bnst[:, bb+12:bb+13],
                    bnst[:, bb+12:bb+13], bnst[:, bb+13:bb+14],
                    ALU.mult, ALU.add).then_inc(h, 1))
                P.wait("scalar", "dve")
                h = P.inc("act")
                E("scalar", lambda S, bb=bb, h=h: S.activation(
                    bnst[:, bb+15:bb+16], bnst[:, bb+14:bb+15],
                    AF.Ln, bias=c_cols[:, 2:3]).then_inc(h, 1))
                P.wait("scalar", "act")
                h = P.inc("act")
                E("scalar", lambda S, pi=pi, par=par, bb=bb, h=h: S.activation(
                    scal_t[:, par*16+rq_slots[pi]:par*16+rq_slots[pi]+1],
                    bnst[:, bb+15:bb+16], AF.Exp, scale=-0.5).then_inc(h, 1))
                rstd_done = P.cnt["act"]
                rsq_act_at[(pi, c)] = rstd_done
                P.wait("vector", "act", max(rstd_done, exps_done))
                src_off = 80 if pi == 0 else 96
                dst_off = 112 if pi == 0 else 128
                h = P.inc("dve")
                E("vector", lambda V, gg=gg, src_off=src_off, dst_off=dst_off, pi=pi, par=par, h=h: V.tensor_scalar(
                    gg[:, dst_off:dst_off+16], gg[:, src_off:src_off+16],
                    scal_t[:, par*16+rq_slots[pi]:par*16+rq_slots[pi]+1], None, ALU.mult).then_inc(h, 1))
                P.wait("vector", "dve")
                dst = qe_bf if pi == 0 else ke_bf
                for half in range(2):
                    h = P.inc("dve")
                    def tt(V, dst=dst, b=banks[half], half=half, par=par, gg=gg, dst_off=dst_off, h=h):
                        p3 = pb[b][:].rearrange("q (g e) -> q g e", e=64)
                        s3 = gg[:, dst_off + half*8: dst_off + half*8 + 8].rearrange("q (g e) -> q g e", e=1)
                        a3, s3b = bass.broadcast_tensor_aps(p3, s3)
                        d3 = dst[par][:, half*512:(half+1)*512].rearrange("q (g e) -> q g e", e=64)
                        return V.tensor_tensor(d3, a3, s3b, ALU.mult).then_inc(h, 1)
                    E("vector", tt)
                bank_free[banks[0]] = ("dve", P.cnt["dve"] - 1)
                bank_free[banks[1]] = ("dve", P.cnt["dve"])
                proj_evac[pi] = P.cnt["dve"]
            qk_evac = P.cnt["dve"]

            P.wait("vector", "pe", vd)
            if ("v", c - 1) in rsq_act_at:
                P.wait("vector", "act", rsq_act_at[("v", c - 1)])
            for half in range(2):
                h = P.inc("dve")
                E("vector", lambda V, half=half, b=(5, 6)[half], h=h: V.bn_stats(
                    bnst[:, 32 + half*6: 32 + half*6 + 6], pb[b][:]).then_inc(h, 1))
            P.wait("vector", "dve")
            h = P.inc("dve")
            E("vector", lambda V, h=h: V.bn_aggr(bnst[:, 44:46], bnst[:, 32:44]).then_inc(h, 1))
            P.wait("vector", "dve")
            h = P.inc("dve")
            E("vector", lambda V, h=h: V.tensor_scalar(
                bnst[:, 46:47], bnst[:, 44:45], bnst[:, 44:45], bnst[:, 45:46],
                ALU.mult, ALU.add).then_inc(h, 1))
            P.wait("scalar", "dve")
            h = P.inc("act")
            E("scalar", lambda S, h=h: S.activation(
                bnst[:, 47:48], bnst[:, 46:47],
                AF.Ln, bias=c_cols[:, 2:3]).then_inc(h, 1))
            P.wait("scalar", "act")
            h = P.inc("act")
            E("scalar", lambda S, par=par, h=h: S.activation(
                scal_t[:, par*16+11:par*16+12], bnst[:, 47:48],
                AF.Exp, scale=-0.5).then_inc(h, 1))
            rsq_act_at[("v", c)] = P.cnt["act"]
            P.wait("scalar", "act")
            for half in range(2):
                h = P.inc("act")
                E("scalar", lambda S, half=half, b=(5, 6)[half], par=par, h=h: S.activation(
                    v_bf[par][:, half*512:(half+1)*512], pb[b][:], AF.Copy,
                    scale=scal_t[:, par*16+11:par*16+12]).then_inc(h, 1))
            bank_free[5] = ("act", P.cnt["act"] - 1)
            bank_free[6] = ("act", P.cnt["act"])
            v_evac = P.cnt["act"]
            scal_free_at[c] = ("act", v_evac)


            pe_wait_bank(2); pe_wait_bank(3)
            gd = pe_mm_group([
                (lambda T, k=k, nh_=nh_, par=par, b=(2, 3)[nh_]: T.matmul(
                    pb[b][:], xnT_s[par][:, k*128:(k+1)*128],
                    WG[:, k*D + nh_*512: k*D + nh_*512 + 512],
                    start=(k == 0), stop=(k == NBK - 1)))
                for k in range(NBK) for nh_ in range(2)])
            P.wait("scalar", "pe", gd)
            for half in range(2):
                h = P.inc("act")
                E("scalar", lambda S, half=half, b=(2, 3)[half], mc=mc, h=h: S.copy(
                    gmac[:, mc*1024 + half*512: mc*1024 + (half+1)*512], pb[b][:]).then_inc(h, 1))
            bank_free[2] = ("act", P.cnt["act"] - 1)
            bank_free[3] = ("act", P.cnt["act"])

            for (src, dstT) in [(qe_bf, qT_s), (ke_bf, kT_s)]:
                for half in range(2):
                    pe_wait_bank(7)
                    P.wait("tensor", "dve", proj_evac[0] if src is qe_bf else proj_evac[1])
                    pe_mm_group([
                        (lambda T, hh=hh, half=half, src=src, par=par: T.transpose(
                            pT[0:64, hh*128:(hh+1)*128],
                            src[par][:, (half*8+hh)*64:(half*8+hh)*64+64], c_id[:]))
                        for hh in range(8)])
                    P.wait("scalar", "pe")
                    h = P.inc("act")
                    E("scalar", lambda S, dstT=dstT, half=half, par=par, h=h: S.copy(
                        dstT[par][0:64, half*1024:(half+1)*1024], pT[0:64, :]).then_inc(h, 1))
                    bank_free[7] = ("act", P.cnt["act"])
            kT_evac = P.cnt["act"]

            if (c - 1) in o_done_at:
                P.wait("vector", "pe", o_done_at[c - 1])
            P.wait("vector", "act", ebc_copy_at)
            for hd in range(NH):
                if hd < NH - 1:
                    E("vector", lambda V, hd=hd, par=par: V.tensor_scalar(
                        Sdec[:, hd*64:(hd+1)*64], S_f[:, hd*64:(hd+1)*64],
                        expBc[par][0:64, hd:hd+1], None, ALU.mult))
                else:
                    h = P.inc("dve")
                    E("vector", lambda V, hd=hd, par=par, h=h: V.tensor_scalar(
                        Sdec[:, hd*64:(hd+1)*64], S_f[:, hd*64:(hd+1)*64],
                        expBc[par][0:64, hd:hd+1], None, ALU.mult).then_inc(h, 1))
            P.wait("vector", "dve")
            h = P.inc("dve")
            E("vector", lambda V, h=h: V.tensor_copy(Sdec_bf[0:64, :], Sdec[:]).then_inc(h, 1))
            sdec_done = P.cnt["dve"]

            P.wait("tensor", "act", kT_evac)
            at_last = 0
            for gidx in range(4):
                ab = 5 + (gidx % 2)
                pe_wait_bank(ab)
                agrp = pe_mm_group([
                    (lambda T, hd=gidx*4+hh, hh=hh, ab=ab, par=par: T.matmul(
                        pb[ab][:, hh*128:(hh+1)*128],
                        kT_s[par][:, hd*128:(hd+1)*128], qT_s[par][:, hd*128:(hd+1)*128],
                        start=True, stop=True))
                    for hh in range(4)])
                P.wait("vector", "pe", agrp)
                h = P.inc("dve")
                E("vector", lambda V, gidx=gidx, ab=ab, par=par, h=h: V.tensor_tensor(
                    AT_bf[par][:, gidx*512:(gidx+1)*512], pb[ab][:], c_mask4[:], ALU.mult).then_inc(h, 1))
                bank_free[ab] = ("dve", P.cnt["dve"])
                at_last = P.cnt["dve"]

            for wave in range(2):
                pe_wait_bank(4)
                P.wait("tensor", "dve", qk_evac)
                P.wait("tensor", "act", v_evac)
                kvd = pe_mm_group([
                    (lambda T, hd=wave*8+hh, hh=hh, par=par: T.matmul(
                        pb[4][0:64, hh*64:(hh+1)*64],
                        ke_bf[par][:, hd*64:(hd+1)*64], v_bf[par][:, hd*64:(hd+1)*64],
                        start=True, stop=True))
                    for hh in range(8)])
                P.wait("vector", "pe", kvd)
                P.wait("vector", "dve", sdec_done)
                h = P.inc("dve")
                E("vector", lambda V, wave=wave, h=h: V.tensor_tensor(
                    S_f[:, wave*512:(wave+1)*512], Sdec[:, wave*512:(wave+1)*512],
                    pb[4][0:64, 0:512], ALU.add).then_inc(h, 1))
                bank_free[4] = ("dve", P.cnt["dve"])

            if mc < 3 and c + 1 < NCH:
                emit_ln(c + 1)

            pe_wait_bank(0); pe_wait_bank(1)
            P.wait("tensor", "dve", max(at_last, sdec_done))
            o_half = {}
            for ob in range(2):
                omms = []
                for hd in range(ob*8, ob*8 + 8):
                    off = (hd % 8) * 64
                    omms.append(lambda T, hd=hd, ob=ob, off=off, par=par: T.matmul(
                        pb[ob][:, off:off+64], AT_bf[par][:, hd*128:(hd+1)*128],
                        v_bf[par][:, hd*64:(hd+1)*64], start=True, stop=False))
                    omms.append(lambda T, hd=hd, ob=ob, off=off, par=par: T.matmul(
                        pb[ob][:, off:off+64], qT_s[par][:, hd*128:(hd+1)*128],
                        Sdec_bf[:, hd*64:(hd+1)*64], start=False, stop=True))
                o_half[ob] = pe_mm_group(omms)
            o_done = o_half[1]
            o_done_at[c] = o_done

            if mc == 0 and tail_og3_at is not None:
                P.wait("scalar", "dve", tail_og3_at)
            for half in range(2):
                P.wait("scalar", "pe", o_half[half])
                h = P.inc("act")
                E("scalar", lambda S, half=half, h=h: S.activation(
                    osq[:, half*512:(half+1)*512], pb[half][:], AF.Square).then_inc(h, 1))
            P.wait("vector", "act")
            h = P.inc("dve")
            E("vector", lambda V, par=par, h=h: V.tensor_reduce(
                rst16[par][:, 16:32].rearrange("q (g o) -> q g o", o=1),
                osq[:].rearrange("q (g e) -> q g e", e=64), AX.X, ALU.add).then_inc(h, 1))
            P.wait("scalar", "dve")
            h = P.inc("act")
            E("scalar", lambda S, par=par, h=h: S.activation(
                rst16[par][:, 0:16], rst16[par][:, 16:32], AF.Ln,
                scale=1.0 / DH, bias=c_cols[:, 2:3]).then_inc(h, 1))
            P.wait("scalar", "act")
            h = P.inc("act")
            E("scalar", lambda S, par=par, h=h: S.activation(
                rst16[par][:, 0:16], rst16[par][:, 0:16], AF.Exp, scale=-0.5).then_inc(h, 1))
            rstd_done = P.cnt["act"]
            P.wait("vector", "act", rstd_done)
            for half in range(2):
                h = P.inc("dve")
                def tt2(V, half=half, par=par, mc=mc, h=h):
                    p3 = pb[half][:].rearrange("q (g e) -> q g e", e=64)
                    s3 = rst16[par][:, half*8:half*8+8].rearrange("q (g e) -> q g e", e=1)
                    a3, s3b = bass.broadcast_tensor_aps(p3, s3)
                    d3 = orms[:, mc*1024 + half*512: mc*1024 + (half+1)*512].rearrange("q (g e) -> q g e", e=64)
                    return V.tensor_tensor(d3, a3, s3b, ALU.mult).then_inc(h, 1)
                E("vector", tt2)
            bank_free[0] = ("dve", P.cnt["dve"] - 1)
            bank_free[1] = ("dve", P.cnt["dve"])

            if mc < 3 and c + 1 < NCH:
                emit_xnT(c + 1)

            chunk_end[c] = {k: P.cnt[k] for k in ("pe", "act", "dve")}

            if mc == 3:
                m0 = c - 3
                gbuf = [FA[:, 4096:5120], FA[:, 10240:11264],
                        FA[:, 11264:12288], osq]
                ogbs = [ogb, BA[:, 71808:72832]]
                ogTs = [ogT_s, BA[:, 69632:70656]]
                P.wait("scalar", "dve")
                silu_at = {}
                for cc2 in range(4):
                    h = P.inc("act")
                    E("scalar", lambda S, cc2=cc2, h=h: S.activation(
                        gbuf[cc2][:], gmac[:, cc2*1024:(cc2+1)*1024], AF.Silu).then_inc(h, 1))
                    silu_at[cc2] = P.cnt["act"]
                trsp_tail = {}
                wo_tail = {}
                for cc2 in range(4):
                    cglob = m0 + cc2
                    P.wait("vector", "act", silu_at[cc2])
                    if (cc2 - 2) in trsp_tail:
                        P.wait("vector", "pe", trsp_tail[cc2 - 2])
                    h = P.inc("dve")
                    E("vector", lambda V, cc2=cc2, h=h: V.tensor_tensor(
                        ogbs[cc2 % 2][:], orms[:, cc2*1024:(cc2+1)*1024],
                        gbuf[cc2][:], ALU.mult).then_inc(h, 1))
                    og_done = P.cnt["dve"]
                    if cc2 == 3:
                        tail_og3_at = og_done
                    pe_wait_bank(7)
                    P.wait("tensor", "dve", og_done)
                    pe_mm_group([
                        (lambda T, f=f, cc2=cc2: T.transpose(pT[:, f*128:(f+1)*128],
                                                             ogbs[cc2 % 2][:, f*128:(f+1)*128], c_id[:]))
                        for f in range(NBK)])
                    trsp_tail[cc2] = P.cnt["pe"]
                    P.wait("scalar", "pe", trsp_tail[cc2])
                    h = P.inc("act")
                    E("scalar", lambda S, cc2=cc2, h=h: S.copy(ogTs[cc2 % 2][:], pT[:]).then_inc(h, 1))
                    ogT_evac = P.cnt["act"]
                    bank_free[7] = ("act", ogT_evac)
                    wbk = (2, 3) if cc2 % 2 == 0 else (5, 6)
                    pe_wait_bank(wbk[0]); pe_wait_bank(wbk[1])
                    P.wait("tensor", "act", ogT_evac)
                    wo_done = pe_mm_group([
                        (lambda T, k=k, nh_=nh_, cc2=cc2, wbk=wbk: T.matmul(
                            pb[wbk[nh_]][:], ogTs[cc2 % 2][:, k*128:(k+1)*128],
                            WO[:, k*D + nh_*512: k*D + nh_*512 + 512],
                            start=(k == 0), stop=(k == NBK - 1)))
                        for k in range(NBK) for nh_ in range(2)])
                    wo_tail[cc2] = wo_done
                    P.wait("vector", "pe", wo_done)
                    if (cglob - 2) in acc_dma_at:
                        P.wait("vector", "dS", acc_dma_at[cglob - 2])
                    for nh_ in range(2):
                        h = P.inc("dve")
                        E("vector", lambda V, nh_=nh_, cglob=cglob, wbk=wbk, h=h: V.scalar_tensor_tensor(
                            accst[cglob % 2][:, nh_*512:(nh_+1)*512], pb[wbk[nh_]][:], 1.0,
                            xbuf[cglob % 4][:, nh_*512:(nh_+1)*512], ALU.mult, ALU.add).then_inc(h, 1))
                    acc_done = P.cnt["dve"]
                    bank_free[wbk[0]] = ("dve", acc_done - 1)
                    bank_free[wbk[1]] = ("dve", acc_done)
                    acc_evac_at[cglob] = acc_done
                    P.wait("sync", "dve", acc_done)
                    h = P.inc("dS", 16)
                    E("sync", lambda S, cglob=cglob, h=h: S.dma_start(
                        out=acc_d.rearrange("(c p) d -> c p d", p=128)[cglob],
                        in_=accst[cglob % 2][:]).then_inc(h, 16))
                    acc_dma_at[cglob] = P.cnt["dS"]
                chunk_end[c] = {k: P.cnt[k] for k in ("pe", "act", "dve")}
                if c + 1 < NCH:
                    emit_xload(c + 1)
                    emit_ln(c + 1)
                    emit_xnT(c + 1)

        for eng in ("tensor", "vector", "scalar", "sync", "gpsimd"):
            for semn in ("pe", "act", "dve"):
                P.wait(eng, semn)
        P.wait("vector", "dS")
        P.wait("tensor", "dS")
        P.wait("scalar", "dS")

        h = P.inc("dve")
        E("vector", lambda V, h=h: V.tensor_reduce(xcol[:], xamaxc[:], AX.X, ALU.max).then_inc(h, 1))
        P.wait("sync", "dve")
        h = P.inc("dS", 16)
        E("sync", lambda S, h=h: S.dma_start(out=fold_bounce.rearrange("o p -> p o"), in_=xcol[:]).then_inc(h, 16))
        P.wait("sync", "dS")
        h = P.inc("dS", 16)
        E("sync", lambda S, h=h: S.dma_start(out=xrow, in_=fold_bounce).then_inc(h, 16))
        P.wait("vector", "dS")
        h = P.inc("dve")
        E("vector", lambda V, h=h: V.memset(ccs, 0.0).then_inc(h, 1))
        P.wait("vector", "dve")
        h = P.inc("dve")
        E("vector", lambda V, h=h: V.tensor_reduce(ccs[0:1, 0:1], xrow, AX.X, ALU.max).then_inc(h, 1))
        P.wait("gpsimd", "dve")
        h = P.inc("dG", 16)
        E("gpsimd", lambda G, h=h: G.dma_start(out=cc1_in.ap(), in_=ccs).then_inc(h, 16))
        P.wait("gpsimd", "dG")
        h = P.inc("cc")
        E("gpsimd", lambda G, h=h: G.collective_compute(
            "AllReduce", ALU.max, replica_groups=[list(range(8))],
            ins=[cc1_in.ap().opt()], outs=[cc1_out.ap().opt()]).then_inc(h, 1))
        P.wait("gpsimd", "cc")
        h = P.inc("dG", 16)
        E("gpsimd", lambda G, h=h: G.dma_start(out=gs1, in_=cc1_out.ap()).then_inc(h, 16))
        P.wait("vector", "dG")
        h = P.inc("dve")
        E("vector", lambda V, h=h: V.tensor_scalar_mul(sc_small[0:1, 0:1], gs1[0:1, 0:1], 1.0 / 32767.0).then_inc(h, 1))
        P.wait("tensor", "dve")
        pe_wait_bank(4)
        rsxmm = pe_mm_group([
            (lambda T: T.matmul(pb[4][:, 0:1], c_ones1[:], sc_small[0:1, 0:1], start=True, stop=True))])
        P.wait("scalar", "pe", rsxmm)
        h = P.inc("act")
        E("scalar", lambda S, h=h: S.copy(r_sx_col[:], pb[4][:, 0:1]).then_inc(h, 1))
        bank_free[4] = ("act", P.cnt["act"])
        P.wait("vector", "act")
        h = P.inc("dve")
        E("vector", lambda V, h=h: V.reciprocal(s_x_col[:], r_sx_col[:]).then_inc(h, 1))
        P.wait("vector", "dve")

        for b in range(NBK):
            h = P.inc("dG", 16)
            E("gpsimd", lambda G, b=b, h=h: G.dma_start(
                out=arenaB[:, b*NT:(b+1)*NT],
                in_=xnT_d.rearrange("(b p) T -> b p T", p=128)[b]).then_inc(h, 16))
        xq_ld = P.cnt["dG"]

        qdone = {}
        ldone = {}
        for i in range(2 * NBK):
            k, hf = i // 2, i % 2
            qe_, qs_ = (("sync", "dS") if i % 2 == 0 else ("scalar", "dA"))
            if (i - 2) in qdone:
                P.wait(qe_, "dve", qdone[i - 2])
            h = P.inc(qs_, 16)
            E(qe_, lambda S, k=k, hf=hf, i=i, h=h: S.dma_start(
                out=wstage[:, (i % 2)*2048:(i % 2)*2048+2048],
                in_=Wv_(W_in, k)[:, hf*2048:(hf+1)*2048]).then_inc(h, 16))
            ldone[i] = P.cnt[qs_]
            P.wait("vector", qs_, ldone[i])
            if k == 0:
                h = P.inc("dve")
                E("vector", lambda V, i=i, hf=hf, h=h: V.tensor_tensor(
                    run[:, hf*2048:(hf+1)*2048], wstage[:, (i % 2)*2048:(i % 2)*2048+2048],
                    wstage[:, (i % 2)*2048:(i % 2)*2048+2048], ALU.mult).then_inc(h, 1))
            else:
                E("vector", lambda V, i=i, hf=hf: V.tensor_tensor(
                    swb[:, hf*2048:(hf+1)*2048], wstage[:, (i % 2)*2048:(i % 2)*2048+2048],
                    wstage[:, (i % 2)*2048:(i % 2)*2048+2048], ALU.mult))
                h = P.inc("dve")
                E("vector", lambda V, hf=hf, h=h: V.tensor_tensor(
                    run[:, hf*2048:(hf+1)*2048], run[:, hf*2048:(hf+1)*2048],
                    swb[:, hf*2048:(hf+1)*2048], ALU.max).then_inc(h, 1))
            qdone[i] = P.cnt["dve"]
        for L in (64, 32, 16, 8, 4, 2, 1):
            P.wait("sync", "dve")
            h = P.inc("dS", 16)
            E("sync", lambda S, L=L, h=h: S.dma_start(out=wstage[0:L, :], in_=run[L:2*L, :]).then_inc(h, 16))
            P.wait("vector", "dS")
            h = P.inc("dve")
            E("vector", lambda V, L=L, h=h: V.tensor_tensor(run[0:L, :], run[0:L, :], wstage[0:L, :], ALU.max).then_inc(h, 1))
        P.wait("vector", "dve")
        h = P.inc("dve")
        E("vector", lambda V, h=h: V.tensor_scalar_max(run[0:1, :], run[0:1, :], 1e-16).then_inc(h, 1))
        P.wait("scalar", "dve")
        h = P.inc("act")
        E("scalar", lambda S, h=h: S.activation(run[32:33, :], run[0:1, :], AF.Ln).then_inc(h, 1))
        P.wait("scalar", "act")
        h = P.inc("act")
        E("scalar", lambda S, h=h: S.activation(run[0:1, :], run[32:33, :], AF.Exp, scale=0.5).then_inc(h, 1))
        h = P.inc("act")
        E("scalar", lambda S, h=h: S.activation(wstage[0:1, :], run[32:33, :], AF.Exp, scale=-0.5).then_inc(h, 1))
        P.wait("sync", "act")
        P.wait("tensor", "act")
        h = P.inc("dS", 16)
        E("sync", lambda S, h=h: S.dma_start(out=wq_bounce, in_=run[0:1, :]).then_inc(h, 16))
        P.wait("sync", "dS")
        h = P.inc("dS", 16)
        E("sync", lambda S, h=h: S.dma_start(out=w32[:], in_=wq_bounce.rearrange("o (a p) -> (o a) p", p=128)).then_inc(h, 16))
        P.wait("tensor", "dS")
        pe_wait_bank(3)
        t32 = pe_mm_group([(lambda T: T.transpose(pb[3][:, 0:32], w32[:], c_idf[0:32, 0:32]))])
        P.wait("scalar", "pe", t32)
        h = P.inc("act")
        E("scalar", lambda S, h=h: S.copy(sc32amax[:], pb[3][:, 0:32]).then_inc(h, 1))
        bank_free[3] = ("act", P.cnt["act"])
        P.wait("vector", "act")
        h = P.inc("dve")
        E("vector", lambda V, h=h: V.tensor_scalar(sc32[:], sc32amax[:], r_sx_col[:], 1.0 / 127.0, ALU.mult, ALU.mult).then_inc(h, 1))
        for cq in range(8):
            bk = cq % 2
            pe_wait_bank(bk)
            h = P.inc("pe")
            E("tensor", lambda T, cq=cq, bk=bk, h=h: T.matmul(
                pb[bk][:], c_c127[:], wstage[0:1, cq*512:(cq+1)*512], start=True, stop=True).then_inc(h, 1))
            P.wait("scalar", "pe")
            h = P.inc("act")
            E("scalar", lambda S, cq=cq, bk=bk, h=h: S.copy(swb[:, cq*512:(cq+1)*512], pb[bk][:]).then_inc(h, 1))
            bank_free[bk] = ("act", P.cnt["act"])
        qdone = {}
        for i in range(2 * NBK):
            k, hf = i // 2, i % 2
            qe_, qs_ = (("sync", "dS") if i % 2 == 0 else ("scalar", "dA"))
            if i <= 1:
                P.wait(qe_, "act")
            if (i - 2) in qdone:
                P.wait(qe_, "dve", qdone[i - 2])
            h = P.inc(qs_, 16)
            E(qe_, lambda S, k=k, hf=hf, i=i, h=h: S.dma_start(
                out=wstage[:, (i % 2)*2048:(i % 2)*2048+2048],
                in_=Wv_(W_in, k)[:, hf*2048:(hf+1)*2048]).then_inc(h, 16))
            ld = P.cnt[qs_]
            P.wait("vector", qs_, ld)
            if i == 0:
                P.wait("vector", "act")
            E("vector", lambda V, i=i, hf=hf: V.tensor_tensor(
                run[:, hf*2048:(hf+1)*2048], wstage[:, (i % 2)*2048:(i % 2)*2048+2048],
                swb[:, hf*2048:(hf+1)*2048], ALU.mult))
            h = P.inc("dve")
            E("vector", lambda V, k=k, hf=hf, h=h: V.tensor_scalar(
                arenaA[:, k*DFF + hf*2048: k*DFF + hf*2048 + 2048], run[:, hf*2048:(hf+1)*2048],
                MAGIC, MAGIC, ALU.add, ALU.subtract).then_inc(h, 1))
            qdone[i] = P.cnt["dve"]

        P.wait("vector", "dG", xq_ld)
        for b in range(NBK):
            E("vector", lambda V, b=b: V.tensor_scalar(
                run[:], arenaB[:, b*NT:(b+1)*NT], s_x_col[:], MAGIC, ALU.mult, ALU.add))
            h = P.inc("dve")
            E("vector", lambda V, b=b, h=h: V.tensor_single_scalar(
                arenaB[:, b*NT:(b+1)*NT], run[:], MAGIC, ALU.subtract).then_inc(h, 1))
        xq_ready = P.cnt["dve"]
        P.wait("tensor", "dve", xq_ready)
        P.wait("tensor", "act")
        E("vector", lambda V: V.memset(hqc[:], 0.0))
        evac_done_at = {}
        dve_red_at = {}
        spill_at = {}
        for quarter in range(4):
            tok0 = quarter * 1024
            for m in range(NBF):
                g = m % 2
                gm = quarter * NBF + m
                if gm >= 2:
                    P.wait("tensor", "act", evac_done_at[gm - 2])
                mml = []
                for k in range(NBK):
                    for t in range(2):
                        mml.append(lambda T, k=k, t=t, m=m, g=g, tok0=tok0: T.matmul(
                            pb[2*g + t][:],
                            arenaA[:, k*DFF + m*128: k*DFF + m*128 + 128],
                            arenaB[:, k*NT + tok0 + t*512: k*NT + tok0 + t*512 + 512],
                            start=(k == 0), stop=(k == NBK - 1)))
                mmd = pe_mm_group(mml)
                P.wait("scalar", "pe", mmd)
                if gm >= 2:
                    P.wait("scalar", "dve", dve_red_at[gm - 2])
                    P.wait("scalar", "dS", spill_at[gm - 2])
                for t in range(2):
                    h = P.inc("act")
                    E("scalar", lambda S, t=t, m=m, g=g, h=h: S.activation(
                        hstage[g][:, t*512:(t+1)*512], pb[2*g + t][:], AF.Gelu,
                        bias=s_bic[:, m:m+1], scale=sc32[:, m:m+1]).then_inc(h, 1))
                evac_done_at[gm] = P.cnt["act"]
                P.wait("vector", "act", evac_done_at[gm])
                h = P.inc("dve")
                E("vector", lambda V, g=g, m=m, h=h: V.tensor_reduce(
                    run[:, m:m+1], hstage[g][:], AX.X, ALU.max,
                    apply_absolute_value=True).then_inc(h, 1))
                P.wait("vector", "dve")
                h = P.inc("dve")
                E("vector", lambda V, m=m, h=h: V.scalar_tensor_tensor(
                    hqc[:, m:m+1], run[:, m:m+1], 1.0, hqc[:, m:m+1], ALU.mult, ALU.max).then_inc(h, 1))
                dve_red_at[gm] = P.cnt["dve"]
                P.wait("sync", "act", evac_done_at[gm])
                h = P.inc("dS", 16)
                E("sync", lambda S, m=m, g=g, tok0=tok0, h=h: S.dma_start(
                    out=hidT.rearrange("(b p) n -> b p n", p=128)[m][:, tok0:tok0+1024],
                    in_=hstage[g][:]).then_inc(h, 16))
                spill_at[gm] = P.cnt["dS"]

        P.wait("vector", "dve")
        h = P.inc("dve")
        E("vector", lambda V, h=h: V.tensor_reduce(xcol[:], hqc[:], AX.X, ALU.max).then_inc(h, 1))
        P.wait("sync", "dve")
        h = P.inc("dS", 16)
        E("sync", lambda S, h=h: S.dma_start(out=fold_bounce.rearrange("o p -> p o"), in_=xcol[:]).then_inc(h, 16))
        P.wait("sync", "dS")
        h = P.inc("dS", 16)
        E("sync", lambda S, h=h: S.dma_start(out=xrow, in_=fold_bounce).then_inc(h, 16))
        P.wait("vector", "dS")
        h = P.inc("dve")
        E("vector", lambda V, h=h: V.memset(ccs, 0.0).then_inc(h, 1))
        P.wait("vector", "dve")
        h = P.inc("dve")
        E("vector", lambda V, h=h: V.tensor_reduce(ccs[0:1, 0:1], xrow, AX.X, ALU.max).then_inc(h, 1))
        P.wait("gpsimd", "dve")
        h = P.inc("dG", 16)
        E("gpsimd", lambda G, h=h: G.dma_start(out=cc2_in.ap(), in_=ccs).then_inc(h, 16))
        P.wait("gpsimd", "dG")
        h = P.inc("cc")
        E("gpsimd", lambda G, h=h: G.collective_compute(
            "AllReduce", ALU.max, replica_groups=[list(range(8))],
            ins=[cc2_in.ap().opt()], outs=[cc2_out.ap().opt()]).then_inc(h, 1))
        P.wait("gpsimd", "cc")
        h = P.inc("dG", 16)
        E("gpsimd", lambda G, h=h: G.dma_start(out=gs2, in_=cc2_out.ap()).then_inc(h, 16))
        P.wait("vector", "dG")
        h = P.inc("dve")
        E("vector", lambda V, h=h: V.tensor_scalar_mul(sc_small[0:1, 1:2], gs2[0:1, 0:1], 1.0 / 32767.0).then_inc(h, 1))
        P.wait("tensor", "dve")
        pe_wait_bank(4)
        rshmm = pe_mm_group([
            (lambda T: T.matmul(pb[4][:, 0:1], c_ones1[:], sc_small[0:1, 1:2], start=True, stop=True))])
        P.wait("scalar", "pe", rshmm)
        h = P.inc("act")
        E("scalar", lambda S, h=h: S.copy(r_sh_col[:], pb[4][:, 0:1]).then_inc(h, 1))
        bank_free[4] = ("act", P.cnt["act"])
        P.wait("vector", "act")
        h = P.inc("dve")
        E("vector", lambda V, h=h: V.reciprocal(s_h_col[:], r_sh_col[:]).then_inc(h, 1))
        P.wait("vector", "dve")

        P.wait("vector", "pe")
        qdone = {}
        for k in range(NBF):
            qe_, qs_ = (("sync", "dS") if k % 2 == 0 else ("scalar", "dA"))
            if (k - 2) in qdone:
                P.wait(qe_, "dve", qdone[k - 2])
            h = P.inc(qs_, 16)
            E(qe_, lambda S, k=k, h=h: S.dma_start(
                out=wstage[:, (k % 2)*1024:(k % 2)*1024+1024], in_=Wv_(W_out, k)).then_inc(h, 16))
            ld = P.cnt[qs_]
            P.wait("vector", qs_, ld)
            h = P.inc("dve")
            E("vector", lambda V, k=k, h=h: V.tensor_reduce(
                amax2c[:, k:k+1], wstage[:, (k % 2)*1024:(k % 2)*1024+1024],
                AX.X, ALU.max, apply_absolute_value=True).then_inc(h, 1))
            P.wait("vector", "dve")
            h = P.inc("dve")
            E("vector", lambda V, k=k, h=h: V.tensor_scalar_max(amax2c[:, k:k+1], amax2c[:, k:k+1], 1e-8).then_inc(h, 1))
            P.wait("vector", "dve")
            h = P.inc("dve")
            E("vector", lambda V, k=k, h=h: V.reciprocal(rcol[:], amax2c[:, k:k+1]).then_inc(h, 1))
            P.wait("vector", "dve")
            E("vector", lambda V, k=k: V.tensor_scalar(
                run[:, (k % 2)*1024:(k % 2)*1024+1024], wstage[:, (k % 2)*1024:(k % 2)*1024+1024],
                rcol[:], 127.0, ALU.mult, ALU.mult))
            h = P.inc("dve")
            E("vector", lambda V, k=k, h=h: V.tensor_scalar(
                arenaA[:, k*1024:(k+1)*1024], run[:, (k % 2)*1024:(k % 2)*1024+1024],
                MAGIC, MAGIC, ALU.add, ALU.subtract).then_inc(h, 1))
            qdone[k] = P.cnt["dve"]
        h = P.inc("dve")
        E("vector", lambda V, h=h: V.tensor_scalar(sc2[:], amax2c[:], r_sh_col[:], 1.0 / 127.0, ALU.mult, ALU.mult).then_inc(h, 1))
        P.wait("vector", "dve")
        P.wait("sync", "dve")
        h = P.inc("dS", 16)
        E("sync", lambda S, h=h: S.dma_start(out=brow[0:1, :], in_=b_out_d).then_inc(h, 16))
        P.wait("tensor", "dS")
        P.wait("tensor", "dve")
        pe_wait_bank(4); pe_wait_bank(5)
        bmm = pe_mm_group([
            (lambda T, n=n: T.matmul(pb[4 + n][:], c_ones1[:], brow[0:1, n*512:(n+1)*512],
                                     start=True, stop=True)) for n in range(2)])
        P.wait("scalar", "pe", bmm)
        for n in range(2):
            h = P.inc("act")
            E("scalar", lambda S, n=n, h=h: S.copy(boutb[:, n*512:(n+1)*512], pb[4 + n][:]).then_inc(h, 1))
        bank_free[4] = ("act", P.cnt["act"] - 1)
        bank_free[5] = ("act", P.cnt["act"])
        P.wait("vector", "act")

        ld_at = {}
        ev_at = {}
        pe_after_sub = 0
        hdbuf4 = [BA[:, 65536:69632], BA[:, 69632:73728]]
        p3scr2 = FA[:, 1024:3072]
        q_done = {}
        l_done = {}
        p2sp = P.cnt["dS"]
        P.wait("scalar", "dS", p2sp)
        for sub in range(4):
            tok0 = sub * 1024
            if sub > 0:
                P.wait("vector", "pe", pe_after_sub)
            for kb0 in range(8):
                kb = kb0
                gi = sub * 8 + kb
                qe_, qs_ = (("sync", "dS") if gi % 2 == 0 else ("scalar", "dA"))
                if (gi - 2) in q_done:
                    P.wait(qe_, "dve", q_done[gi - 2])
                h = P.inc(qs_, 16)
                E(qe_, lambda S, kb=kb, tok0=tok0, h=h: S.dma_start(
                    out=hdbuf4[kb % 2].rearrange("p (b n) -> p b n", n=1024),
                    in_=hidT.rearrange("(a b p) n -> a p b n", b=4, p=128)[kb][:, :, tok0:tok0+1024]).then_inc(h, 16))
                l_done[gi] = P.cnt[qs_]
                P.wait("vector", qs_, l_done[gi])
                for half in range(2):
                    E("vector", lambda V, kb=kb, half=half: V.tensor_scalar(
                        p3scr2[:], hdbuf4[kb % 2][:, half*2048:(half+1)*2048],
                        s_h_col[:], MAGIC, ALU.mult, ALU.add))
                    for j in range(2):
                        k0 = kb * 4 + half * 2 + j
                        h = P.inc("dve")
                        E("vector", lambda V, k0=k0, j=j, h=h: V.tensor_scalar(
                            arenaB[:, k0*1024:(k0+1)*1024], p3scr2[:, j*1024:(j+1)*1024],
                            MAGIC, sc2[:, k0:k0+1], ALU.subtract, ALU.mult).then_inc(h, 1))
                q_done[gi] = P.cnt["dve"]
            hs_ready = P.cnt["dve"]
            P.wait("tensor", "dve", hs_ready)
            P.wait("tensor", "act")
            for mt in range(8):
                g = mt % 2
                gm = sub * 8 + mt
                cglob = sub * 8 + mt
                if gm >= 2:
                    P.wait("sync", "dve", ev_at[gm - 2])
                h = P.inc("dS", 16)
                E("sync", lambda S, cglob=cglob, g=g, h=h: S.dma_start(
                    out=accbuf[g][:],
                    in_=acc_d.rearrange("(c p) d -> c p d", p=128)[cglob]).then_inc(h, 16))
                accl = P.cnt["dS"]
                if gm >= 2:
                    P.wait("tensor", "dve", ev_at[gm - 2])
                mml = []
                for k in range(NBF):
                    for n in range(2):
                        mml.append(lambda T, k=k, n=n, mt=mt, g=g: T.matmul(
                            pb[2*g + n][:],
                            arenaB[:, k*1024 + mt*128: k*1024 + mt*128 + 128],
                            arenaA[:, k*1024 + n*512: k*1024 + n*512 + 512],
                            start=(k == 0), stop=(k == NBF - 1)))
                mmd = pe_mm_group(mml)
                P.wait("vector", "pe", mmd)
                P.wait("vector", "dS", accl)
                if gm >= 2:
                    P.wait("vector", "dS", ld_at[gm - 2])
                for n in range(2):
                    E("vector", lambda V, n=n, g=g: V.scalar_tensor_tensor(
                        ostage[g][:, n*512:(n+1)*512], pb[2*g + n][:], 1.0,
                        accbuf[g][:, n*512:(n+1)*512], ALU.mult, ALU.add))
                h = P.inc("dve")
                E("vector", lambda V, g=g, h=h: V.tensor_tensor(
                    ostage[g][:], ostage[g][:], boutb[:], ALU.add).then_inc(h, 1))
                ev_at[gm] = P.cnt["dve"]
                P.wait("sync", "dve", ev_at[gm])
                h = P.inc("dS", 16)
                E("sync", lambda S, cglob=cglob, g=g, h=h: S.dma_start(
                    out=out_d.rearrange("(c p) d -> c p d", p=128)[cglob],
                    in_=ostage[g][:]).then_inc(h, 16))
                ld_at[gm] = P.cnt["dS"]
            pe_after_sub = P.cnt["pe"]

        P.wait("sync", "dS")
        P.wait("sync", "dG")
        P.wait("sync", "dA")
        P.wait("gpsimd", "dG")
        P.wait("gpsimd", "dS")
        P.wait("scalar", "dA")

        with nc.Block() as block:
            P.run(block)
    return nc


def _make_consts():
    j = np.arange(128)
    return dict(
        ident_bf=np.eye(128).astype(ml_dtypes.bfloat16),
        ident_f32=np.eye(128).astype(np.float32),
        triu16=(np.where(j[:, None] <= j[None, :], 1.0, 0.0) / 16.0).astype(np.float32),
        slo16=(np.where(j[:, None] > j[None, :], -1.0, 0.0) / 16.0).astype(np.float32),
        onescol16=np.full((128, 1), 1.0 / 16.0, dtype=np.float32),
        ones_1x128=np.ones((1, 128), dtype=np.float32),
        c127_1x128=np.full((1, 128), 127.0, dtype=np.float32),
        mask4=np.tile(np.where(j[:, None] <= j[None, :], 1.0, 0.0).astype(np.float32), (1, 4)),
        c_cols=np.tile(np.array([[1.0, 1e-5, 1e-6, LN8, 0, 0, 0, 0]], np.float32), (128, 1)),
    )


_CACHED = {}


def kernel(L0_attn_x, L0_attn_Wq, L0_attn_Wk, L0_attn_Wv, L0_attn_Wg, L0_attn_Wgt,
           L0_attn_Wo, L0_ffn_W_in, L0_ffn_b_in, L0_ffn_W_out, L0_ffn_b_out):
    C = _make_consts()
    if "nc" not in _CACHED:
        _CACHED["nc"] = build(bass.Bass("TRN2", target_bir_lowering=False))
    nc = _CACHED["nc"]
    in_maps = []
    for c in range(8):
        m = {
            "x": np.ascontiguousarray(np.asarray(L0_attn_x)[c]),
            "Wq": np.asarray(L0_attn_Wq), "Wk": np.asarray(L0_attn_Wk),
            "Wv": np.asarray(L0_attn_Wv), "Wg": np.asarray(L0_attn_Wg),
            "Wgt": np.asarray(L0_attn_Wgt), "Wo": np.asarray(L0_attn_Wo),
            "W_in": np.asarray(L0_ffn_W_in),
            "b_in_cols": np.ascontiguousarray(np.asarray(L0_ffn_b_in).reshape(NBF, 128).T),
            "W_out": np.asarray(L0_ffn_W_out),
            "b_out": np.ascontiguousarray(np.asarray(L0_ffn_b_out).reshape(1, D)),
        }
        m.update(C)
        in_maps.append(m)
    res = run_bass_kernel_spmd(nc, in_maps, core_ids=list(range(8)))
    return np.stack([np.asarray(res.results[c]["out"]) for c in range(8)], axis=0)
